# version 02
# baseline (speedup 1.0000x reference)
import numpy as np
import ml_dtypes
import concourse.bass as bass
import concourse.mybir as mybir
from concourse.bass_utils import run_bass_kernel_spmd

F32 = mybir.dt.float32
F16 = mybir.dt.float16
BF16 = mybir.dt.bfloat16
F8 = mybir.dt.float8e4
ALU = mybir.AluOpType
ACT = mybir.ActivationFunctionType

FUSED = True
DEBUG = False

Lc = 2048
D = 1024
CC = 2048
NST = 16
WT = 127
NW = 17
LASTV = Lc - (NW - 1) * WT
N_MM = 512
TP = 2176
NSLOT = 9
GRPW = [range(0, 8), range(8, 16), range(16, 17)]
WOG = [range(2 * i, 2 * i + 2) for i in range(8)] + [range(16, 17)]
ENGINES = ("sync", "vector", "scalar", "tensor", "gpsimd")
PERSLOT = ("winT", "wxT", "wdtTb", "woT", "cw", "cb", "alogT", "dpar",
           "arinH", "aroutH", "arinX", "aroutX")


class KB:
    PARANOID = ("vector",)

    def __init__(self, nc):
        self.nc = nc
        self.ops = {e: [] for e in ENGINES}
        self.cnt = {}
        self.sems = {}
        self.lastw = {}

    def sem(self, name):
        if name not in self.sems:
            self.sems[name] = self.nc.semaphore(name).__enter__()
            self.cnt[name] = 0
        return self.sems[name]

    def wait(self, engine, name, value=None):
        h = self.sem(name)
        v = self.cnt[name] if value is None else value
        if v > 0:
            self.ops[engine].append(lambda eng, h=h, v=v: eng.wait_ge(h, v))

    REGION = True
    OWN = {"vector": "dve", "scalar": "act", "tensor": "pe"}

    def wait_all(self, engine):
        wl = self.lastw.setdefault(engine, {})
        for nm in list(self.cnt):
            v = self.cnt.get(nm, 0)
            if v > 0 and nm != self.OWN.get(engine) and wl.get(nm, 0) < v:
                self.wait(engine, nm)
                wl[nm] = v

    def step(self, engine, name, amount, fn):
        if (self.PARANOID is True or (self.PARANOID and engine in self.PARANOID)) and self.REGION:
            self.wait_all(engine)
        h = self.sem(name)
        self.cnt[name] = self.cnt.get(name, 0) + amount
        if engine in ("vector", "scalar", "tensor") and name != "dmaG" and name != "dmaL":
            def wrapped(eng, fn=fn, h=h, a=amount):
                fn(eng)
                eng.drain().then_inc(h, a)
        else:
            def wrapped(eng, fn=fn, h=h, a=amount):
                fn(eng).then_inc(h, a)
        self.ops[engine].append(wrapped)
        return self.cnt[name]

    def pe_fence(self):
        h = self.sem("pe")
        self.cnt["pe"] = self.cnt.get("pe", 0) + 1
        def wrapped(eng, h=h):
            eng.drain().then_inc(h, 1)
        self.ops["tensor"].append(wrapped)
        return self.cnt["pe"]

    def run_into_block(self, block):
        for e in ENGINES:
            ops = self.ops[e]
            if not ops:
                continue
            def make(ops):
                def body(eng):
                    for f in ops:
                        f(eng)
                return body
            getattr(block, e)(make(ops))


def sbuf(nc, name, shape, dt):
    return nc.sbuf_tensor(name, shape, dt).__enter__()


def pst(nc, name, shape, dt):
    return nc.psum_tensor(name, shape, dt).__enter__()


class Ctx:
    def __init__(self, nc):
        self.nc = nc
        self.xm = [sbuf(nc, f"xm{i}", [128, Lc], F16) for i in range(8)]
        self.ustg = sbuf(nc, "ustg", [128, 2048], F16)
        self.zsil = sbuf(nc, "zsil", [128, 16 * WT], F16)
        self.dwin = sbuf(nc, "dwin", [128, CC], F16)
        self.dlw = sbuf(nc, "dlw", [128, CC], F16)
        self.duw = sbuf(nc, "duw", [128, CC], F16)
        self.rg = sbuf(nc, "rg", [128, CC], BF16)
        self.ww = sbuf(nc, "ww", [128, CC], BF16)
        self.yw = sbuf(nc, "yw", [128, CC], F16)
        self.statn = sbuf(nc, "statn", [128, 128], BF16)
        self.trilmid = sbuf(nc, "trilmid", [128, 128], F16)
        self.trilsum = sbuf(nc, "trilsum", [128, 128], BF16)
        self.ident = sbuf(nc, "ident", [128, 128], F16)
        self.bct = sbuf(nc, "bct", [128, 2 * NST], F16)
        self.bctf = sbuf(nc, "bctf", [128, 2 * NST], F32)
        self.abc = [sbuf(nc, f"abc{i}", [128, CC], BF16) for i in range(NST)]
        self.alpha16 = sbuf(nc, "alpha16", [17, CC], F16)
        self.alrep = sbuf(nc, "alrep", [128, CC], BF16)
        self.gA = sbuf(nc, "gA", [128, CC], BF16)
        self.gB = sbuf(nc, "gB", [128, CC], BF16)
        self.dmde = [sbuf(nc, f"dmde{g}", [32, 1024], F16) for g in range(3)]
        self.hselA = sbuf(nc, "hselA", [128, 8 * 32], F16)
        self.hselB = sbuf(nc, "hselB", [128, 8 * 32], F16)
        self.indA0 = sbuf(nc, "indA0", [32, 128], F16)
        self.indA1 = sbuf(nc, "indA1", [32, 128], F16)
        self.indB0 = sbuf(nc, "indB0", [32, 128], F16)
        self.indB1 = sbuf(nc, "indB1", [32, 128], F16)
        self.rep8 = sbuf(nc, "rep8", [NST, 128], F16)
        self.maskw16 = sbuf(nc, "maskw16", [128, 1], F32)
        self.hrun = sbuf(nc, "hrun", [NST, CC], F16)
        self.hraw = sbuf(nc, "hraw", [NST, CC], BF16)
        self.gsl = sbuf(nc, "gsl", [NST, CC], BF16)
        self.dpb = sbuf(nc, "dpb", [128, CC], F16)
        self.xdblb = sbuf(nc, "xdblb", [128, Lc + WT], F16)
        self.wrhs = [sbuf(nc, f"wrhs{i}", [128, 2 * WT], F16) for i in range(16)]
        self.w32 = sbuf(nc, "w32", [128, 1024], F32)
        self.wkp = [sbuf(nc, f"wkp{k}", [128, 256], F16) for k in range(8)]
        self.convw = sbuf(nc, "convw", [128, 16 * 4], F32)
        self.convb = sbuf(nc, "convb", [128, 16], F32)
        self.wxtb = sbuf(nc, "wxtb", [128, 16 * 96], F16)
        self.wdtb = sbuf(nc, "wdtb", [128, CC], F16)
        self.xhead = sbuf(nc, "xhead", [128, 3], F16)
        self.zpad = sbuf(nc, "zpad", [128, TP - Lc], F16)
        _wos0 = sbuf(nc, "wos0", [128, D], F16)
        self.wos = [_wos0, _wos0]
        self.xmt = sbuf(nc, "xmt", [128, 32], F16)
        self.ones1 = sbuf(nc, "ones1", [1, 128], F16)
        self.onescol = sbuf(nc, "onescol", [128, 1], F16)
        self.xtl = sbuf(nc, "xtl", [128, 32], F16)
        self.mcols = sbuf(nc, "mcols", [128, 48], F32)
        self.fcols = sbuf(nc, "fcols", [128, 2], F32)
        self.scol = sbuf(nc, "scol", [128, 16], F16)
        self.h2f8 = sbuf(nc, "h2f8", [128, 4096], F8)
        self.w1p = sbuf(nc, "w1p", [128, 1024], F8)
        self.w2s = sbuf(nc, "w2s", [128, 1024], F8)
        self.h1q8 = sbuf(nc, "h1q8", [128, 512], F8)
        self.bq = sbuf(nc, "bq", [128, 40], F32)
        self.psA = pst(nc, "psA", [128, 1024], F32)
        self.psB = pst(nc, "psB", [128, 1024], F32)
        self.psC = pst(nc, "psC", [128, 1024], F32)
        self.psD = pst(nc, "psD", [128, 512], F32)
        self.psT = pst(nc, "psT", [128, 1024], F16)


def host_consts():
    bf = ml_dtypes.bfloat16
    trilmid = np.zeros((128, 128), np.float32)
    for t in range(WT):
        for s in range(WT):
            trilmid[s, t] = (1.0 if s <= t else 0.0) - (1.0 if s <= 63 else 0.0)
    trilsum = np.zeros((128, 128), np.float32)
    for t in range(WT):
        trilsum[: t + 1, t] = 1.0
    trilsum[127, :] = 1.0
    trilsum[:WT, 127] = 1.0
    ident = np.eye(128, dtype=np.float32)
    hselA = np.zeros((128, 8 * 32), np.float32)
    hselB = np.zeros((128, 8 * 32), np.float32)
    for wl in range(8):
        hselA[:64, 32 * wl + 2 * wl] = 1.0
        hselA[64:WT, 32 * wl + 2 * wl + 1] = 1.0
        hselB[:64, 32 * wl + 16 + 2 * wl] = 1.0
        hselB[64:WT, 32 * wl + 17 + 2 * wl] = 1.0
    indA0 = np.zeros((32, 128), np.float32)
    indA1 = np.zeros((32, 128), np.float32)
    indB0 = np.zeros((32, 128), np.float32)
    indB1 = np.zeros((32, 128), np.float32)
    for wl in range(8):
        indA0[2 * wl, 16 * wl:16 * wl + 16] = 1.0
        indA1[16 + 2 * wl, 16 * wl:16 * wl + 16] = 1.0
        indB0[2 * wl + 1, 16 * wl:16 * wl + 16] = 1.0
        indB1[16 + 2 * wl + 1, 16 * wl:16 * wl + 16] = 1.0
    rep8 = np.zeros((16, 128), np.float32)
    for wl in range(8):
        for n in range(16):
            rep8[n, 16 * wl + n] = 1.0
    maskw16 = np.zeros((128, 1), np.float32)
    maskw16[:LASTV] = 1.0
    f16 = np.float16
    onesrow = np.zeros((1, Lc + WT), np.float16)
    onesrow[0, :Lc] = 1.0
    return {
        "trilmid": trilmid.astype(f16), "trilsum": trilsum.astype(bf),
        "ident": ident.astype(f16), "hselA": hselA.astype(f16), "hselB": hselB.astype(f16),
        "indA0": indA0.astype(f16), "indA1": indA1.astype(f16),
        "indB0": indB0.astype(f16), "indB1": indB1.astype(f16),
        "rep8": rep8.astype(f16), "maskw16": maskw16,
        "ones1": np.ones((1, 128), np.float32).astype(f16),
        "onescol": np.ones((128, 1), np.float32).astype(f16),
        "onesrow": onesrow,
    }


CONST_NAMES = ("trilmid", "trilsum", "ident", "hselA", "hselB",
               "indA0", "indA1", "indB0", "indB1", "rep8", "maskw16",
               "ones1", "onescol")


def emit_load_consts(kb, cx, key):
    for name in CONST_NAMES:
        t = getattr(cx, name)
        kb.step("sync", "dmaL", 16, lambda e, t=t, n=name: e.dma_start(t[:], key(n)[:]))
    kb.step("sync", "dmaL", 16, lambda e: e.dma_start(cx.fcols[:], key("fcols")[:]))
    kb.wait("vector", "dmaL")
    kb.step("vector", "dve", 1, lambda e: e.memset(cx.zpad[:], 0.0))
    kb.step("vector", "dve", 1, lambda e: e.memset(cx.hrun[:], 0.0))
    kb.wait("gpsimd", "dve")
    kb.step("gpsimd", "dmaG", 16, lambda e: e.dma_start(cx.bct[127:128, :], key("onesrow")[0:1, 0:2 * NST]))


def emit_modvecs(kb, cx, key):
    kb.wait_all("sync")
    kb.step("sync", "dmaL", 16, lambda e: e.dma_start(cx.w32[:, 0:8], key("ccol")[:]))
    kb.wait_all("scalar")
    kb.step("scalar", "act", 1, lambda e: e.activation(cx.w32[:, 8:16], cx.w32[:, 0:8], ACT.Silu))
    kb.step("vector", "dve", 1, lambda e: e.memset(cx.scol[:], 0.0))
    for k in range(8):
        kb.step("vector", "dve", 1, lambda e, k=k: e.tensor_copy(cx.scol[:, 2 * k:2 * k + 1], cx.w32[:, 8 + k:9 + k]))
    for jc in range(6):
        for kt in range(8):
            kb.wait_all("sync")
            kb.step("sync", "dmaL", 16, lambda e, jc=jc, kt=kt: e.dma_start(
                cx.w32[:, 0:1024], key("wmod")[128 * kt:128 * (kt + 1), 1024 * jc:1024 * (jc + 1)]))
            kb.wait("vector", "dmaL")
            kb.step("vector", "dve", 1, lambda e: e.tensor_copy(cx.duw[:, 0:1024], cx.w32[:, 0:1024]))
            kb.wait_all("tensor")
            for jj in range(8):
                kb.step("tensor", "pe", 1, lambda e, jj=jj, kt=kt: e.matmul(
                    cx.psD[:, 2 * jj:2 * jj + 2], cx.duw[:, 128 * jj:128 * (jj + 1)],
                    cx.scol[:, 2 * kt:2 * kt + 2], start=True, stop=True))
            kb.pe_fence()
            kb.wait("vector", "pe")
            for jj in range(8):
                if kt == 0:
                    kb.step("vector", "dve", 1, lambda e, jc=jc, jj=jj: e.tensor_copy(
                        cx.mcols[:, 8 * jc + jj:8 * jc + jj + 1], cx.psD[:, 2 * jj:2 * jj + 1]))
                else:
                    kb.step("vector", "dve", 1, lambda e, jc=jc, jj=jj: e.tensor_tensor(
                        cx.mcols[:, 8 * jc + jj:8 * jc + jj + 1],
                        cx.mcols[:, 8 * jc + jj:8 * jc + jj + 1],
                        cx.psD[:, 2 * jj:2 * jj + 1], ALU.add))
    kb.step("vector", "dve", 1, lambda e: e.tensor_scalar(cx.mcols[:, 8:16], cx.mcols[:, 8:16], 1.0, None, ALU.add))
    kb.step("vector", "dve", 1, lambda e: e.tensor_scalar(cx.mcols[:, 32:40], cx.mcols[:, 32:40], 1.0, None, ALU.add))
    for base in (16, 40):
        kb.step("vector", "dve", 1, lambda e, base=base: e.tensor_scalar(
            cx.w32[:, 0:8], cx.mcols[:, base:base + 8], 0.0, None, ALU.is_ge))
        kb.step("vector", "dve", 1, lambda e: e.tensor_scalar(
            cx.w32[:, 0:8], cx.w32[:, 0:8], 2.0, -1.0, ALU.mult, ALU.add))
        kb.step("vector", "dve", 1, lambda e, base=base: e.tensor_scalar(
            cx.w32[:, 8:16], cx.mcols[:, base:base + 8], -1.0, None, ALU.mult))
        kb.step("vector", "dve", 1, lambda e, base=base: e.tensor_tensor(
            cx.w32[:, 8:16], cx.mcols[:, base:base + 8], cx.w32[:, 8:16], ALU.max))
        kb.step("vector", "dve", 1, lambda e: e.tensor_scalar(
            cx.w32[:, 8:16], cx.w32[:, 8:16], 0.25, None, ALU.max))
        kb.step("vector", "dve", 1, lambda e, base=base: e.tensor_tensor(
            cx.mcols[:, base:base + 8], cx.w32[:, 0:8], cx.w32[:, 8:16], ALU.mult))


def emit_ln(kb, cx, key, phase):
    if phase == 1:
        kb.wait("vector", "dmaL")
        kb.step("vector", "dve", 1, lambda e: e.memset(cx.xtl[:], 0.0))
        for kt in range(8):
            for h in range(2):
                kb.wait_all("sync")
                kb.step("sync", "dmaL", 16, lambda e, kt=kt, h=h: e.dma_start(
                    cx.w32[:, 0:1024], key("XA")[128 * kt:128 * (kt + 1), 3 + 1024 * h:3 + 1024 * (h + 1)]))
                kb.wait("vector", "dmaL")
                kb.step("vector", "dve", 1, lambda e, kt=kt, h=h: e.tensor_copy(
                    cx.xm[kt][:, 1024 * h:1024 * (h + 1)], cx.w32[:, 0:1024]))
            kb.wait_all("sync")
            kb.step("sync", "dmaL", 16, lambda e, kt=kt: e.dma_start(
                cx.w32[:, 0:3], key("XA")[128 * kt:128 * (kt + 1), 0:3]))
            kb.wait("vector", "dmaL")
            kb.step("vector", "dve", 1, lambda e, kt=kt: e.tensor_copy(
                cx.xtl[:, 4 * kt:4 * kt + 3], cx.w32[:, 0:3]))
    for th in range(2):
        for kt in range(8):
            kb.step("vector", "dve", 1, lambda e, kt=kt, th=th: e.tensor_tensor(
                cx.duw[:, 0:1024], cx.xm[kt][:, 1024 * th:1024 * (th + 1)],
                cx.xm[kt][:, 1024 * th:1024 * (th + 1)], ALU.mult))
            kb.wait_all("tensor")
            for c in range(2):
                kb.step("tensor", "pe", 1, lambda e, kt=kt, th=th, c=c: e.matmul(
                    cx.psA[0:1, 512 * c:512 * (c + 1)], cx.onescol[:],
                    cx.xm[kt][:, 1024 * th + 512 * c:1024 * th + 512 * (c + 1)],
                    start=(kt == 0), stop=(kt == 7)))
                kb.step("tensor", "pe", 1, lambda e, kt=kt, c=c: e.matmul(
                    cx.psB[0:1, 512 * c:512 * (c + 1)], cx.onescol[:],
                    cx.duw[:, 512 * c:512 * (c + 1)], start=(kt == 0), stop=(kt == 7)))
        kb.pe_fence()
        kb.wait("vector", "pe")
        kb.step("vector", "dve", 1, lambda e, th=th: e.tensor_scalar(
            cx.ustg[0:1, 1024 * th:1024 * (th + 1)], cx.psA[0:1, 0:1024], 1.0 / 1024, None, ALU.mult))
        kb.step("vector", "dve", 1, lambda e, th=th: e.tensor_scalar(
            cx.dpb[0:1, 1024 * th:1024 * (th + 1)], cx.psB[0:1, 0:1024], 1.0 / 1024, None, ALU.mult))
    kb.step("vector", "dve", 1, lambda e: e.tensor_tensor(
        cx.dwin[0:1, :], cx.ustg[0:1, :], cx.ustg[0:1, :], ALU.mult))
    kb.step("vector", "dve", 1, lambda e: e.tensor_tensor(
        cx.dpb[0:1, :], cx.dpb[0:1, :], cx.dwin[0:1, :], ALU.subtract))
    kb.step("vector", "dve", 1, lambda e: e.tensor_scalar(
        cx.dpb[0:1, :], cx.dpb[0:1, :], 1e-5, None, ALU.add))
    kb.wait_all("scalar")
    kb.step("scalar", "act", 1, lambda e: e.activation(
        cx.dwin[0:1, :], cx.dpb[0:1, :], ACT.Ln, bias=0.0, scale=1.0))
    kb.step("scalar", "act", 1, lambda e: e.activation(
        cx.dpb[0:1, :], cx.dwin[0:1, :], ACT.Exp, bias=0.0, scale=-0.5))
    kb.wait_all("tensor")
    for c in range(4):
        kb.step("tensor", "pe", 1, lambda e, c=c: e.matmul(
            (cx.psA if c < 2 else cx.psB)[:, 512 * (c % 2):512 * (c % 2 + 1)],
            cx.ones1[:], cx.ustg[0:1, 512 * c:512 * (c + 1)], start=True, stop=True))
    kb.pe_fence()
    kb.wait("vector", "pe")
    kb.step("vector", "dve", 1, lambda e: e.tensor_copy(cx.dwin[:, 0:1024], cx.psA[:, :1024]))
    kb.step("vector", "dve", 1, lambda e: e.tensor_copy(cx.dwin[:, 1024:], cx.psB[:, :1024]))
    kb.wait_all("tensor")
    for c in range(4):
        kb.step("tensor", "pe", 1, lambda e, c=c: e.matmul(
            (cx.psA if c < 2 else cx.psB)[:, 512 * (c % 2):512 * (c % 2 + 1)],
            cx.ones1[:], cx.dpb[0:1, 512 * c:512 * (c + 1)], start=True, stop=True))
    kb.pe_fence()
    kb.wait("vector", "pe")
    kb.step("vector", "dve", 1, lambda e: e.tensor_copy(cx.dlw[:, 0:1024], cx.psA[:, :1024]))
    kb.step("vector", "dve", 1, lambda e: e.tensor_copy(cx.dlw[:, 1024:], cx.psB[:, :1024]))
    if phase == 1:
        kb.step("vector", "dve", 1, lambda e: e.tensor_tensor(
            cx.duw[:, 0:32], cx.xtl[:], cx.xtl[:], ALU.mult))
        kb.wait_all("tensor")
        for kt in range(8):
            kb.step("tensor", "pe", 1, lambda e, kt=kt: e.matmul(
                cx.psD[0:1, 0:4], cx.onescol[:], cx.xtl[:, 4 * kt:4 * (kt + 1)],
                start=(kt == 0), stop=(kt == 7)))
        for kt in range(8):
            kb.step("tensor", "pe", 1, lambda e, kt=kt: e.matmul(
                cx.psD[0:1, 4:8], cx.onescol[:], cx.duw[:, 4 * kt:4 * (kt + 1)],
                start=(kt == 0), stop=(kt == 7)))
        kb.pe_fence()
        kb.wait("vector", "pe")
        kb.step("vector", "dve", 1, lambda e: e.tensor_scalar(
            cx.bct[0:1, 0:4], cx.psD[0:1, 0:4], 1.0 / 1024, None, ALU.mult))
        kb.step("vector", "dve", 1, lambda e: e.tensor_scalar(
            cx.bct[0:1, 4:8], cx.psD[0:1, 4:8], 1.0 / 1024, None, ALU.mult))
        kb.step("vector", "dve", 1, lambda e: e.tensor_tensor(
            cx.bct[0:1, 8:12], cx.bct[0:1, 0:4], cx.bct[0:1, 0:4], ALU.mult))
        kb.step("vector", "dve", 1, lambda e: e.tensor_tensor(
            cx.bct[0:1, 4:8], cx.bct[0:1, 4:8], cx.bct[0:1, 8:12], ALU.subtract))
        kb.step("vector", "dve", 1, lambda e: e.tensor_scalar(
            cx.bct[0:1, 4:8], cx.bct[0:1, 4:8], 1e-5, None, ALU.add))
        kb.wait_all("scalar")
        kb.step("scalar", "act", 1, lambda e: e.activation(
            cx.bct[0:1, 8:12], cx.bct[0:1, 4:8], ACT.Ln, bias=0.0, scale=1.0))
        kb.step("scalar", "act", 1, lambda e: e.activation(
            cx.bct[0:1, 4:8], cx.bct[0:1, 8:12], ACT.Exp, bias=0.0, scale=-0.5))
        kb.wait_all("tensor")
        kb.step("tensor", "pe", 1, lambda e: e.matmul(
            cx.psD[:, 8:12], cx.ones1[:], cx.bct[0:1, 0:4], start=True, stop=True))
        kb.step("tensor", "pe", 1, lambda e: e.matmul(
            cx.psD[:, 12:16], cx.ones1[:], cx.bct[0:1, 4:8], start=True, stop=True))
        kb.pe_fence()
        kb.wait("vector", "pe")
        kb.step("vector", "dve", 1, lambda e: e.tensor_copy(cx.statn[:, 0:4], cx.psD[:, 8:12]))
        kb.step("vector", "dve", 1, lambda e: e.tensor_copy(cx.statn[:, 4:8], cx.psD[:, 12:16]))
        for kt in range(8):
            kb.step("vector", "dve", 1, lambda e, kt=kt: e.tensor_tensor(
                cx.xtl[:, 4 * kt:4 * (kt + 1)], cx.xtl[:, 4 * kt:4 * (kt + 1)],
                cx.statn[:, 0:4], ALU.subtract))
            kb.step("vector", "dve", 1, lambda e, kt=kt: e.tensor_tensor(
                cx.xtl[:, 4 * kt:4 * (kt + 1)], cx.xtl[:, 4 * kt:4 * (kt + 1)],
                cx.statn[:, 4:8], ALU.mult))
            kb.step("vector", "dve", 1, lambda e, kt=kt: e.tensor_scalar(
                cx.xtl[:, 4 * kt:4 * (kt + 1)], cx.xtl[:, 4 * kt:4 * (kt + 1)],
                cx.mcols[:, 8 + kt:9 + kt], cx.mcols[:, kt:kt + 1], ALU.mult, ALU.add))
    cfoff = 8 if phase == 1 else 32
    soff = 0 if phase == 1 else 24
    for kt in range(8):
        kb.step("vector", "dve", 1, lambda e, kt=kt: e.tensor_tensor(
            cx.duw[:, :], cx.xm[kt][:, :], cx.dwin[:, :], ALU.subtract))
        kb.step("vector", "dve", 1, lambda e: e.tensor_tensor(
            cx.duw[:, :], cx.duw[:, :], cx.dlw[:, :], ALU.mult))
        if phase == 1:
            kb.step("vector", "dve", 1, lambda e, kt=kt: e.tensor_scalar(
                cx.xm[kt][:, :], cx.duw[:, :],
                cx.mcols[:, cfoff + kt:cfoff + kt + 1], cx.mcols[:, soff + kt:soff + kt + 1],
                ALU.mult, ALU.add))
        else:
            kb.step("vector", "dve", 1, lambda e, kt=kt: e.tensor_scalar(
                cx.h2f8[:, 0:2048], cx.duw[:, :],
                cx.mcols[:, cfoff + kt:cfoff + kt + 1], cx.mcols[:, soff + kt:soff + kt + 1],
                ALU.mult, ALU.add))
            kb.wait_all("gpsimd")
            kb.step("gpsimd", "dmaG", 16, lambda e, kt=kt: e.dma_start(
                key("h2_hbm")[kt, :, :], cx.h2f8[:, 0:2048]))
    if phase == 1:
        kb.step("vector", "dve", 1, lambda e: e.tensor_scalar(
            cx.xmt[:], cx.xtl[:], cx.fcols[:, 0:1], None, ALU.mult))


def emit_slot_head(kb, cx, key, s):
    kb.wait_all("gpsimd")
    for kt in range(8):
        kb.step("gpsimd", "dmaG", 16, lambda e, kt=kt: e.dma_start(
            cx.xtl[:, 4 * kt:4 * kt + 3], cx.xm[kt][:, 2045:2048]))
    kb.wait_all("vector")
    kb.step("vector", "dve", 1, lambda e: e.tensor_copy(cx.w32[:, 0:32], cx.xtl[:]))
    kb.step("vector", "dve", 1, lambda e: e.tensor_scalar(
        cx.w32[:, 32:64], cx.w32[:, 0:32], cx.fcols[:, 1:2], None, ALU.mult))
    kb.step("vector", "dve", 1, lambda e: e.tensor_scalar(
        cx.hrun[:], cx.hrun[:], cx.fcols[0:16, 1:2], None, ALU.mult))
    kb.wait_all("gpsimd")
    kb.step("gpsimd", "dmaG", 16, lambda e, s=s: e.dma_start(key(f"arinX{s}")[:], cx.w32[:, 32:64]))
    kb.step("gpsimd", "dmaG", 16, lambda e, s=s: e.dma_start(key(f"arinH{s}")[:], cx.hrun[:]))
    if FUSED:
        kb.wait("gpsimd", "dmaG")
        kb.step("gpsimd", "cc", 1, lambda e, s=s: e.collective_compute(
            "AllReduce", ALU.add, replica_groups=[[0, 1], [2, 3], [4, 5], [6, 7]],
            ins=[key(f"arinH{s}")[:].opt()], outs=[key(f"aroutH{s}")[:].opt()]))
        kb.step("gpsimd", "cc", 1, lambda e, s=s: e.collective_compute(
            "AllReduce", ALU.add, replica_groups=[[0, 1], [2, 3], [4, 5], [6, 7]],
            ins=[key(f"arinX{s}")[:].opt()], outs=[key(f"aroutX{s}")[:].opt()]))
    if s > 0:
        if FUSED:
            kb.wait("gpsimd", "cc", 2 * s)
        kb.step("gpsimd", "dmaG", 16, lambda e, s=s: e.dma_start(
            cx.w32[:, 64:96], key(f"aroutX{s - 1}")[:]))
        kb.wait_all("vector")
        kb.step("vector", "dve", 1, lambda e: e.tensor_scalar(
            cx.w32[:, 64:96], cx.w32[:, 64:96], cx.fcols[:, 0:1], None, ALU.mult))
        kb.step("vector", "dve", 1, lambda e: e.tensor_tensor(
            cx.w32[:, 32:64], cx.w32[:, 32:64], cx.w32[:, 64:96], ALU.add))
        kb.step("vector", "dve", 1, lambda e: e.tensor_copy(cx.xmt[:], cx.w32[:, 32:64]))


def emit_ar_consume(kb, cx, key, s):
    if FUSED:
        kb.wait("gpsimd", "cc", 2 * s + 2)
    kb.wait_all("gpsimd")
    kb.step("gpsimd", "dmaG", 16, lambda e, s=s: e.dma_start(cx.hrun[:], key(f"aroutH{s}")[:]))
    kb.wait_all("vector")
    kb.step("vector", "dve", 1, lambda e: e.tensor_scalar(
        cx.hrun[:], cx.hrun[:], cx.fcols[0:16, 0:1], None, ALU.mult))


def emit_layer(kb, cx, key, pre_p3=None):
    nc = kb.nc

    kb.wait("gpsimd", "dmaL")
    kb.wait("sync", "dve")
    kb.step("sync", "dmaL", 16, lambda e: e.dma_start(cx.convw[:], key("cw")[:]))
    kb.step("sync", "dmaL", 16, lambda e: e.dma_start(cx.convb[:], key("cb")[:]))
    for h in range(2):
        kb.wait("sync", "dve")
        kb.step("sync", "dmaL", 16, lambda e, h=h: e.dma_start(cx.w32[0:16, :], key("alogT")[:, 1024 * h:1024 * (h + 1)]))
        kb.wait("scalar", "dmaL")
        kb.step("scalar", "act", 1, lambda e: e.activation(cx.w32[0:16, :], cx.w32[0:16, :], ACT.Exp))
        kb.wait("vector", "act")
        kb.step("vector", "dve", 1, lambda e, h=h: e.tensor_copy(cx.alpha16[0:16, 1024 * h:1024 * (h + 1)], cx.w32[0:16, :]))
    for h in range(2):
        kb.wait("sync", "dve")
        kb.step("sync", "dmaL", 16, lambda e, h=h: e.dma_start(cx.w32[0:1, :], key("dpar")[:, 1024 * h:1024 * (h + 1)]))
        kb.wait("vector", "dmaL")
        kb.step("vector", "dve", 1, lambda e, h=h: e.tensor_copy(cx.ustg[0:1, 0:1024], cx.w32[0:1, :]))
        kb.wait("gpsimd", "dve")
        kb.step("gpsimd", "dmaG", 16, lambda e, h=h: e.dma_start(cx.alpha16[16:17, 1024 * h:1024 * (h + 1)], cx.ustg[0:1, 0:1024]))
        kb.wait("vector", "dmaG")
        kb.wait("sync", "dve")
        kb.step("sync", "dmaL", 16, lambda e, h=h: e.dma_start(cx.w32[0:128, :], key("wdtTb")[:, 1024 * h:1024 * (h + 1)]))
        kb.wait("vector", "dmaL")
        kb.step("vector", "dve", 1, lambda e, h=h: e.tensor_copy(cx.wdtb[:, 1024 * h:1024 * (h + 1)], cx.w32[0:128, :]))
    for h in range(2):
        w0, w1 = 1024 * h, min(16 * 96, 1024 * (h + 1))
        kb.wait("sync", "dve")
        kb.step("sync", "dmaL", 16, lambda e, w0=w0, w1=w1: e.dma_start(cx.w32[:, 0:w1 - w0], key("wxT")[:, w0:w1]))
        kb.wait("vector", "dmaL")
        kb.step("vector", "dve", 1, lambda e, w0=w0, w1=w1: e.tensor_copy(cx.wxtb[:, w0:w1], cx.w32[:, 0:w1 - w0]))
    for n in range(NST + 1):
        kb.wait("gpsimd", "pe")
        kb.wait("gpsimd", "dve")
        kb.step("gpsimd", "dmaG", 16, lambda e, n=n: e.dma_start(cx.ustg[0:1, 0:CC], cx.alpha16[n:n + 1, :]))
        kb.wait("tensor", "dmaG")
        for h in range(4):
            kb.step("tensor", "pe", 1, lambda e, h=h: e.matmul(
                (cx.psA if h < 2 else cx.psB)[:, 512 * (h % 2):512 * (h % 2 + 1)], cx.ones1[:],
                cx.ustg[0:1, 512 * h:512 * (h + 1)], start=True, stop=True))
        kb.pe_fence()
        kb.wait("vector", "pe")
        if n < NST:
            kb.step("vector", "dve", 1, lambda e, n=n: e.tensor_copy(cx.abc[n][:, :1024], cx.psA[:, :1024]))
            kb.step("vector", "dve", 1, lambda e, n=n: e.tensor_copy(cx.abc[n][:, 1024:], cx.psB[:, :1024]))
        else:
            kb.step("vector", "dve", 1, lambda e: e.tensor_copy(cx.dpb[:, :1024], cx.psA[:, :1024]))
            kb.step("vector", "dve", 1, lambda e: e.tensor_copy(cx.dpb[:, 1024:], cx.psB[:, :1024]))
    kb.wait("tensor", "dve")
    for h in range(4):
        kb.step("tensor", "pe", 1, lambda e, h=h: e.matmul(
            (cx.psA if h < 2 else cx.psB)[:, 512 * (h % 2):512 * (h % 2 + 1)], cx.rep8[:],
            cx.alpha16[0:16, 512 * h:512 * (h + 1)], start=True, stop=True))
    kb.pe_fence()
    kb.wait("vector", "pe")
    kb.step("vector", "dve", 1, lambda e: e.tensor_copy(cx.alrep[:, 0:1024], cx.psA[:, :1024]))
    kb.step("vector", "dve", 1, lambda e: e.tensor_copy(cx.alrep[:, 1024:], cx.psB[:, :1024]))
    for k in range(16):
        kb.wait("sync", "dve")
        kb.step("sync", "dmaL", 16, lambda e, k=k: e.dma_start(cx.w32[:, 0:1024], key("woT")[128 * k:128 * (k + 1), :]))
        kb.wait("vector", "dmaL")
        kb.step("vector", "dve", 1, lambda e: e.tensor_copy(cx.wos[0][:], cx.w32[:, 0:1024]))
        kb.wait("gpsimd", "dve")
        kb.step("gpsimd", "dmaG", 16, lambda e, k=k: e.dma_start(key("wo_hbm")[128 * k:128 * (k + 1), :], cx.wos[0][:]))
        kb.wait("vector", "dmaG")

    for mg in range(16):
        for k in range(8):
            kb.wait("sync", "dve")
            kb.step("sync", "dmaL", 16, lambda e, mg=mg, k=k: e.dma_start(
                cx.w32[:, 0:256], key("winT")[128 * k:128 * (k + 1), 256 * mg:256 * (mg + 1)]))
            kb.wait("vector", "dmaL")
            kb.step("vector", "dve", 1, lambda e, k=k: e.tensor_copy(cx.wkp[k][:], cx.w32[:, 0:256]))
        for mi in range(2):
            m = mg * 2 + mi
            kb.wait("tensor", "dve")
            kb.wait("tensor", "act")
            for th in range(2):
                tgt = cx.psA if th == 0 else cx.psB
                for nn in range(2):
                    for k in range(8):
                        kb.step("tensor", "pe", 1, lambda e, mi=mi, th=th, nn=nn, k=k, tgt=tgt: e.matmul(
                            tgt[:, N_MM * nn:N_MM * (nn + 1)],
                            cx.wkp[k][:, 128 * mi:128 * (mi + 1)],
                            cx.xm[k][:, 1024 * th + N_MM * nn:1024 * th + N_MM * (nn + 1)],
                            start=(k == 0), stop=(k == 7)))
            if m < 16:
                ct = m
                for k in range(8):
                    kb.step("tensor", "pe", 1, lambda e, mi=mi, k=k: e.matmul(
                        cx.psD[:, 0:4], cx.wkp[k][:, 128 * mi:128 * (mi + 1)],
                        cx.xmt[:, 4 * k:4 * (k + 1)], start=(k == 0), stop=(k == 7)))
                kb.pe_fence()
                kb.wait("vector", "pe")
                kb.wait("vector", "dmaG")
                kb.step("vector", "dve", 1, lambda e: e.tensor_copy(cx.dwin[:, 0:1024], cx.psA[:, :1024]))
                kb.step("vector", "dve", 1, lambda e: e.tensor_copy(cx.dwin[:, 1024:], cx.psB[:, :1024]))
                kb.step("vector", "dve", 1, lambda e: e.tensor_copy(cx.xhead[:, 0:3], cx.psD[:, 0:3]))
                kb.step("vector", "dve", 1, lambda e, ct=ct: e.tensor_scalar(
                    cx.ustg[:, 3:Lc], cx.dwin[:, 0:Lc - 3], cx.convw[:, 4 * ct:4 * ct + 1], None, ALU.mult))
                kb.step("vector", "dve", 1, lambda e, ct=ct: e.tensor_scalar(
                    cx.ustg[:, 0:3], cx.xhead[:, 0:3], cx.convw[:, 4 * ct:4 * ct + 1], None, ALU.mult))
                for tap in range(1, 4):
                    kb.step("vector", "dve", 1, lambda e, ct=ct, tap=tap: e.scalar_tensor_tensor(
                        cx.ustg[:, 3 - tap:Lc], cx.dwin[:, 0:Lc - 3 + tap],
                        cx.convw[:, 4 * ct + tap:4 * ct + tap + 1],
                        cx.ustg[:, 3 - tap:Lc], ALU.mult, ALU.add))
                    if tap < 3:
                        kb.step("vector", "dve", 1, lambda e, ct=ct, tap=tap: e.scalar_tensor_tensor(
                            cx.ustg[:, 0:3 - tap], cx.xhead[:, tap:3],
                            cx.convw[:, 4 * ct + tap:4 * ct + tap + 1],
                            cx.ustg[:, 0:3 - tap], ALU.mult, ALU.add))
                kb.wait("scalar", "dve")
                kb.step("scalar", "act", 1, lambda e, ct=ct: e.activation(
                    cx.dwin[:, 0:Lc], cx.ustg[:, 0:Lc], ACT.Silu, bias=cx.convb[:, ct:ct + 1], scale=1.0))
                kb.wait("gpsimd", "act")
                kb.wait("gpsimd", "dmaL")
                kb.step("gpsimd", "dmaG", 16, lambda e, ct=ct: e.dma_start(
                    key("u_hbm")[ct, :, 0:Lc], cx.dwin[:, 0:Lc]))
                kb.step("gpsimd", "dmaG", 16, lambda e, ct=ct: e.dma_start(
                    key("u_hbm")[ct, :, Lc:TP], cx.zpad[:]))
            else:
                ct = m - 16
                kb.pe_fence()
                kb.wait("scalar", "pe")
                kb.wait("scalar", "dmaG")
                kb.step("scalar", "act", 1, lambda e: e.activation(
                    cx.ustg[:, 0:1024], cx.psA[:, :1024], ACT.Silu))
                kb.step("scalar", "act", 1, lambda e: e.activation(
                    cx.ustg[:, 1024:2048], cx.psB[:, :1024], ACT.Silu))
                kb.wait("gpsimd", "act")
                kb.wait("gpsimd", "dmaL")
                kb.step("gpsimd", "dmaG", 16, lambda e, ct=ct: e.dma_start(key("z_hbm")[ct, :, 0:Lc], cx.ustg[:]))
                kb.step("gpsimd", "dmaG", 16, lambda e, ct=ct: e.dma_start(key("z_hbm")[ct, :, Lc:TP], cx.zpad[:]))

    kb.wait("vector", "pe")
    kb.step("vector", "dve", 1, lambda e: e.memset(cx.xdblb[:], 0.0))
    kb.wait("gpsimd", "dve")
    kb.step("gpsimd", "dmaG", 16, lambda e: e.dma_start(cx.xdblb[96:97, :], key("onesrow")[:]))
    kb.wait("vector", "dmaG")
    for ct in range(16):
        kb.wait("sync", "pe")
        kb.wait("sync", "dve")
        kb.wait("sync", "dmaG")
        kb.step("sync", "dmaL", 16, lambda e, ct=ct: e.dma_start(cx.ustg[:, 0:Lc], key("u_hbm")[ct, :, 0:Lc]))
        kb.wait("tensor", "dmaL")
        for nn in range(4):
            tgt = cx.psA if nn < 2 else cx.psB
            kb.step("tensor", "pe", 1, lambda e, ct=ct, nn=nn, tgt=tgt: e.matmul(
                tgt[0:96, N_MM * (nn % 2):N_MM * (nn % 2 + 1)],
                cx.wxtb[:, 96 * ct:96 * (ct + 1)],
                cx.ustg[:, N_MM * nn:N_MM * (nn + 1)],
                start=(ct == 0), stop=(ct == 15)))
    kb.pe_fence()
    kb.wait("vector", "pe")
    kb.step("vector", "dve", 1, lambda e: e.tensor_copy(cx.xdblb[0:96, 0:1024], cx.psA[0:96, :]))
    kb.step("vector", "dve", 1, lambda e: e.tensor_copy(cx.xdblb[0:96, 1024:2048], cx.psB[0:96, :]))

    for g, wr in enumerate(GRPW):
        for w in wr:
            wl = w - wr.start
            off = w * WT
            kb.wait("tensor", "dve")
            kb.wait("tensor", "dmaG")
            for ch in range(2):
                tgt = cx.psA if ch == 0 else cx.psB
                for nn in range(2):
                    kb.step("tensor", "pe", 1, lambda e, off=off, ch=ch, nn=nn, tgt=tgt: e.matmul(
                        tgt[:WT, N_MM * nn:N_MM * (nn + 1)],
                        cx.xdblb[0:64, off:off + WT],
                        cx.wdtb[0:64, 1024 * ch + N_MM * nn:1024 * ch + N_MM * (nn + 1)],
                        start=True, stop=False))
                    kb.wait("tensor", "pe")
                    kb.step("tensor", "pe", 1, lambda e, off=off, ch=ch, nn=nn, tgt=tgt: e.matmul(
                        tgt[:WT, N_MM * nn:N_MM * (nn + 1)],
                        cx.xdblb[64:128, off:off + WT],
                        cx.wdtb[64:128, 1024 * ch + N_MM * nn:1024 * ch + N_MM * (nn + 1)],
                        start=False, stop=True))
            kb.pe_fence()
            kb.wait("vector", "pe")
            for q in range(4):
                src = cx.psA if q < 2 else cx.psB
                so = 512 * (q % 2)
                kb.step("vector", "dve", 1, lambda e, src=src, so=so: e.tensor_scalar(
                    cx.w32[:WT, 0:512], src[:WT, so:so + 512], -30.0, 30.0, ALU.max, ALU.min))
                kb.wait("scalar", "dve")
                kb.step("scalar", "act", 1, lambda e: e.activation(
                    cx.w32[:WT, 512:1024], cx.w32[:WT, 0:512], ACT.Exp))
                kb.step("scalar", "act", 1, lambda e, q=q: e.activation(
                    cx.dwin[:WT, 512 * q:512 * (q + 1)], cx.w32[:WT, 512:1024],
                    ACT.Ln, bias=1.0, scale=1.0))
                kb.wait("vector", "act")
            if w == NW - 1:
                kb.step("vector", "dve", 1, lambda e: e.tensor_scalar(
                    cx.dwin[:WT, :], cx.dwin[:WT, :], cx.maskw16[:WT, :], None, ALU.mult))
            kb.wait("gpsimd", "dve")
            kb.wait("gpsimd", "dmaL")
            kb.step("gpsimd", "dmaG", 16, lambda e, w=w: e.dma_start(key("d_hbm")[w, :, :], cx.dwin[:]))
            kb.wait("tensor", "dve")
            for ch in range(2):
                sel = cx.hselA if ch == 0 else cx.hselB
                for nn in range(2):
                    kb.step("tensor", "pe", 1, lambda e, sel=sel, wl=wl, ch=ch, nn=nn, w=w, wr=wr: e.matmul(
                        cx.psC[0:32, N_MM * nn:N_MM * (nn + 1)],
                        sel[:, 32 * wl:32 * (wl + 1)],
                        cx.dwin[:, 1024 * ch + N_MM * nn:1024 * ch + N_MM * (nn + 1)],
                        start=(wl == 0 and ch == 0), stop=(w == wr.stop - 1 and ch == 1)))
        kb.pe_fence()
        kb.wait("vector", "pe")
        kb.step("vector", "dve", 1, lambda e, g=g: e.tensor_copy(cx.dmde[g][:], cx.psC[0:32, :]))
        kb.wait("tensor", "dve")

    if pre_p3 is not None:
        pre_p3()

    wog_idx = 0
    for g, wr in enumerate(GRPW):
        kb.wait("tensor", "dve")
        for which in range(2):
            i0 = cx.indA0 if which == 0 else cx.indB0
            i1 = cx.indA1 if which == 0 else cx.indB1
            gdst = cx.gA if which == 0 else cx.gB
            for ch in range(2):
                ind = i0 if ch == 0 else i1
                kb.wait("tensor", "dve")
                for nn in range(2):
                    kb.step("tensor", "pe", 1, lambda e, ind=ind, nn=nn, g=g: e.matmul(
                        cx.psA[:, 512 * nn:512 * (nn + 1)], ind[:],
                        cx.dmde[g][:, 512 * nn:512 * (nn + 1)], start=True, stop=True))
                kb.pe_fence()
                kb.wait("vector", "pe")
                kb.step("vector", "dve", 1, lambda e, ch=ch: e.tensor_tensor(
                    cx.w32[:, :], cx.alrep[:, 1024 * ch:1024 * (ch + 1)],
                    cx.psA[:, :1024], ALU.mult))
                kb.wait("scalar", "dve")
                kb.step("scalar", "act", 1, lambda e, gdst=gdst, ch=ch: e.activation(
                    gdst[:, 1024 * ch:1024 * (ch + 1)], cx.w32[:], ACT.Exp, bias=0.0, scale=-1.0))
                kb.wait("vector", "act")
        kb.wait("vector", "act")

        for w in wr:
            wl = w - wr.start
            off = w * WT
            wo_g = WOG[wog_idx]
            wo_l = w - wo_g.start
            kb.wait("sync", "dve")
            kb.wait("sync", "act")
            kb.wait("sync", "dmaG")
            kb.step("sync", "dmaL", 16, lambda e, w=w: e.dma_start(cx.dwin[:], key("d_hbm")[w, :, :]))
            kb.wait("sync", "pe")
            for cb in range(16):
                kb.step("sync", "dmaL", 16, lambda e, cb=cb, off=off: e.dma_start(
                    cx.ustg[:, WT * cb:WT * (cb + 1)], key("u_hbm")[cb, :, off:off + WT]))
            uld = kb.cnt["dmaL"]
            kb.wait("gpsimd", "dve")
            kb.wait("gpsimd", "act")
            kb.wait("gpsimd", "dmaG", None)
            for cb in range(16):
                kb.step("gpsimd", "dmaG", 16, lambda e, cb=cb, off=off: e.dma_start(
                    cx.zsil[:, WT * cb:WT * (cb + 1)], key("z_hbm")[cb, :, off:off + WT]))
            kb.wait("tensor", "dmaL", uld)
            kb.wait("tensor", "dve")
            for rnd in range(2):
                for cb8 in range(8):
                    cb = rnd * 8 + cb8
                    kb.step("tensor", "pe", 1, lambda e, cb=cb, cb8=cb8: e.transpose(
                        cx.psT[0:128, 128 * cb8:128 * (cb8 + 1)],
                        cx.ustg[:, WT * cb:WT * cb + 128], cx.ident[:]))
                kb.pe_fence()
                kb.wait("vector", "pe")
                kb.step("vector", "dve", 1, lambda e, rnd=rnd: e.tensor_copy(
                    cx.duw[:WT, 1024 * rnd:1024 * (rnd + 1)], cx.psT[:WT, :]))
                if rnd == 0:
                    kb.wait("tensor", "dve")
            kb.step("vector", "dve", 1, lambda e: e.tensor_tensor(cx.yw[:WT, :], cx.duw[:WT, :], cx.dpb[:WT, :], ALU.mult))
            kb.wait("tensor", "dve")
            for ch in range(2):
                tgt = cx.psA if ch == 0 else cx.psB
                for nn in range(2):
                    kb.step("tensor", "pe", 1, lambda e, ch=ch, nn=nn, tgt=tgt: e.matmul(
                        tgt[:, N_MM * nn:N_MM * (nn + 1)], cx.trilmid[:],
                        cx.dwin[:, 1024 * ch + N_MM * nn:1024 * ch + N_MM * (nn + 1)],
                        start=True, stop=True))
            kb.pe_fence()
            kb.wait("vector", "pe")
            kb.step("vector", "dve", 1, lambda e: e.tensor_copy(cx.dlw[:, 0:1024], cx.psA[:, :1024]))
            kb.step("vector", "dve", 1, lambda e: e.tensor_copy(cx.dlw[:, 1024:], cx.psB[:, :1024]))
            kb.step("vector", "dve", 1, lambda e: e.tensor_tensor(cx.duw[:WT, :], cx.dwin[:WT, :], cx.duw[:WT, :], ALU.mult))
            kb.wait("tensor", "dve")
            kb.step("tensor", "pe", 1, lambda e, off=off: e.transpose(
                cx.psT[0:128, 0:32], cx.xdblb[64:96, off:off + 128], cx.ident[64:96, 64:96]))
            kb.pe_fence()
            kb.wait("vector", "pe")
            kb.step("vector", "dve", 1, lambda e: e.tensor_copy(cx.bct[:WT, :], cx.psT[:WT, 0:32]))
            kb.step("vector", "dve", 1, lambda e: e.tensor_copy(cx.bctf[:], cx.bct[:]))
            kb.wait("gpsimd", "act")
            kb.wait("gpsimd", "dve")
            kb.wait("gpsimd", "dmaG", None)
            kb.step("gpsimd", "dmaG", 16, lambda e, wl=wl: e.dma_start(cx.gsl[:], cx.gA[16 * wl:16 * (wl + 1), :]))
            kb.wait("vector", "dmaG")
            kb.step("vector", "dve", 1, lambda e: e.tensor_tensor(
                cx.gsl[:], cx.hrun[:], cx.gsl[:], ALU.mult))

            for n in range(NST):
                kb.wait("vector", "pe")
                kb.wait("vector", "act")
                kb.step("vector", "dve", 1, lambda e, n=n: e.tensor_tensor(
                    cx.dwin[:], cx.dlw[:], cx.abc[n][:], ALU.mult))
                kb.wait("scalar", "dve")
                kb.step("scalar", "act", 1, lambda e: e.activation(cx.rg[:], cx.dwin[:], ACT.Exp))
                kb.wait("vector", "act")
                kb.wait("vector", "dmaG")
                kb.step("vector", "dve", 1, lambda e: e.tensor_tensor(
                    cx.ww[:WT, :], cx.rg[:WT, :], cx.duw[:WT, :], ALU.mult))
                kb.step("vector", "dve", 1, lambda e, n=n: e.tensor_scalar(
                    cx.statn[:], cx.trilsum[:], cx.bctf[:, n:n + 1], None, ALU.mult))
                kb.wait("gpsimd", "dve")
                kb.wait("gpsimd", "dmaG", None)
                kb.step("gpsimd", "dmaG", 16, lambda e, n=n: e.dma_start(
                    cx.ww[127:128, :], cx.gsl[n:n + 1, :]))
                kb.wait("scalar", "dve")
                kb.step("scalar", "act", 1, lambda e: e.activation(cx.rg[:], cx.dwin[:], ACT.Exp, bias=0.0, scale=-1.0))
                kb.wait("tensor", "act")
                wd, wg = kb.cnt["dve"], kb.cnt["dmaG"]
                kb.wait("tensor", "dve", wd)
                kb.wait("tensor", "dmaG", wg)
                for ch in range(2):
                    tgt = cx.psA if ch == 0 else cx.psB
                    for nn in range(2):
                        kb.step("tensor", "pe", 1, lambda e, ch=ch, nn=nn, tgt=tgt: e.matmul(
                            tgt[:, N_MM * nn:N_MM * (nn + 1)], cx.statn[:],
                            cx.ww[:, 1024 * ch + N_MM * nn:1024 * ch + N_MM * (nn + 1)],
                            start=True, stop=True))
                mmd = kb.pe_fence()
                kb.wait("vector", "pe", mmd)
                kb.step("vector", "dve", 1, lambda e: e.tensor_tensor(
                    cx.ww[:, 0:1024], cx.rg[:, 0:1024], cx.psA[:, :], ALU.mult))
                kb.step("vector", "dve", 1, lambda e: e.tensor_tensor(
                    cx.ww[:, 1024:], cx.rg[:, 1024:], cx.psB[:, :], ALU.mult))
                kb.wait("gpsimd", "dve")
                kb.step("gpsimd", "dmaG", 16, lambda e, n=n: e.dma_start(cx.hraw[n:n + 1, :], cx.ww[127:128, :]))
                kb.step("vector", "dve", 1, lambda e, n=n: e.scalar_tensor_tensor(
                    cx.yw[:WT, :], cx.ww[:WT, :], cx.bctf[:WT, NST + n:NST + n + 1], cx.yw[:WT, :],
                    ALU.mult, ALU.add))
            kb.wait("gpsimd", "dve")
            kb.wait("gpsimd", "dmaG", None)
            kb.step("gpsimd", "dmaG", 16, lambda e, wl=wl: e.dma_start(cx.gsl[:], cx.gB[16 * wl:16 * (wl + 1), :]))
            kb.wait("vector", "dmaG")
            kb.step("vector", "dve", 1, lambda e: e.tensor_tensor(
                cx.hrun[:], cx.hraw[:], cx.gsl[:], ALU.mult))
            kb.wait("tensor", "dve")
            for rnd in range(2):
                for cb8 in range(8):
                    cb = rnd * 8 + cb8
                    kb.step("tensor", "pe", 1, lambda e, cb=cb, cb8=cb8: e.transpose(
                        cx.psT[0:128, 128 * cb8:128 * (cb8 + 1)],
                        cx.yw[:, 128 * cb:128 * (cb + 1)], cx.ident[:]))
                kb.pe_fence()
                kb.wait("vector", "pe")
                for cb8 in range(8):
                    cb = rnd * 8 + cb8
                    kb.step("vector", "dve", 1, lambda e, cb=cb, cb8=cb8, wo_l=wo_l: e.tensor_tensor(
                        cx.wrhs[cb][:, WT * wo_l:WT * (wo_l + 1)],
                        cx.psT[:, 128 * cb8:128 * cb8 + WT],
                        cx.zsil[:, WT * cb:WT * (cb + 1)], ALU.mult))
                if rnd == 0:
                    kb.wait("tensor", "dve")
            if w == WOG[wog_idx].stop - 1:
                emit_wout(kb, cx, key, WOG[wog_idx])
                wog_idx += 1


def emit_wout(kb, cx, key, wog):
    vw = len(wog) * WT if wog.stop <= 16 else LASTV
    goff = wog.start * WT
    kb.wait("tensor", "dve")
    kb.wait("sync", "dmaG")
    slots = [(cx.psA, 0), (cx.psA, 512), (cx.psB, 0), (cx.psB, 512),
             (cx.psC, 0), (cx.psC, 512), (cx.psD, 0)]
    n0 = 0
    n1 = vw
    for mpass, ms in enumerate((range(0, 7), range(7, 8))):
        for cb in range(16):
            sl = cx.wos[cb % 2]
            kb.wait("sync", "pe")
            kb.wait("sync", "dve")
            kb.step("sync", "dmaL", 16, lambda e, cb=cb, sl=sl: e.dma_start(
                sl[:], key("wo_hbm")[128 * cb:128 * (cb + 1), :]))
            kb.wait("tensor", "dmaL")
            for m in ms:
                ps, po = slots[m - ms.start]
                kb.step("tensor", "pe", 1, lambda e, m=m, cb=cb, n0=n0, n1=n1, ps=ps, po=po, sl=sl: e.matmul(
                    ps[:, po:po + (n1 - n0)],
                    sl[:, 128 * m:128 * (m + 1)],
                    cx.wrhs[cb][:, n0:n1], start=(cb == 0), stop=(cb == 15)))
        kb.pe_fence()
        kb.wait("vector", "pe")
        for m in ms:
            ps, po = slots[m - ms.start]
            kb.step("vector", "dve", 1, lambda e, m=m, goff=goff, n0=n0, n1=n1, ps=ps, po=po: e.tensor_tensor(
                cx.xm[m][:, goff + n0:goff + n1], cx.xm[m][:, goff + n0:goff + n1],
                ps[:, po:po + (n1 - n0)], ALU.add))
        kb.wait("tensor", "dve")


def emit_x2(kb, cx, key):
    for kt in range(8):
        for h in range(2):
            kb.wait_all("sync")
            kb.step("sync", "dmaL", 16, lambda e, kt=kt, h=h: e.dma_start(
                cx.w32[:, 0:1024], key("XA")[128 * kt:128 * (kt + 1), 3 + 1024 * h:3 + 1024 * (h + 1)]))
            kb.wait("vector", "dmaL")
            kb.step("vector", "dve", 1, lambda e, kt=kt, h=h: e.scalar_tensor_tensor(
                cx.w32[:, 0:1024], cx.xm[kt][:, 1024 * h:1024 * (h + 1)],
                cx.mcols[:, 16 + kt:17 + kt], cx.w32[:, 0:1024], ALU.mult, ALU.add))
            kb.step("vector", "dve", 1, lambda e, kt=kt, h=h: e.tensor_copy(
                cx.xm[kt][:, 1024 * h:1024 * (h + 1)], cx.w32[:, 0:1024]))
            kb.wait_all("gpsimd")
            kb.step("gpsimd", "dmaG", 16, lambda e, kt=kt, h=h: e.dma_start(
                key("x2_hbm")[kt, :, 1024 * h:1024 * (h + 1)], cx.w32[:, 0:1024]))


def emit_ffn_wcast(kb, cx, key):
    kb.wait_all("sync")
    kb.step("sync", "dmaL", 16, lambda e: e.dma_start(cx.w32[:, 0:32], key("b1T")[:]))
    kb.step("sync", "dmaL", 16, lambda e: e.dma_start(cx.w32[:, 32:40], key("b2T")[:]))
    kb.wait("vector", "dmaL")
    kb.step("vector", "dve", 1, lambda e: e.tensor_copy(cx.w1p[:, 0:40], cx.w32[:, 0:40]))
    kb.step("vector", "dve", 1, lambda e: e.tensor_copy(cx.bq[:, 0:40], cx.w1p[:, 0:40]))
    for kt in range(8):
        for c in range(4):
            kb.wait_all("sync")
            kb.step("sync", "dmaL", 16, lambda e, kt=kt, c=c: e.dma_start(
                cx.w32[:, 0:1024], key("w1T")[128 * kt:128 * (kt + 1), 1024 * c:1024 * (c + 1)]))
            kb.wait("vector", "dmaL")
            kb.step("vector", "dve", 1, lambda e: e.tensor_copy(cx.w1p[:, 0:1024], cx.w32[:, 0:1024]))
            kb.wait_all("gpsimd")
            for mj in range(8):
                kb.step("gpsimd", "dmaG", 16, lambda e, kt=kt, c=c, mj=mj: e.dma_start(
                    key("w1q_hbm")[8 * c + mj, :, 128 * kt:128 * (kt + 1)],
                    cx.w1p[:, 128 * mj:128 * (mj + 1)]))
    for m in range(32):
        kb.wait_all("sync")
        kb.step("sync", "dmaL", 16, lambda e, m=m: e.dma_start(
            cx.w32[:, 0:1024], key("w2T")[128 * m:128 * (m + 1), :]))
        kb.wait("vector", "dmaL")
        kb.step("vector", "dve", 1, lambda e: e.tensor_copy(cx.w1p[:, 0:1024], cx.w32[:, 0:1024]))
        kb.wait_all("gpsimd")
        kb.step("gpsimd", "dmaG", 16, lambda e, m=m: e.dma_start(key("w2q_hbm")[m, :, :], cx.w1p[:, 0:1024]))


def emit_ffn(kb, cx, key):
    slots = [(cx.psA, 0), (cx.psA, 512), (cx.psB, 0), (cx.psB, 512),
             (cx.psC, 0), (cx.psC, 512), (cx.psD, 0)]
    for tch in range(4):
        toff = 512 * tch
        kb.wait_all("sync")
        for kt in range(8):
            kb.step("sync", "dmaL", 16, lambda e, kt=kt, toff=toff: e.dma_start(
                cx.h2f8[:, 512 * kt:512 * (kt + 1)], key("h2_hbm")[kt, :, toff:toff + 512]))
        for m in range(32):
            kb.wait_all("sync")
            kb.step("sync", "dmaL", 16, lambda e, m=m: e.dma_start(
                cx.w1p[:, 0:1024], key("w1q_hbm")[m, :, :]))
            kb.wait("tensor", "dmaL")
            kb.wait("tensor", "dve")
            for kt in range(8):
                kb.step("tensor", "pe", 1, lambda e, m=m, kt=kt: e.matmul(
                    cx.psD[:, 0:512], cx.w1p[:, 128 * kt:128 * (kt + 1)],
                    cx.h2f8[:, 512 * kt:512 * (kt + 1)], start=(kt == 0), stop=(kt == 7)))
            kb.pe_fence()
            kb.wait("vector", "pe")
            kb.step("vector", "dve", 1, lambda e, m=m: e.tensor_scalar(
                cx.dwin[:, 0:512], cx.psD[:, 0:512], cx.bq[:, m:m + 1], 0.0, ALU.add, ALU.max))
            kb.step("vector", "dve", 1, lambda e: e.tensor_copy(cx.h1q8[:], cx.dwin[:, 0:512]))
            kb.wait_all("gpsimd")
            kb.step("gpsimd", "dmaG", 16, lambda e, m=m: e.dma_start(key("h1_hbm")[m, :, :], cx.h1q8[:]))
        for dmg in (range(0, 7), range(7, 8)):
            for m in range(32):
                kb.wait_all("sync")
                kb.step("sync", "dmaL", 16, lambda e, m=m: e.dma_start(
                    cx.w1p[:, 0:512], key("h1_hbm")[m, :, :]))
                kb.step("sync", "dmaL", 16, lambda e, m=m, dmg=dmg: e.dma_start(
                    cx.w2s[:, 0:128 * len(dmg)],
                    key("w2q_hbm")[m, :, 128 * dmg.start:128 * dmg.stop]))
                kb.wait("tensor", "dmaL")
                kb.wait("tensor", "dve")
                for dm in dmg:
                    ps, po = slots[dm - dmg.start]
                    kb.step("tensor", "pe", 1, lambda e, dm=dm, m=m, ps=ps, po=po, dmg=dmg: e.matmul(
                        ps[:, po:po + 512], cx.w2s[:, 128 * (dm - dmg.start):128 * (dm - dmg.start + 1)],
                        cx.w1p[:, 0:512], start=(m == 0), stop=(m == 31)))
            kb.pe_fence()
            kb.wait("vector", "pe")
            for dm in dmg:
                ps, po = slots[dm - dmg.start]
                kb.wait_all("sync")
                kb.step("sync", "dmaL", 16, lambda e, dm=dm, toff=toff: e.dma_start(
                    cx.w32[:, 512:1024], key("x2_hbm")[dm, :, toff:toff + 512]))
                kb.wait_all("gpsimd")
                kb.step("gpsimd", "dmaG", 16, lambda e, dm=dm, toff=toff: e.dma_start(
                    cx.dlw[:, 0:512], key("maskT")[dm, :, toff:toff + 512]))
                kb.wait("vector", "dmaL")
                kb.wait("vector", "dmaG")
                kb.step("vector", "dve", 1, lambda e, dm=dm, ps=ps, po=po: e.tensor_scalar(
                    cx.dwin[:, 0:512], ps[:, po:po + 512], cx.bq[:, 32 + dm:33 + dm], None, ALU.add))
                kb.step("vector", "dve", 1, lambda e: e.tensor_tensor(
                    cx.w32[:, 0:512], cx.dwin[:, 0:512], cx.dlw[:, 0:512], ALU.mult))
                kb.step("vector", "dve", 1, lambda e, dm=dm: e.scalar_tensor_tensor(
                    cx.w32[:, 512:1024], cx.w32[:, 0:512], cx.mcols[:, 40 + dm:41 + dm],
                    cx.w32[:, 512:1024], ALU.mult, ALU.add))
                kb.wait_all("gpsimd")
                kb.step("gpsimd", "dmaG", 16, lambda e, dm=dm, toff=toff: e.dma_start(
                    key("OUT")[dm, :, toff:toff + 512], cx.w32[:, 512:1024]))


def build_program(fused):
    nc = bass.Bass("TRN2", target_bir_lowering=False)
    dram = {}

    def d(name, shape, dt, kind="Internal"):
        dname = "i_" + name if kind == "ExternalInput" else name
        dram[name] = nc.dram_tensor(dname, shape, dt, kind=kind)
        return dram[name]

    d("XA", [1024, 2051], F32, kind="ExternalInput")
    d("ccol", [128, 8], F32, kind="ExternalInput")
    d("wmod", [1024, 6144], F32, kind="ExternalInput")
    d("fcols", [128, 2], F32, kind="ExternalInput")
    for s in range(NSLOT):
        d(f"winT{s}", [1024, 4096], F32, kind="ExternalInput")
        d(f"wxT{s}", [128, 1536], F32, kind="ExternalInput")
        d(f"wdtTb{s}", [128, 2048], F32, kind="ExternalInput")
        d(f"woT{s}", [2048, 1024], F32, kind="ExternalInput")
        d(f"cw{s}", [128, 64], F32, kind="ExternalInput")
        d(f"cb{s}", [128, 16], F32, kind="ExternalInput")
        d(f"alogT{s}", [16, 2048], F32, kind="ExternalInput")
        d(f"dpar{s}", [1, 2048], F32, kind="ExternalInput")
        if fused:
            d(f"arinH{s}", [16, 2048], F16)
            d(f"aroutH{s}", [16, 2048], F16)
            d(f"arinX{s}", [128, 32], F32)
            d(f"aroutX{s}", [128, 32], F32)
        else:
            d(f"arinH{s}", [16, 2048], F16, kind="ExternalOutput")
            d(f"aroutH{s}", [16, 2048], F16, kind="ExternalInput")
            d(f"arinX{s}", [128, 32], F32, kind="ExternalOutput")
            d(f"aroutX{s}", [128, 32], F32, kind="ExternalInput")
    d("w1T", [1024, 4096], F32, kind="ExternalInput")
    d("b1T", [128, 32], F32, kind="ExternalInput")
    d("w2T", [4096, 1024], F32, kind="ExternalInput")
    d("b2T", [128, 8], F32, kind="ExternalInput")
    d("maskT", [8, 128, 2048], F16, kind="ExternalInput")
    for n, a in host_consts().items():
        dt = {np.dtype("float32"): F32, np.dtype(ml_dtypes.bfloat16): BF16,
              np.dtype("float16"): F16}[a.dtype]
        d(n, list(a.shape), dt, kind="ExternalInput")
    d("u_hbm", [16, 128, TP], F16)
    d("z_hbm", [16, 128, TP], F16)
    d("d_hbm", [17, 128, 2048], F16)
    d("wo_hbm", [2048, 1024], F16)
    d("x2_hbm", [8, 128, 2048], F32)
    d("h2_hbm", [8, 128, 2048], F8)
    d("h1_hbm", [32, 128, 512], F8)
    d("w1q_hbm", [32, 128, 1024], F8)
    d("w2q_hbm", [32, 128, 1024], F8)
    d("OUT", [8, 128, 2048], F32, kind="ExternalOutput")
    if DEBUG:
        d("DBGM", [128, 48], F32, kind="ExternalOutput")
        d("DBGH", [8, 128, 2048], F16, kind="ExternalOutput")
        d("DBGS", [8, 128, 2048], F16, kind="ExternalOutput")
        d("DBGX2", [8, 128, 2048], F32, kind="ExternalOutput")
        d("DBGH2", [8, 128, 2048], F8, kind="ExternalOutput")

    kb = KB(nc)
    cx = Ctx(nc)

    def mkkey(s):
        def key2(name):
            if name in ("winT", "wxT", "wdtTb", "woT", "cw", "cb", "alogT", "dpar"):
                return dram[f"{name}{s}"]
            return dram[name]
        return key2

    gkey = mkkey(0)

    with nc.Block() as block:
        emit_load_consts(kb, cx, gkey)
        emit_modvecs(kb, cx, gkey)
        if DEBUG:
            kb.wait_all("gpsimd")
            kb.step("gpsimd", "dmaG", 16, lambda e: e.dma_start(dram["DBGM"][:], cx.mcols[:]))
        emit_ln(kb, cx, gkey, phase=1)
        if DEBUG:
            kb.wait_all("gpsimd")
            for kt in range(8):
                kb.step("gpsimd", "dmaG", 16, lambda e, kt=kt: e.dma_start(dram["DBGH"][kt, :, :], cx.xm[kt][:]))
        for s in range(NSLOT):
            key = mkkey(s)
            emit_slot_head(kb, cx, key, s)
            emit_layer(kb, cx, key, pre_p3=lambda s=s, key=key: emit_ar_consume(kb, cx, key, s))
        if DEBUG:
            kb.wait_all("gpsimd")
            for kt in range(8):
                kb.step("gpsimd", "dmaG", 16, lambda e, kt=kt: e.dma_start(dram["DBGS"][kt, :, :], cx.xm[kt][:]))
        emit_x2(kb, cx, gkey)
        emit_ln(kb, cx, gkey, phase=2)
        if DEBUG:
            kb.wait_all("sync")
            kb.wait_all("gpsimd")
            kb.step("gpsimd", "dmaG", 16, lambda e: e.dma_start(dram["DBGX2"][:], dram["x2_hbm"][:]))
            kb.step("gpsimd", "dmaG", 16, lambda e: e.dma_start(dram["DBGH2"][:], dram["h2_hbm"][:]))
        emit_ffn_wcast(kb, cx, gkey)
        emit_ffn(kb, cx, gkey)
        kb.wait("gpsimd", "dmaG")
        kb.wait("sync", "dmaL")
        kb.run_into_block(block)
    return nc


def host_inputs(inputs):
    f32 = np.float32
    x = np.asarray(inputs["L0_attn_x"], f32)
    W_in = np.asarray(inputs["L0_attn_W_in"], f32)
    conv_w = np.asarray(inputs["L0_attn_conv_w"], f32)
    conv_b = np.asarray(inputs["L0_attn_conv_b"], f32)
    W_x = np.asarray(inputs["L0_attn_W_x"], f32)
    W_dt = np.asarray(inputs["L0_attn_W_dt"], f32)
    b_dt = np.asarray(inputs["L0_attn_b_dt"], f32)
    A_log = np.asarray(inputs["L0_attn_A_log"], f32)
    D_par = np.asarray(inputs["L0_attn_D_par"], f32)
    W_out = np.asarray(inputs["L0_attn_W_out"], f32)
    w1 = np.asarray(inputs["L0_ffn_w1"], f32)
    b1 = np.asarray(inputs["L0_ffn_b1"], f32)
    w2 = np.asarray(inputs["L0_ffn_w2"], f32)
    b2 = np.asarray(inputs["L0_ffn_b2"], f32)
    c = np.asarray(inputs["L0_c"], f32)
    wmod = np.asarray(inputs["L0_wmod"], f32)

    def layer_set(i):
        if i is None:
            return {
                "winT": np.zeros((1024, 4096), f32), "wxT": np.zeros((128, 1536), f32),
                "wdtTb": np.zeros((128, 2048), f32), "woT": np.zeros((2048, 1024), f32),
                "cw": np.zeros((128, 64), f32), "cb": np.zeros((128, 16), f32),
                "alogT": np.zeros((16, 2048), f32), "dpar": np.zeros((1, 2048), f32),
            }
        return {
            "winT": np.ascontiguousarray(W_in[i].T),
            "wxT": np.ascontiguousarray(W_x[i].T.reshape(16, 128, 96).transpose(1, 0, 2).reshape(128, 16 * 96)),
            "wdtTb": np.ascontiguousarray(np.concatenate(
                [W_dt[i].T, np.zeros((32, 2048), f32), b_dt[i][None, :], np.zeros((31, 2048), f32)], 0)),
            "woT": np.ascontiguousarray(W_out[i].T),
            "cw": np.ascontiguousarray(conv_w[i].reshape(16, 128, 4).transpose(1, 0, 2).reshape(128, 64)),
            "cb": np.ascontiguousarray(conv_b[i].reshape(16, 128).T),
            "alogT": np.ascontiguousarray(A_log[i].T),
            "dpar": np.ascontiguousarray(D_par[i][None, :]),
        }

    lsets = [layer_set(i) for i in range(8)]
    zset = layer_set(None)
    consts = host_consts()

    import jax
    _k42 = lambda: jax.random.key(42, impl="threefry2x32")
    try:
        with jax.default_device(jax.local_devices(backend="cpu")[0]):
            keep = np.asarray(jax.random.bernoulli(_k42(), 0.8, (4, 4096, 1024)))
    except Exception:
        keep = np.asarray(jax.random.bernoulli(_k42(), 0.8, (4, 4096, 1024)))
    maskf = keep.astype(np.float16) * np.float16(1.25)

    shared = {
        "wmod": np.ascontiguousarray(wmod),
        "w1T": np.ascontiguousarray(w1.T),
        "b1T": np.ascontiguousarray(b1.reshape(32, 128).T),
        "w2T": np.ascontiguousarray(w2.T),
        "b2T": np.ascontiguousarray(b2.reshape(8, 128).T),
    }
    shared.update(consts)

    per_core = []
    for core in range(8):
        b, hf = core // 2, core % 2
        off = 2048 * hf
        XA = np.zeros((1024, 2051), f32)
        XA[:, 3:] = x[b, off:off + 2048].T
        if hf:
            XA[:, 0:3] = x[b, 2045:2048].T
        fcols = np.zeros((128, 2), f32)
        fcols[:, 0] = float(hf)
        fcols[:, 1] = 1.0 - float(hf)
        m = {
            "XA": XA,
            "ccol": np.ascontiguousarray(c[b].reshape(8, 128).T),
            "fcols": fcols,
            "maskT": np.ascontiguousarray(maskf[b, off:off + 2048].T.reshape(8, 128, 2048)),
        }
        m.update(shared)
        for s in range(NSLOT):
            li = s if hf == 0 else s - 1
            src = zset if (li < 0 or li > 7) else lsets[li]
            for kname, arr in src.items():
                m[f"{kname}{s}"] = arr
        if not FUSED:
            for s in range(NSLOT):
                m[f"aroutH{s}"] = np.zeros((16, 2048), np.float16)
                m[f"aroutX{s}"] = np.zeros((128, 32), f32)
        per_core.append(m)
    return per_core


def _prefix(per_core):
    return [{("i_" + k): v for k, v in m.items()} for m in per_core]


def kernel(**inputs):
    per_core = host_inputs(inputs)
    nc = build_program(FUSED)
    out = np.empty((4, 4096, 1024), np.float32)
    if FUSED:
        res = run_bass_kernel_spmd(nc, _prefix(per_core), core_ids=list(range(8)))
        for core in range(8):
            b, hf = core // 2, core % 2
            arr = np.asarray(res.results[core]["OUT"])
            out[b, 2048 * hf:2048 * (hf + 1)] = arr.transpose(2, 0, 1).reshape(2048, 1024)
    else:
        res1 = run_bass_kernel_spmd(nc, _prefix(per_core), core_ids=list(range(8)))
        per2 = []
        for core in range(8):
            b, hf = core // 2, core % 2
            m = dict(per_core[core])
            if hf == 1:
                r0 = res1.results[core - 1]
                for s in range(NSLOT):
                    m[f"aroutH{s}"] = np.asarray(r0[f"arinH{s}"])
                    m[f"aroutX{s}"] = np.asarray(r0[f"arinX{s}"])
            per2.append(m)
        res2 = run_bass_kernel_spmd(nc, _prefix(per2), core_ids=list(range(8)))
        for core in range(8):
            b, hf = core // 2, core % 2
            r = (res1 if hf == 0 else res2).results[core]
            arr = np.asarray(r["OUT"])
            out[b, 2048 * hf:2048 * (hf + 1)] = arr.transpose(2, 0, 1).reshape(2048, 1024)
    return out


# version 03
# speedup vs baseline: 1.3494x; 1.3494x over previous
import numpy as np
import ml_dtypes
import concourse.bass as bass
import concourse.mybir as mybir
from concourse.bass_utils import run_bass_kernel_spmd

F32 = mybir.dt.float32
F16 = mybir.dt.float16
BF16 = mybir.dt.bfloat16
F8 = mybir.dt.float8e4
ALU = mybir.AluOpType
ACT = mybir.ActivationFunctionType

FUSED = True
DEBUG = False

Lc = 2048
D = 1024
CC = 2048
NST = 16
WT = 127
NW = 17
LASTV = Lc - (NW - 1) * WT
N_MM = 512
TP = 2176
NSLOT = 9
GRPW = [range(0, 8), range(8, 16), range(16, 17)]
WOG = [range(2 * i, 2 * i + 2) for i in range(8)] + [range(16, 17)]
ENGINES = ("sync", "vector", "scalar", "tensor", "gpsimd")
PERSLOT = ("winT", "wxT", "wdtTb", "woT", "cw", "cb", "alogT", "dpar",
           "arinH", "aroutH", "arinX", "aroutX")


class KB:
    PARANOID = ()

    def __init__(self, nc):
        self.nc = nc
        self.ops = {e: [] for e in ENGINES}
        self.cnt = {}
        self.sems = {}
        self.lastw = {}

    def sem(self, name):
        if name not in self.sems:
            self.sems[name] = self.nc.semaphore(name).__enter__()
            self.cnt[name] = 0
        return self.sems[name]

    def wait(self, engine, name, value=None):
        h = self.sem(name)
        v = self.cnt[name] if value is None else value
        if v > 0:
            self.ops[engine].append(lambda eng, h=h, v=v: eng.wait_ge(h, v))

    REGION = True
    OWN = {"vector": "dve", "scalar": "act", "tensor": "pe"}

    def wait_all(self, engine):
        wl = self.lastw.setdefault(engine, {})
        for nm in list(self.cnt):
            v = self.cnt.get(nm, 0)
            if v > 0 and nm != self.OWN.get(engine) and wl.get(nm, 0) < v:
                self.wait(engine, nm)
                wl[nm] = v

    def step(self, engine, name, amount, fn):
        if (self.PARANOID is True or (self.PARANOID and engine in self.PARANOID)) and self.REGION:
            self.wait_all(engine)
        if engine == "tensor" and name == "pe":
            self.sem(name)
            self.ops[engine].append(lambda eng, fn=fn: fn(eng))
            return self.cnt[name]
        h = self.sem(name)
        self.cnt[name] = self.cnt.get(name, 0) + amount
        if engine in ("vector", "scalar", "tensor") and name != "dmaG" and name != "dmaL":
            def wrapped(eng, fn=fn, h=h, a=amount):
                fn(eng)
                eng.drain().then_inc(h, a)
        else:
            def wrapped(eng, fn=fn, h=h, a=amount):
                fn(eng).then_inc(h, a)
        self.ops[engine].append(wrapped)
        return self.cnt[name]

    def tensor_drain(self):
        self.ops["tensor"].append(lambda eng: eng.drain())

    def pe_fence(self):
        h = self.sem("pe")
        self.cnt["pe"] = self.cnt.get("pe", 0) + 1
        def wrapped(eng, h=h):
            eng.drain().then_inc(h, 1)
        self.ops["tensor"].append(wrapped)
        return self.cnt["pe"]

    def run_into_block(self, block):
        for e in ENGINES:
            ops = self.ops[e]
            if not ops:
                continue
            def make(ops):
                def body(eng):
                    for f in ops:
                        f(eng)
                return body
            getattr(block, e)(make(ops))


def sbuf(nc, name, shape, dt):
    return nc.sbuf_tensor(name, shape, dt).__enter__()


def pst(nc, name, shape, dt):
    return nc.psum_tensor(name, shape, dt).__enter__()


class Ctx:
    def __init__(self, nc):
        self.nc = nc
        self.xm = [sbuf(nc, f"xm{i}", [128, Lc], F16) for i in range(8)]
        self.ustg = sbuf(nc, "ustg", [128, 2048], F16)
        self.zsil = sbuf(nc, "zsil", [128, 16 * WT], F16)
        self.dwin = sbuf(nc, "dwin", [128, CC], F16)
        self.dlw = sbuf(nc, "dlw", [128, CC], F16)
        self.duw = sbuf(nc, "duw", [128, CC], F16)
        self.rg = sbuf(nc, "rg", [128, CC], BF16)
        self.ww = sbuf(nc, "ww", [128, CC], BF16)
        self.yw = sbuf(nc, "yw", [128, CC], F16)
        self.statn = sbuf(nc, "statn", [128, 128], BF16)
        self.trilmid = sbuf(nc, "trilmid", [128, 128], F16)
        self.trilsum = sbuf(nc, "trilsum", [128, 128], BF16)
        self.ident = sbuf(nc, "ident", [128, 128], F16)
        self.bct = sbuf(nc, "bct", [128, 2 * NST], F16)
        self.bctf = sbuf(nc, "bctf", [128, 2 * NST], F32)
        self.abc = [sbuf(nc, f"abc{i}", [128, CC], BF16) for i in range(NST)]
        self.alpha16 = sbuf(nc, "alpha16", [17, CC], F16)
        self.alrep = sbuf(nc, "alrep", [128, CC], BF16)
        self.gA = sbuf(nc, "gA", [128, CC], BF16)
        self.gB = sbuf(nc, "gB", [128, CC], BF16)
        _dmde = sbuf(nc, "dmde", [96, 1024], F16)
        self.dmde = [_dmde[32 * g:32 * (g + 1)] for g in range(3)]
        self.hselA = sbuf(nc, "hselA", [128, 8 * 32], F16)
        self.hselB = sbuf(nc, "hselB", [128, 8 * 32], F16)
        self.indA0 = sbuf(nc, "indA0", [96, 128], F16)
        self.indA1 = sbuf(nc, "indA1", [96, 128], F16)
        self.indB0 = sbuf(nc, "indB0", [96, 128], F16)
        self.indB1 = sbuf(nc, "indB1", [96, 128], F16)
        self.rep8 = sbuf(nc, "rep8", [NST, 128], F16)
        self.maskw16 = sbuf(nc, "maskw16", [128, 1], F32)
        self.hrun = sbuf(nc, "hrun", [NST, CC], F16)
        self.hraw = sbuf(nc, "hraw", [NST, CC], BF16)
        self.gsl = sbuf(nc, "gsl", [NST, CC], BF16)
        self.dpb = sbuf(nc, "dpb", [128, CC], F16)
        self.xdblb = sbuf(nc, "xdblb", [128, Lc + WT], F16)
        self.wrhs = [sbuf(nc, f"wrhs{i}", [128, 2 * WT], F16) for i in range(16)]
        self.w32 = sbuf(nc, "w32", [128, 1024], F32)
        self.wkp = [sbuf(nc, f"wkp{k}", [128, 256], F16) for k in range(8)]
        self.convw = sbuf(nc, "convw", [128, 16 * 4], F32)
        self.convb = sbuf(nc, "convb", [128, 16], F32)
        self.wxtb = sbuf(nc, "wxtb", [128, 16 * 96], F16)
        self.wdtb = sbuf(nc, "wdtb", [128, CC], F16)
        self.xhead = sbuf(nc, "xhead", [128, 3], F16)
        self.zpad = sbuf(nc, "zpad", [128, TP - Lc], F16)
        self.wos = [sbuf(nc, f"wos{k}", [128, D], F16) for k in range(2)]
        self.xmt = sbuf(nc, "xmt", [128, 32], F16)
        self.ones1 = sbuf(nc, "ones1", [1, 128], F16)
        self.onescol = sbuf(nc, "onescol", [128, 1], F16)
        self.xtl = sbuf(nc, "xtl", [128, 32], F16)
        self.mcols = sbuf(nc, "mcols", [128, 48], F32)
        self.fcols = sbuf(nc, "fcols", [128, 2], F32)
        self.scol = sbuf(nc, "scol", [128, 16], F16)
        self.h2f8 = sbuf(nc, "h2f8", [128, 4096], F8)
        self.w1p = sbuf(nc, "w1p", [128, 1024], F8)
        self.w2s = sbuf(nc, "w2s", [128, 1024], F8)
        self.h1q8 = sbuf(nc, "h1q8", [128, 512], F8)
        self.bq = sbuf(nc, "bq", [128, 40], F32)
        self.psA = pst(nc, "psA", [128, 1024], F32)
        self.psB = pst(nc, "psB", [128, 1024], F32)
        self.psC = pst(nc, "psC", [128, 1024], F32)
        self.psD = pst(nc, "psD", [128, 512], F32)
        self.psT = pst(nc, "psT", [128, 1024], F16)


def host_consts():
    bf = ml_dtypes.bfloat16
    trilmid = np.zeros((128, 128), np.float32)
    for t in range(WT):
        for s in range(WT):
            trilmid[s, t] = (1.0 if s <= t else 0.0) - (1.0 if s <= 63 else 0.0)
    trilsum = np.zeros((128, 128), np.float32)
    for t in range(WT):
        trilsum[: t + 1, t] = 1.0
    trilsum[127, :] = 1.0
    trilsum[:WT, 127] = 1.0
    ident = np.eye(128, dtype=np.float32)
    hselA = np.zeros((128, 8 * 32), np.float32)
    hselB = np.zeros((128, 8 * 32), np.float32)
    for wl in range(8):
        hselA[:64, 32 * wl + 2 * wl] = 1.0
        hselA[64:WT, 32 * wl + 2 * wl + 1] = 1.0
        hselB[:64, 32 * wl + 16 + 2 * wl] = 1.0
        hselB[64:WT, 32 * wl + 17 + 2 * wl] = 1.0
    indA0 = np.zeros((32, 128), np.float32)
    indA1 = np.zeros((32, 128), np.float32)
    indB0 = np.zeros((32, 128), np.float32)
    indB1 = np.zeros((32, 128), np.float32)
    for wl in range(8):
        indA0[2 * wl, 16 * wl:16 * wl + 16] = 1.0
        indA1[16 + 2 * wl, 16 * wl:16 * wl + 16] = 1.0
        indB0[2 * wl + 1, 16 * wl:16 * wl + 16] = 1.0
        indB1[16 + 2 * wl + 1, 16 * wl:16 * wl + 16] = 1.0
    rep8 = np.zeros((16, 128), np.float32)
    for wl in range(8):
        for n in range(16):
            rep8[n, 16 * wl + n] = 1.0
    maskw16 = np.zeros((128, 1), np.float32)
    maskw16[:LASTV] = 1.0
    f16 = np.float16
    onesrow = np.zeros((1, Lc + WT), np.float16)
    onesrow[0, :Lc] = 1.0
    return {
        "trilmid": trilmid.astype(f16), "trilsum": trilsum.astype(bf),
        "ident": ident.astype(f16), "hselA": hselA.astype(f16), "hselB": hselB.astype(f16),
        "indA0": np.tile(indA0, (3, 1)).astype(f16), "indA1": np.tile(indA1, (3, 1)).astype(f16),
        "indB0": np.tile(indB0, (3, 1)).astype(f16), "indB1": np.tile(indB1, (3, 1)).astype(f16),
        "rep8": rep8.astype(f16), "maskw16": maskw16,
        "ones1": np.ones((1, 128), np.float32).astype(f16),
        "onescol": np.ones((128, 1), np.float32).astype(f16),
        "onesrow": onesrow,
    }


CONST_NAMES = ("trilmid", "trilsum", "ident", "hselA", "hselB",
               "indA0", "indA1", "indB0", "indB1", "rep8", "maskw16",
               "ones1", "onescol")


def emit_load_consts(kb, cx, key):
    for name in CONST_NAMES:
        t = getattr(cx, name)
        kb.step("sync", "dmaL", 16, lambda e, t=t, n=name: e.dma_start(t[:], key(n)[:]))
    kb.step("sync", "dmaL", 16, lambda e: e.dma_start(cx.fcols[:], key("fcols")[:]))
    kb.wait("vector", "dmaL")
    kb.step("vector", "dve", 1, lambda e: e.memset(cx.zpad[:], 0.0))
    kb.step("vector", "dve", 1, lambda e: e.memset(cx.hrun[:], 0.0))
    kb.wait("gpsimd", "dve")
    kb.step("gpsimd", "dmaG", 16, lambda e: e.dma_start(cx.bct[127:128, :], key("onesrow")[0:1, 0:2 * NST]))


def emit_modvecs(kb, cx, key):
    kb.wait_all("sync")
    kb.step("sync", "dmaL", 16, lambda e: e.dma_start(cx.w32[:, 0:8], key("ccol")[:]))
    kb.wait_all("scalar")
    kb.step("scalar", "act", 1, lambda e: e.activation(cx.w32[:, 8:16], cx.w32[:, 0:8], ACT.Silu))
    kb.wait("vector", "act")
    kb.step("vector", "dve", 1, lambda e: e.memset(cx.scol[:], 0.0))
    for k in range(8):
        kb.step("vector", "dve", 1, lambda e, k=k: e.tensor_copy(cx.scol[:, 2 * k:2 * k + 1], cx.w32[:, 8 + k:9 + k]))
    for jc in range(6):
        for kt in range(8):
            kb.wait_all("sync")
            kb.step("sync", "dmaL", 16, lambda e, jc=jc, kt=kt: e.dma_start(
                cx.w32[:, 0:1024], key("wmod")[128 * kt:128 * (kt + 1), 1024 * jc:1024 * (jc + 1)]))
            kb.wait("vector", "dmaL")
            kb.step("vector", "dve", 1, lambda e: e.tensor_copy(cx.duw[:, 0:1024], cx.w32[:, 0:1024]))
            kb.wait_all("tensor")
            for jj in range(8):
                kb.step("tensor", "pe", 1, lambda e, jj=jj, kt=kt: e.matmul(
                    cx.psD[:, 2 * jj:2 * jj + 2], cx.duw[:, 128 * jj:128 * (jj + 1)],
                    cx.scol[:, 2 * kt:2 * kt + 2], start=True, stop=True))
            kb.pe_fence()
            kb.wait("vector", "pe")
            for jj in range(8):
                if kt == 0:
                    kb.step("vector", "dve", 1, lambda e, jc=jc, jj=jj: e.tensor_copy(
                        cx.mcols[:, 8 * jc + jj:8 * jc + jj + 1], cx.psD[:, 2 * jj:2 * jj + 1]))
                else:
                    kb.step("vector", "dve", 1, lambda e, jc=jc, jj=jj: e.tensor_tensor(
                        cx.mcols[:, 8 * jc + jj:8 * jc + jj + 1],
                        cx.mcols[:, 8 * jc + jj:8 * jc + jj + 1],
                        cx.psD[:, 2 * jj:2 * jj + 1], ALU.add))
    kb.step("vector", "dve", 1, lambda e: e.tensor_scalar(cx.mcols[:, 8:16], cx.mcols[:, 8:16], 1.0, None, ALU.add))
    kb.step("vector", "dve", 1, lambda e: e.tensor_scalar(cx.mcols[:, 32:40], cx.mcols[:, 32:40], 1.0, None, ALU.add))
    for base in (16, 40):
        kb.step("vector", "dve", 1, lambda e, base=base: e.tensor_scalar(
            cx.w32[:, 0:8], cx.mcols[:, base:base + 8], 0.0, None, ALU.is_ge))
        kb.step("vector", "dve", 1, lambda e: e.tensor_scalar(
            cx.w32[:, 0:8], cx.w32[:, 0:8], 2.0, -1.0, ALU.mult, ALU.add))
        kb.step("vector", "dve", 1, lambda e, base=base: e.tensor_scalar(
            cx.w32[:, 8:16], cx.mcols[:, base:base + 8], -1.0, None, ALU.mult))
        kb.step("vector", "dve", 1, lambda e, base=base: e.tensor_tensor(
            cx.w32[:, 8:16], cx.mcols[:, base:base + 8], cx.w32[:, 8:16], ALU.max))
        kb.step("vector", "dve", 1, lambda e: e.tensor_scalar(
            cx.w32[:, 8:16], cx.w32[:, 8:16], 0.25, None, ALU.max))
        kb.step("vector", "dve", 1, lambda e, base=base: e.tensor_tensor(
            cx.mcols[:, base:base + 8], cx.w32[:, 0:8], cx.w32[:, 8:16], ALU.mult))


def emit_ln(kb, cx, key, phase):
    if phase == 1:
        kb.wait("vector", "dmaL")
        kb.step("vector", "dve", 1, lambda e: e.memset(cx.xtl[:], 0.0))
        for kt in range(8):
            for h in range(2):
                kb.wait_all("sync")
                kb.step("sync", "dmaL", 16, lambda e, kt=kt, h=h: e.dma_start(
                    cx.w32[:, 0:1024], key("XA")[128 * kt:128 * (kt + 1), 3 + 1024 * h:3 + 1024 * (h + 1)]))
                kb.wait("vector", "dmaL")
                kb.step("vector", "dve", 1, lambda e, kt=kt, h=h: e.tensor_copy(
                    cx.xm[kt][:, 1024 * h:1024 * (h + 1)], cx.w32[:, 0:1024]))
            kb.wait_all("sync")
            kb.step("sync", "dmaL", 16, lambda e, kt=kt: e.dma_start(
                cx.w32[:, 0:3], key("XA")[128 * kt:128 * (kt + 1), 0:3]))
            kb.wait("vector", "dmaL")
            kb.step("vector", "dve", 1, lambda e, kt=kt: e.tensor_copy(
                cx.xtl[:, 4 * kt:4 * kt + 3], cx.w32[:, 0:3]))
    for th in range(2):
        for kt in range(8):
            kb.wait("vector", "pe")
            kb.step("vector", "dve", 1, lambda e, kt=kt, th=th: e.tensor_tensor(
                cx.duw[:, 0:1024], cx.xm[kt][:, 1024 * th:1024 * (th + 1)],
                cx.xm[kt][:, 1024 * th:1024 * (th + 1)], ALU.mult))
            kb.wait_all("tensor")
            for c in range(2):
                kb.step("tensor", "pe", 1, lambda e, kt=kt, th=th, c=c: e.matmul(
                    cx.psA[0:1, 512 * c:512 * (c + 1)], cx.onescol[:],
                    cx.xm[kt][:, 1024 * th + 512 * c:1024 * th + 512 * (c + 1)],
                    start=(kt == 0), stop=(kt == 7)))
                kb.step("tensor", "pe", 1, lambda e, kt=kt, c=c: e.matmul(
                    cx.psB[0:1, 512 * c:512 * (c + 1)], cx.onescol[:],
                    cx.duw[:, 512 * c:512 * (c + 1)], start=(kt == 0), stop=(kt == 7)))
            kb.pe_fence()
        kb.wait("vector", "pe")
        kb.step("vector", "dve", 1, lambda e, th=th: e.tensor_scalar(
            cx.ustg[0:1, 1024 * th:1024 * (th + 1)], cx.psA[0:1, 0:1024], 1.0 / 1024, None, ALU.mult))
        kb.step("vector", "dve", 1, lambda e, th=th: e.tensor_scalar(
            cx.dpb[0:1, 1024 * th:1024 * (th + 1)], cx.psB[0:1, 0:1024], 1.0 / 1024, None, ALU.mult))
    kb.step("vector", "dve", 1, lambda e: e.tensor_tensor(
        cx.dwin[0:1, :], cx.ustg[0:1, :], cx.ustg[0:1, :], ALU.mult))
    kb.step("vector", "dve", 1, lambda e: e.tensor_tensor(
        cx.dpb[0:1, :], cx.dpb[0:1, :], cx.dwin[0:1, :], ALU.subtract))
    kb.step("vector", "dve", 1, lambda e: e.tensor_scalar(
        cx.dpb[0:1, :], cx.dpb[0:1, :], 1e-5, None, ALU.add))
    kb.wait_all("scalar")
    kb.step("scalar", "act", 1, lambda e: e.activation(
        cx.dwin[0:1, :], cx.dpb[0:1, :], ACT.Ln, bias=0.0, scale=1.0))
    kb.step("scalar", "act", 1, lambda e: e.activation(
        cx.dpb[0:1, :], cx.dwin[0:1, :], ACT.Exp, bias=0.0, scale=-0.5))
    kb.wait_all("tensor")
    for c in range(4):
        kb.step("tensor", "pe", 1, lambda e, c=c: e.matmul(
            (cx.psA if c < 2 else cx.psB)[:, 512 * (c % 2):512 * (c % 2 + 1)],
            cx.ones1[:], cx.ustg[0:1, 512 * c:512 * (c + 1)], start=True, stop=True))
    kb.pe_fence()
    kb.wait("vector", "pe")
    kb.step("vector", "dve", 1, lambda e: e.tensor_copy(cx.dwin[:, 0:1024], cx.psA[:, :1024]))
    kb.step("vector", "dve", 1, lambda e: e.tensor_copy(cx.dwin[:, 1024:], cx.psB[:, :1024]))
    kb.wait_all("tensor")
    for c in range(4):
        kb.step("tensor", "pe", 1, lambda e, c=c: e.matmul(
            (cx.psA if c < 2 else cx.psB)[:, 512 * (c % 2):512 * (c % 2 + 1)],
            cx.ones1[:], cx.dpb[0:1, 512 * c:512 * (c + 1)], start=True, stop=True))
    kb.pe_fence()
    kb.wait("vector", "pe")
    kb.step("vector", "dve", 1, lambda e: e.tensor_copy(cx.dlw[:, 0:1024], cx.psA[:, :1024]))
    kb.step("vector", "dve", 1, lambda e: e.tensor_copy(cx.dlw[:, 1024:], cx.psB[:, :1024]))
    if phase == 1:
        kb.step("vector", "dve", 1, lambda e: e.tensor_tensor(
            cx.duw[:, 0:32], cx.xtl[:], cx.xtl[:], ALU.mult))
        kb.wait_all("tensor")
        for kt in range(8):
            kb.step("tensor", "pe", 1, lambda e, kt=kt: e.matmul(
                cx.psD[0:1, 0:4], cx.onescol[:], cx.xtl[:, 4 * kt:4 * (kt + 1)],
                start=(kt == 0), stop=(kt == 7)))
        for kt in range(8):
            kb.step("tensor", "pe", 1, lambda e, kt=kt: e.matmul(
                cx.psD[0:1, 4:8], cx.onescol[:], cx.duw[:, 4 * kt:4 * (kt + 1)],
                start=(kt == 0), stop=(kt == 7)))
        kb.pe_fence()
        kb.wait("vector", "pe")
        kb.step("vector", "dve", 1, lambda e: e.tensor_scalar(
            cx.bct[0:1, 0:4], cx.psD[0:1, 0:4], 1.0 / 1024, None, ALU.mult))
        kb.step("vector", "dve", 1, lambda e: e.tensor_scalar(
            cx.bct[0:1, 4:8], cx.psD[0:1, 4:8], 1.0 / 1024, None, ALU.mult))
        kb.step("vector", "dve", 1, lambda e: e.tensor_tensor(
            cx.bct[0:1, 8:12], cx.bct[0:1, 0:4], cx.bct[0:1, 0:4], ALU.mult))
        kb.step("vector", "dve", 1, lambda e: e.tensor_tensor(
            cx.bct[0:1, 4:8], cx.bct[0:1, 4:8], cx.bct[0:1, 8:12], ALU.subtract))
        kb.step("vector", "dve", 1, lambda e: e.tensor_scalar(
            cx.bct[0:1, 4:8], cx.bct[0:1, 4:8], 1e-5, None, ALU.add))
        kb.wait_all("scalar")
        kb.step("scalar", "act", 1, lambda e: e.activation(
            cx.bct[0:1, 8:12], cx.bct[0:1, 4:8], ACT.Ln, bias=0.0, scale=1.0))
        kb.step("scalar", "act", 1, lambda e: e.activation(
            cx.bct[0:1, 4:8], cx.bct[0:1, 8:12], ACT.Exp, bias=0.0, scale=-0.5))
        kb.wait_all("tensor")
        kb.step("tensor", "pe", 1, lambda e: e.matmul(
            cx.psD[:, 8:12], cx.ones1[:], cx.bct[0:1, 0:4], start=True, stop=True))
        kb.step("tensor", "pe", 1, lambda e: e.matmul(
            cx.psD[:, 12:16], cx.ones1[:], cx.bct[0:1, 4:8], start=True, stop=True))
        kb.pe_fence()
        kb.wait("vector", "pe")
        kb.step("vector", "dve", 1, lambda e: e.tensor_copy(cx.statn[:, 0:4], cx.psD[:, 8:12]))
        kb.step("vector", "dve", 1, lambda e: e.tensor_copy(cx.statn[:, 4:8], cx.psD[:, 12:16]))
        for kt in range(8):
            kb.step("vector", "dve", 1, lambda e, kt=kt: e.tensor_tensor(
                cx.xtl[:, 4 * kt:4 * (kt + 1)], cx.xtl[:, 4 * kt:4 * (kt + 1)],
                cx.statn[:, 0:4], ALU.subtract))
            kb.step("vector", "dve", 1, lambda e, kt=kt: e.tensor_tensor(
                cx.xtl[:, 4 * kt:4 * (kt + 1)], cx.xtl[:, 4 * kt:4 * (kt + 1)],
                cx.statn[:, 4:8], ALU.mult))
            kb.step("vector", "dve", 1, lambda e, kt=kt: e.tensor_scalar(
                cx.xtl[:, 4 * kt:4 * (kt + 1)], cx.xtl[:, 4 * kt:4 * (kt + 1)],
                cx.mcols[:, 8 + kt:9 + kt], cx.mcols[:, kt:kt + 1], ALU.mult, ALU.add))
    cfoff = 8 if phase == 1 else 32
    soff = 0 if phase == 1 else 24
    for kt in range(8):
        kb.step("vector", "dve", 1, lambda e, kt=kt: e.tensor_tensor(
            cx.duw[:, :], cx.xm[kt][:, :], cx.dwin[:, :], ALU.subtract))
        kb.step("vector", "dve", 1, lambda e: e.tensor_tensor(
            cx.duw[:, :], cx.duw[:, :], cx.dlw[:, :], ALU.mult))
        if phase == 1:
            kb.step("vector", "dve", 1, lambda e, kt=kt: e.tensor_scalar(
                cx.xm[kt][:, :], cx.duw[:, :],
                cx.mcols[:, cfoff + kt:cfoff + kt + 1], cx.mcols[:, soff + kt:soff + kt + 1],
                ALU.mult, ALU.add))
        else:
            kb.wait("vector", "dmaG")
            kb.step("vector", "dve", 1, lambda e, kt=kt: e.tensor_scalar(
                cx.h2f8[:, 0:2048], cx.duw[:, :],
                cx.mcols[:, cfoff + kt:cfoff + kt + 1], cx.mcols[:, soff + kt:soff + kt + 1],
                ALU.mult, ALU.add))
            kb.wait_all("gpsimd")
            kb.step("gpsimd", "dmaG", 16, lambda e, kt=kt: e.dma_start(
                key("h2_hbm")[kt, :, :], cx.h2f8[:, 0:2048]))
    if phase == 1:
        kb.step("vector", "dve", 1, lambda e: e.tensor_scalar(
            cx.xmt[:], cx.xtl[:], cx.fcols[:, 0:1], None, ALU.mult))


def emit_slot_head(kb, cx, key, s):
    kb.wait_all("gpsimd")
    for kt in range(8):
        kb.step("gpsimd", "dmaG", 16, lambda e, kt=kt: e.dma_start(
            cx.xtl[:, 4 * kt:4 * kt + 3], cx.xm[kt][:, 2045:2048]))
    kb.wait_all("vector")
    kb.step("vector", "dve", 1, lambda e: e.tensor_copy(cx.w32[:, 0:32], cx.xtl[:]))
    kb.step("vector", "dve", 1, lambda e: e.tensor_scalar(
        cx.w32[:, 32:64], cx.w32[:, 0:32], cx.fcols[:, 1:2], None, ALU.mult))
    kb.step("vector", "dve", 1, lambda e: e.tensor_scalar(
        cx.hrun[:], cx.hrun[:], cx.fcols[0:16, 1:2], None, ALU.mult))
    kb.wait_all("gpsimd")
    kb.step("gpsimd", "dmaG", 16, lambda e, s=s: e.dma_start(key(f"arinX{s}")[:], cx.w32[:, 32:64]))
    kb.step("gpsimd", "dmaG", 16, lambda e, s=s: e.dma_start(key(f"arinH{s}")[:], cx.hrun[:]))
    if FUSED:
        kb.wait("gpsimd", "dmaG")
        kb.step("gpsimd", "cc", 1, lambda e, s=s: e.collective_compute(
            "AllReduce", ALU.add, replica_groups=[[0, 1], [2, 3], [4, 5], [6, 7]],
            ins=[key(f"arinH{s}")[:].opt()], outs=[key(f"aroutH{s}")[:].opt()]))
        kb.step("gpsimd", "cc", 1, lambda e, s=s: e.collective_compute(
            "AllReduce", ALU.add, replica_groups=[[0, 1], [2, 3], [4, 5], [6, 7]],
            ins=[key(f"arinX{s}")[:].opt()], outs=[key(f"aroutX{s}")[:].opt()]))
    if s > 0:
        if FUSED:
            kb.wait("gpsimd", "cc", 2 * s)
        kb.step("gpsimd", "dmaG", 16, lambda e, s=s: e.dma_start(
            cx.w32[:, 64:96], key(f"aroutX{s - 1}")[:]))
        kb.wait_all("vector")
        kb.step("vector", "dve", 1, lambda e: e.tensor_scalar(
            cx.w32[:, 64:96], cx.w32[:, 64:96], cx.fcols[:, 0:1], None, ALU.mult))
        kb.step("vector", "dve", 1, lambda e: e.tensor_tensor(
            cx.w32[:, 32:64], cx.w32[:, 32:64], cx.w32[:, 64:96], ALU.add))
        kb.step("vector", "dve", 1, lambda e: e.tensor_copy(cx.xmt[:], cx.w32[:, 32:64]))


def emit_ar_consume(kb, cx, key, s):
    if FUSED:
        kb.wait("gpsimd", "cc", 2 * s + 2)
    kb.wait_all("gpsimd")
    kb.step("gpsimd", "dmaG", 16, lambda e, s=s: e.dma_start(cx.hrun[:], key(f"aroutH{s}")[:]))
    kb.wait_all("vector")
    kb.step("vector", "dve", 1, lambda e: e.tensor_scalar(
        cx.hrun[:], cx.hrun[:], cx.fcols[0:16, 0:1], None, ALU.mult))


def emit_layer(kb, cx, key, pre_p3=None):
    nc = kb.nc

    kb.wait("gpsimd", "dmaL")
    kb.wait("vector", "pe")
    kb.wait("sync", "dve")
    kb.step("sync", "dmaL", 16, lambda e: e.dma_start(cx.convw[:], key("cw")[:]))
    kb.step("sync", "dmaL", 16, lambda e: e.dma_start(cx.convb[:], key("cb")[:]))
    for h in range(2):
        kb.wait("sync", "dve")
        kb.step("sync", "dmaL", 16, lambda e, h=h: e.dma_start(cx.w32[0:16, :], key("alogT")[:, 1024 * h:1024 * (h + 1)]))
        kb.wait("scalar", "dmaL")
        kb.step("scalar", "act", 1, lambda e: e.activation(cx.w32[0:16, :], cx.w32[0:16, :], ACT.Exp))
        kb.wait("vector", "act")
        kb.step("vector", "dve", 1, lambda e, h=h: e.tensor_copy(cx.alpha16[0:16, 1024 * h:1024 * (h + 1)], cx.w32[0:16, :]))
    for h in range(2):
        kb.wait("sync", "dve")
        kb.step("sync", "dmaL", 16, lambda e, h=h: e.dma_start(cx.w32[0:1, :], key("dpar")[:, 1024 * h:1024 * (h + 1)]))
        kb.wait("vector", "dmaL")
        kb.step("vector", "dve", 1, lambda e, h=h: e.tensor_copy(cx.ustg[0:1, 0:1024], cx.w32[0:1, :]))
        kb.wait("gpsimd", "dve")
        kb.step("gpsimd", "dmaG", 16, lambda e, h=h: e.dma_start(cx.alpha16[16:17, 1024 * h:1024 * (h + 1)], cx.ustg[0:1, 0:1024]))
        kb.wait("vector", "dmaG")
        kb.wait("sync", "dve")
        kb.step("sync", "dmaL", 16, lambda e, h=h: e.dma_start(cx.w32[0:128, :], key("wdtTb")[:, 1024 * h:1024 * (h + 1)]))
        kb.wait("vector", "dmaL")
        kb.step("vector", "dve", 1, lambda e, h=h: e.tensor_copy(cx.wdtb[:, 1024 * h:1024 * (h + 1)], cx.w32[0:128, :]))
    for h in range(2):
        w0, w1 = 1024 * h, min(16 * 96, 1024 * (h + 1))
        kb.wait("sync", "dve")
        kb.step("sync", "dmaL", 16, lambda e, w0=w0, w1=w1: e.dma_start(cx.w32[:, 0:w1 - w0], key("wxT")[:, w0:w1]))
        kb.wait("vector", "dmaL")
        kb.step("vector", "dve", 1, lambda e, w0=w0, w1=w1: e.tensor_copy(cx.wxtb[:, w0:w1], cx.w32[:, 0:w1 - w0]))
    for n in range(NST + 1):
        kb.wait("gpsimd", "pe")
        kb.wait("gpsimd", "dve")
        kb.step("gpsimd", "dmaG", 16, lambda e, n=n: e.dma_start(cx.ustg[0:1, 0:CC], cx.alpha16[n:n + 1, :]))
        kb.wait("tensor", "dmaG")
        for h in range(4):
            kb.step("tensor", "pe", 1, lambda e, h=h: e.matmul(
                (cx.psA if h < 2 else cx.psB)[:, 512 * (h % 2):512 * (h % 2 + 1)], cx.ones1[:],
                cx.ustg[0:1, 512 * h:512 * (h + 1)], start=True, stop=True))
        kb.pe_fence()
        kb.wait("vector", "pe")
        if n < NST:
            kb.step("vector", "dve", 1, lambda e, n=n: e.tensor_copy(cx.abc[n][:, :1024], cx.psA[:, :1024]))
            kb.step("vector", "dve", 1, lambda e, n=n: e.tensor_copy(cx.abc[n][:, 1024:], cx.psB[:, :1024]))
        else:
            kb.step("vector", "dve", 1, lambda e: e.tensor_copy(cx.dpb[:, :1024], cx.psA[:, :1024]))
            kb.step("vector", "dve", 1, lambda e: e.tensor_copy(cx.dpb[:, 1024:], cx.psB[:, :1024]))
    kb.wait("tensor", "dve")
    for h in range(4):
        kb.step("tensor", "pe", 1, lambda e, h=h: e.matmul(
            (cx.psA if h < 2 else cx.psB)[:, 512 * (h % 2):512 * (h % 2 + 1)], cx.rep8[:],
            cx.alpha16[0:16, 512 * h:512 * (h + 1)], start=True, stop=True))
    kb.pe_fence()
    kb.wait("vector", "pe")
    kb.step("vector", "dve", 1, lambda e: e.tensor_copy(cx.alrep[:, 0:1024], cx.psA[:, :1024]))
    kb.step("vector", "dve", 1, lambda e: e.tensor_copy(cx.alrep[:, 1024:], cx.psB[:, :1024]))
    for k in range(16):
        kb.wait("sync", "dve")
        kb.step("sync", "dmaL", 16, lambda e, k=k: e.dma_start(cx.w32[:, 0:1024], key("woT")[128 * k:128 * (k + 1), :]))
        kb.wait("vector", "dmaL")
        kb.step("vector", "dve", 1, lambda e: e.tensor_copy(cx.wos[0][:], cx.w32[:, 0:1024]))
        kb.wait("gpsimd", "dve")
        kb.step("gpsimd", "dmaG", 16, lambda e, k=k: e.dma_start(key("wo_hbm")[128 * k:128 * (k + 1), :], cx.wos[0][:]))
        kb.wait("vector", "dmaG")

    for mg in range(16):
        kb.wait("vector", "pe")
        for k in range(8):
            kb.wait("sync", "dve")
            kb.step("sync", "dmaL", 16, lambda e, mg=mg, k=k: e.dma_start(
                cx.w32[:, 0:256], key("winT")[128 * k:128 * (k + 1), 256 * mg:256 * (mg + 1)]))
            kb.wait("vector", "dmaL")
            kb.step("vector", "dve", 1, lambda e, k=k: e.tensor_copy(cx.wkp[k][:], cx.w32[:, 0:256]))
        for mi in range(2):
            m = mg * 2 + mi
            kb.wait("tensor", "dve")
            kb.wait("tensor", "act")
            for th in range(2):
                tgt = cx.psA if th == 0 else cx.psB
                for nn in range(2):
                    for k in range(8):
                        kb.step("tensor", "pe", 1, lambda e, mi=mi, th=th, nn=nn, k=k, tgt=tgt: e.matmul(
                            tgt[:, N_MM * nn:N_MM * (nn + 1)],
                            cx.wkp[k][:, 128 * mi:128 * (mi + 1)],
                            cx.xm[k][:, 1024 * th + N_MM * nn:1024 * th + N_MM * (nn + 1)],
                            start=(k == 0), stop=(k == 7)))
            if m < 16:
                ct = m
                for k in range(8):
                    kb.step("tensor", "pe", 1, lambda e, mi=mi, k=k: e.matmul(
                        cx.psD[:, 0:4], cx.wkp[k][:, 128 * mi:128 * (mi + 1)],
                        cx.xmt[:, 4 * k:4 * (k + 1)], start=(k == 0), stop=(k == 7)))
                kb.pe_fence()
                kb.wait("vector", "pe")
                kb.wait("vector", "dmaG")
                kb.step("vector", "dve", 1, lambda e: e.tensor_copy(cx.dwin[:, 0:1024], cx.psA[:, :1024]))
                kb.step("vector", "dve", 1, lambda e: e.tensor_copy(cx.dwin[:, 1024:], cx.psB[:, :1024]))
                kb.step("vector", "dve", 1, lambda e: e.tensor_copy(cx.xhead[:, 0:3], cx.psD[:, 0:3]))
                kb.step("vector", "dve", 1, lambda e, ct=ct: e.tensor_scalar(
                    cx.ustg[:, 3:Lc], cx.dwin[:, 0:Lc - 3], cx.convw[:, 4 * ct:4 * ct + 1], None, ALU.mult))
                kb.step("vector", "dve", 1, lambda e, ct=ct: e.tensor_scalar(
                    cx.ustg[:, 0:3], cx.xhead[:, 0:3], cx.convw[:, 4 * ct:4 * ct + 1], None, ALU.mult))
                for tap in range(1, 4):
                    kb.step("vector", "dve", 1, lambda e, ct=ct, tap=tap: e.scalar_tensor_tensor(
                        cx.ustg[:, 3 - tap:Lc], cx.dwin[:, 0:Lc - 3 + tap],
                        cx.convw[:, 4 * ct + tap:4 * ct + tap + 1],
                        cx.ustg[:, 3 - tap:Lc], ALU.mult, ALU.add))
                    if tap < 3:
                        kb.step("vector", "dve", 1, lambda e, ct=ct, tap=tap: e.scalar_tensor_tensor(
                            cx.ustg[:, 0:3 - tap], cx.xhead[:, tap:3],
                            cx.convw[:, 4 * ct + tap:4 * ct + tap + 1],
                            cx.ustg[:, 0:3 - tap], ALU.mult, ALU.add))
                kb.wait("scalar", "dve")
                kb.step("scalar", "act", 1, lambda e, ct=ct: e.activation(
                    cx.dwin[:, 0:Lc], cx.ustg[:, 0:Lc], ACT.Silu, bias=cx.convb[:, ct:ct + 1], scale=1.0))
                kb.wait("gpsimd", "act")
                kb.wait("gpsimd", "dmaL")
                kb.step("gpsimd", "dmaG", 16, lambda e, ct=ct: e.dma_start(
                    key("u_hbm")[ct, :, 0:Lc], cx.dwin[:, 0:Lc]))
                kb.step("gpsimd", "dmaG", 16, lambda e, ct=ct: e.dma_start(
                    key("u_hbm")[ct, :, Lc:TP], cx.zpad[:]))
            else:
                ct = m - 16
                kb.pe_fence()
                kb.wait("scalar", "pe")
                kb.wait("scalar", "dmaG")
                kb.step("scalar", "act", 1, lambda e: e.activation(
                    cx.ustg[:, 0:1024], cx.psA[:, :1024], ACT.Silu))
                kb.step("scalar", "act", 1, lambda e: e.activation(
                    cx.ustg[:, 1024:2048], cx.psB[:, :1024], ACT.Silu))
                kb.wait("gpsimd", "act")
                kb.wait("gpsimd", "dmaL")
                kb.step("gpsimd", "dmaG", 16, lambda e, ct=ct: e.dma_start(key("z_hbm")[ct, :, 0:Lc], cx.ustg[:]))
                kb.step("gpsimd", "dmaG", 16, lambda e, ct=ct: e.dma_start(key("z_hbm")[ct, :, Lc:TP], cx.zpad[:]))

    kb.wait("vector", "pe")
    kb.step("vector", "dve", 1, lambda e: e.memset(cx.xdblb[:], 0.0))
    kb.wait("gpsimd", "dve")
    kb.step("gpsimd", "dmaG", 16, lambda e: e.dma_start(cx.xdblb[96:97, :], key("onesrow")[:]))
    kb.wait("vector", "dmaG")
    for ct in range(16):
        kb.wait("sync", "pe")
        kb.wait("sync", "dve")
        kb.wait("sync", "dmaG")
        kb.step("sync", "dmaL", 16, lambda e, ct=ct: e.dma_start(cx.ustg[:, 0:Lc], key("u_hbm")[ct, :, 0:Lc]))
        kb.wait("tensor", "dmaL")
        for nn in range(4):
            tgt = cx.psA if nn < 2 else cx.psB
            kb.step("tensor", "pe", 1, lambda e, ct=ct, nn=nn, tgt=tgt: e.matmul(
                tgt[0:96, N_MM * (nn % 2):N_MM * (nn % 2 + 1)],
                cx.wxtb[:, 96 * ct:96 * (ct + 1)],
                cx.ustg[:, N_MM * nn:N_MM * (nn + 1)],
                start=(ct == 0), stop=(ct == 15)))
    kb.pe_fence()
    kb.wait("vector", "pe")
    kb.step("vector", "dve", 1, lambda e: e.tensor_copy(cx.xdblb[0:96, 0:1024], cx.psA[0:96, :]))
    kb.step("vector", "dve", 1, lambda e: e.tensor_copy(cx.xdblb[0:96, 1024:2048], cx.psB[0:96, :]))

    for g, wr in enumerate(GRPW):
        for w in wr:
            wl = w - wr.start
            off = w * WT
            kb.wait("tensor", "dve")
            kb.wait("tensor", "dmaG")
            for ch in range(2):
                tgt = cx.psA if ch == 0 else cx.psB
                for nn in range(2):
                    kb.step("tensor", "pe", 1, lambda e, off=off, ch=ch, nn=nn, tgt=tgt: e.matmul(
                        tgt[:WT, N_MM * nn:N_MM * (nn + 1)],
                        cx.xdblb[0:64, off:off + WT],
                        cx.wdtb[0:64, 1024 * ch + N_MM * nn:1024 * ch + N_MM * (nn + 1)],
                        start=True, stop=False))
                    kb.tensor_drain()
                    kb.step("tensor", "pe", 1, lambda e, off=off, ch=ch, nn=nn, tgt=tgt: e.matmul(
                        tgt[:WT, N_MM * nn:N_MM * (nn + 1)],
                        cx.xdblb[64:128, off:off + WT],
                        cx.wdtb[64:128, 1024 * ch + N_MM * nn:1024 * ch + N_MM * (nn + 1)],
                        start=False, stop=True))
            kb.pe_fence()
            kb.wait("vector", "pe")
            for q in range(4):
                src = cx.psA if q < 2 else cx.psB
                so = 512 * (q % 2)
                kb.step("vector", "dve", 1, lambda e, src=src, so=so: e.tensor_scalar(
                    cx.w32[:WT, 0:512], src[:WT, so:so + 512], -30.0, 30.0, ALU.max, ALU.min))
                kb.wait("scalar", "dve")
                kb.step("scalar", "act", 1, lambda e: e.activation(
                    cx.w32[:WT, 512:1024], cx.w32[:WT, 0:512], ACT.Exp))
                kb.step("scalar", "act", 1, lambda e, q=q: e.activation(
                    cx.dwin[:WT, 512 * q:512 * (q + 1)], cx.w32[:WT, 512:1024],
                    ACT.Ln, bias=1.0, scale=1.0))
                kb.wait("vector", "act")
            if w == NW - 1:
                kb.step("vector", "dve", 1, lambda e: e.tensor_scalar(
                    cx.dwin[:WT, :], cx.dwin[:WT, :], cx.maskw16[:WT, :], None, ALU.mult))
            kb.wait("gpsimd", "dve")
            kb.wait("gpsimd", "dmaL")
            kb.step("gpsimd", "dmaG", 16, lambda e, w=w: e.dma_start(key("d_hbm")[w, :, :], cx.dwin[:]))
            kb.wait("tensor", "dve")
            for ch in range(2):
                sel = cx.hselA if ch == 0 else cx.hselB
                for nn in range(2):
                    kb.step("tensor", "pe", 1, lambda e, sel=sel, wl=wl, ch=ch, nn=nn, w=w, wr=wr, g=g: e.matmul(
                        cx.psC[32 * g:32 * (g + 1), N_MM * nn:N_MM * (nn + 1)],
                        sel[:, 32 * wl:32 * (wl + 1)],
                        cx.dwin[:, 1024 * ch + N_MM * nn:1024 * ch + N_MM * (nn + 1)],
                        start=(wl == 0 and ch == 0), stop=(w == wr.stop - 1 and ch == 1)))
        kb.pe_fence()
        kb.wait("vector", "pe")
        kb.step("vector", "dve", 1, lambda e, g=g: e.tensor_copy(cx.dmde[g], cx.psC[32 * g:32 * (g + 1), :]))
        kb.wait("tensor", "dve")

    if pre_p3 is not None:
        pre_p3()

    wog_idx = 0
    for g, wr in enumerate(GRPW):
        kb.wait("tensor", "dve")
        for which in range(2):
            i0 = cx.indA0 if which == 0 else cx.indB0
            i1 = cx.indA1 if which == 0 else cx.indB1
            gdst = cx.gA if which == 0 else cx.gB
            for ch in range(2):
                ind = i0 if ch == 0 else i1
                kb.wait("tensor", "dve")
                for nn in range(2):
                    kb.step("tensor", "pe", 1, lambda e, ind=ind, nn=nn, g=g: e.matmul(
                        cx.psA[:, 512 * nn:512 * (nn + 1)], ind[32 * g:32 * (g + 1), :],
                        cx.dmde[g][:, 512 * nn:512 * (nn + 1)], start=True, stop=True))
                kb.pe_fence()
                kb.wait("vector", "pe")
                kb.step("vector", "dve", 1, lambda e, ch=ch: e.tensor_tensor(
                    cx.w32[:, :], cx.alrep[:, 1024 * ch:1024 * (ch + 1)],
                    cx.psA[:, :1024], ALU.mult))
                kb.wait("scalar", "dve")
                kb.step("scalar", "act", 1, lambda e, gdst=gdst, ch=ch: e.activation(
                    gdst[:, 1024 * ch:1024 * (ch + 1)], cx.w32[:], ACT.Exp, bias=0.0, scale=-1.0))
                kb.wait("vector", "act")
        kb.wait("vector", "act")

        for w in wr:
            wl = w - wr.start
            off = w * WT
            wo_g = WOG[wog_idx]
            wo_l = w - wo_g.start
            kb.wait("sync", "dve")
            kb.wait("sync", "act")
            kb.wait("sync", "dmaG")
            kb.step("sync", "dmaL", 16, lambda e, w=w: e.dma_start(cx.dwin[:], key("d_hbm")[w, :, :]))
            kb.wait("sync", "pe")
            for cb in range(16):
                kb.step("sync", "dmaL", 16, lambda e, cb=cb, off=off: e.dma_start(
                    cx.ustg[:, WT * cb:WT * (cb + 1)], key("u_hbm")[cb, :, off:off + WT]))
            uld = kb.cnt["dmaL"]
            kb.wait("gpsimd", "dve")
            kb.wait("gpsimd", "act")
            kb.wait("gpsimd", "dmaG", None)
            for cb in range(16):
                kb.step("gpsimd", "dmaG", 16, lambda e, cb=cb, off=off: e.dma_start(
                    cx.zsil[:, WT * cb:WT * (cb + 1)], key("z_hbm")[cb, :, off:off + WT]))
            kb.wait("tensor", "dmaL", uld)
            kb.wait("tensor", "dve")
            for rnd in range(2):
                for cb8 in range(8):
                    cb = rnd * 8 + cb8
                    kb.step("tensor", "pe", 1, lambda e, cb=cb, cb8=cb8: e.transpose(
                        cx.psT[0:128, 128 * cb8:128 * (cb8 + 1)],
                        cx.ustg[:, WT * cb:WT * cb + 128], cx.ident[:]))
                kb.pe_fence()
                kb.wait("vector", "pe")
                kb.step("vector", "dve", 1, lambda e, rnd=rnd: e.tensor_copy(
                    cx.duw[:WT, 1024 * rnd:1024 * (rnd + 1)], cx.psT[:WT, :]))
                if rnd == 0:
                    kb.wait("tensor", "dve")
            kb.step("vector", "dve", 1, lambda e: e.tensor_tensor(cx.yw[:WT, :], cx.duw[:WT, :], cx.dpb[:WT, :], ALU.mult))
            kb.wait("tensor", "dve")
            for ch in range(2):
                tgt = cx.psA if ch == 0 else cx.psB
                for nn in range(2):
                    kb.step("tensor", "pe", 1, lambda e, ch=ch, nn=nn, tgt=tgt: e.matmul(
                        tgt[:, N_MM * nn:N_MM * (nn + 1)], cx.trilmid[:],
                        cx.dwin[:, 1024 * ch + N_MM * nn:1024 * ch + N_MM * (nn + 1)],
                        start=True, stop=True))
            kb.pe_fence()
            kb.wait("vector", "pe")
            kb.step("vector", "dve", 1, lambda e: e.tensor_copy(cx.dlw[:, 0:1024], cx.psA[:, :1024]))
            kb.step("vector", "dve", 1, lambda e: e.tensor_copy(cx.dlw[:, 1024:], cx.psB[:, :1024]))
            kb.step("vector", "dve", 1, lambda e: e.tensor_tensor(cx.duw[:WT, :], cx.dwin[:WT, :], cx.duw[:WT, :], ALU.mult))
            kb.wait("tensor", "dve")
            kb.step("tensor", "pe", 1, lambda e, off=off: e.transpose(
                cx.psT[0:128, 0:32], cx.xdblb[64:96, off:off + 128], cx.ident[64:96, 64:96]))
            kb.pe_fence()
            kb.wait("vector", "pe")
            kb.step("vector", "dve", 1, lambda e: e.tensor_copy(cx.bct[:WT, :], cx.psT[:WT, 0:32]))
            kb.step("vector", "dve", 1, lambda e: e.tensor_copy(cx.bctf[:], cx.bct[:]))
            kb.wait("gpsimd", "act")
            kb.wait("gpsimd", "dve")
            kb.wait("gpsimd", "dmaG", None)
            kb.step("gpsimd", "dmaG", 16, lambda e, wl=wl: e.dma_start(cx.gsl[:], cx.gA[16 * wl:16 * (wl + 1), :]))
            kb.wait("vector", "dmaG")
            kb.step("vector", "dve", 1, lambda e: e.tensor_tensor(
                cx.gsl[:], cx.hrun[:], cx.gsl[:], ALU.mult))

            for n in range(NST):
                kb.wait("vector", "pe")
                kb.wait("vector", "act")
                kb.step("vector", "dve", 1, lambda e, n=n: e.tensor_tensor(
                    cx.dwin[:], cx.dlw[:], cx.abc[n][:], ALU.mult))
                kb.wait("scalar", "dve")
                kb.step("scalar", "act", 1, lambda e: e.activation(cx.rg[:], cx.dwin[:], ACT.Exp))
                kb.wait("vector", "act")
                kb.wait("vector", "dmaG")
                kb.step("vector", "dve", 1, lambda e: e.tensor_tensor(
                    cx.ww[:WT, :], cx.rg[:WT, :], cx.duw[:WT, :], ALU.mult))
                kb.step("vector", "dve", 1, lambda e, n=n: e.tensor_scalar(
                    cx.statn[:], cx.trilsum[:], cx.bctf[:, n:n + 1], None, ALU.mult))
                kb.wait("gpsimd", "dve")
                kb.wait("gpsimd", "dmaG", None)
                kb.step("gpsimd", "dmaG", 16, lambda e, n=n: e.dma_start(
                    cx.ww[127:128, :], cx.gsl[n:n + 1, :]))
                kb.wait("scalar", "dve")
                kb.step("scalar", "act", 1, lambda e: e.activation(cx.rg[:], cx.dwin[:], ACT.Exp, bias=0.0, scale=-1.0))
                kb.wait("tensor", "act")
                wd, wg = kb.cnt["dve"], kb.cnt["dmaG"]
                kb.wait("tensor", "dve", wd)
                kb.wait("tensor", "dmaG", wg)
                for ch in range(2):
                    tgt = cx.psA if ch == 0 else cx.psB
                    for nn in range(2):
                        kb.step("tensor", "pe", 1, lambda e, ch=ch, nn=nn, tgt=tgt: e.matmul(
                            tgt[:, N_MM * nn:N_MM * (nn + 1)], cx.statn[:],
                            cx.ww[:, 1024 * ch + N_MM * nn:1024 * ch + N_MM * (nn + 1)],
                            start=True, stop=True))
                mmd = kb.pe_fence()
                kb.wait("vector", "pe", mmd)
                kb.step("vector", "dve", 1, lambda e: e.tensor_tensor(
                    cx.ww[:, 0:1024], cx.rg[:, 0:1024], cx.psA[:, :], ALU.mult))
                kb.step("vector", "dve", 1, lambda e: e.tensor_tensor(
                    cx.ww[:, 1024:], cx.rg[:, 1024:], cx.psB[:, :], ALU.mult))
                kb.wait("gpsimd", "dve")
                kb.step("gpsimd", "dmaG", 16, lambda e, n=n: e.dma_start(cx.hraw[n:n + 1, :], cx.ww[127:128, :]))
                kb.step("vector", "dve", 1, lambda e, n=n: e.scalar_tensor_tensor(
                    cx.yw[:WT, :], cx.ww[:WT, :], cx.bctf[:WT, NST + n:NST + n + 1], cx.yw[:WT, :],
                    ALU.mult, ALU.add))
            kb.wait("gpsimd", "dve")
            kb.wait("gpsimd", "dmaG", None)
            kb.step("gpsimd", "dmaG", 16, lambda e, wl=wl: e.dma_start(cx.gsl[:], cx.gB[16 * wl:16 * (wl + 1), :]))
            kb.wait("vector", "dmaG")
            kb.step("vector", "dve", 1, lambda e: e.tensor_tensor(
                cx.hrun[:], cx.hraw[:], cx.gsl[:], ALU.mult))
            kb.wait("tensor", "dve")
            for rnd in range(2):
                for cb8 in range(8):
                    cb = rnd * 8 + cb8
                    kb.step("tensor", "pe", 1, lambda e, cb=cb, cb8=cb8: e.transpose(
                        cx.psT[0:128, 128 * cb8:128 * (cb8 + 1)],
                        cx.yw[:, 128 * cb:128 * (cb + 1)], cx.ident[:]))
                kb.pe_fence()
                kb.wait("vector", "pe")
                for cb8 in range(8):
                    cb = rnd * 8 + cb8
                    kb.step("vector", "dve", 1, lambda e, cb=cb, cb8=cb8, wo_l=wo_l: e.tensor_tensor(
                        cx.wrhs[cb][:, WT * wo_l:WT * (wo_l + 1)],
                        cx.psT[:, 128 * cb8:128 * cb8 + WT],
                        cx.zsil[:, WT * cb:WT * (cb + 1)], ALU.mult))
                if rnd == 0:
                    kb.wait("tensor", "dve")
            if w == WOG[wog_idx].stop - 1:
                emit_wout(kb, cx, key, WOG[wog_idx])
                wog_idx += 1


def emit_wout(kb, cx, key, wog):
    vw = len(wog) * WT if wog.stop <= 16 else LASTV
    goff = wog.start * WT
    kb.wait("tensor", "dve")
    kb.wait("sync", "dmaG")
    slots = [(cx.psA, 0), (cx.psA, 512), (cx.psB, 0), (cx.psB, 512),
             (cx.psC, 0), (cx.psC, 512), (cx.psD, 0)]
    n0 = 0
    n1 = vw
    for mpass, ms in enumerate((range(0, 7), range(7, 8))):
        fcnt = {}
        for cb in range(16):
            sl = cx.wos[cb % 2]
            if cb >= 2:
                kb.wait("sync", "pe", fcnt[cb - 2])
            else:
                kb.wait("sync", "pe")
            kb.wait("sync", "dve")
            kb.step("sync", "dmaL", 16, lambda e, cb=cb, sl=sl: e.dma_start(
                sl[:], key("wo_hbm")[128 * cb:128 * (cb + 1), :]))
            kb.wait("tensor", "dmaL")
            for m in ms:
                ps, po = slots[m - ms.start]
                kb.step("tensor", "pe", 1, lambda e, m=m, cb=cb, n0=n0, n1=n1, ps=ps, po=po, sl=sl: e.matmul(
                    ps[:, po:po + (n1 - n0)],
                    sl[:, 128 * m:128 * (m + 1)],
                    cx.wrhs[cb][:, n0:n1], start=(cb == 0), stop=(cb == 15)))
            fcnt[cb] = kb.pe_fence()
        kb.wait("vector", "pe")
        for m in ms:
            ps, po = slots[m - ms.start]
            kb.step("vector", "dve", 1, lambda e, m=m, goff=goff, n0=n0, n1=n1, ps=ps, po=po: e.tensor_tensor(
                cx.xm[m][:, goff + n0:goff + n1], cx.xm[m][:, goff + n0:goff + n1],
                ps[:, po:po + (n1 - n0)], ALU.add))
        kb.wait("tensor", "dve")


def emit_x2(kb, cx, key):
    for kt in range(8):
        for h in range(2):
            kb.wait_all("sync")
            kb.step("sync", "dmaL", 16, lambda e, kt=kt, h=h: e.dma_start(
                cx.w32[:, 0:1024], key("XA")[128 * kt:128 * (kt + 1), 3 + 1024 * h:3 + 1024 * (h + 1)]))
            kb.wait("vector", "dmaL")
            kb.step("vector", "dve", 1, lambda e, kt=kt, h=h: e.scalar_tensor_tensor(
                cx.w32[:, 0:1024], cx.xm[kt][:, 1024 * h:1024 * (h + 1)],
                cx.mcols[:, 16 + kt:17 + kt], cx.w32[:, 0:1024], ALU.mult, ALU.add))
            kb.step("vector", "dve", 1, lambda e, kt=kt, h=h: e.tensor_copy(
                cx.xm[kt][:, 1024 * h:1024 * (h + 1)], cx.w32[:, 0:1024]))
            kb.wait_all("gpsimd")
            kb.step("gpsimd", "dmaG", 16, lambda e, kt=kt, h=h: e.dma_start(
                key("x2_hbm")[kt, :, 1024 * h:1024 * (h + 1)], cx.w32[:, 0:1024]))


def emit_ffn_wcast(kb, cx, key):
    kb.wait_all("sync")
    kb.step("sync", "dmaL", 16, lambda e: e.dma_start(cx.w32[:, 0:32], key("b1T")[:]))
    kb.step("sync", "dmaL", 16, lambda e: e.dma_start(cx.w32[:, 32:40], key("b2T")[:]))
    kb.wait("vector", "dmaL")
    kb.step("vector", "dve", 1, lambda e: e.tensor_copy(cx.w1p[:, 0:40], cx.w32[:, 0:40]))
    kb.step("vector", "dve", 1, lambda e: e.tensor_copy(cx.bq[:, 0:40], cx.w1p[:, 0:40]))
    for kt in range(8):
        for c in range(4):
            kb.wait_all("sync")
            kb.step("sync", "dmaL", 16, lambda e, kt=kt, c=c: e.dma_start(
                cx.w32[:, 0:1024], key("w1T")[128 * kt:128 * (kt + 1), 1024 * c:1024 * (c + 1)]))
            kb.wait("vector", "dmaL")
            kb.wait("vector", "dmaG")
            kb.step("vector", "dve", 1, lambda e: e.tensor_copy(cx.w1p[:, 0:1024], cx.w32[:, 0:1024]))
            kb.wait_all("gpsimd")
            for mj in range(8):
                kb.step("gpsimd", "dmaG", 16, lambda e, kt=kt, c=c, mj=mj: e.dma_start(
                    key("w1q_hbm")[8 * c + mj, :, 128 * kt:128 * (kt + 1)],
                    cx.w1p[:, 128 * mj:128 * (mj + 1)]))
    for m in range(32):
        kb.wait_all("sync")
        kb.step("sync", "dmaL", 16, lambda e, m=m: e.dma_start(
            cx.w32[:, 0:1024], key("w2T")[128 * m:128 * (m + 1), :]))
        kb.wait("vector", "dmaL")
        kb.wait("vector", "dmaG")
        kb.step("vector", "dve", 1, lambda e: e.tensor_copy(cx.w1p[:, 0:1024], cx.w32[:, 0:1024]))
        kb.wait_all("gpsimd")
        kb.step("gpsimd", "dmaG", 16, lambda e, m=m: e.dma_start(key("w2q_hbm")[m, :, :], cx.w1p[:, 0:1024]))


def emit_ffn(kb, cx, key):
    slots = [(cx.psA, 0), (cx.psA, 512), (cx.psB, 0), (cx.psB, 512),
             (cx.psC, 0), (cx.psC, 512), (cx.psD, 0)]
    for tch in range(4):
        toff = 512 * tch
        kb.wait_all("sync")
        for kt in range(8):
            kb.step("sync", "dmaL", 16, lambda e, kt=kt, toff=toff: e.dma_start(
                cx.h2f8[:, 512 * kt:512 * (kt + 1)], key("h2_hbm")[kt, :, toff:toff + 512]))
        for m in range(32):
            kb.wait_all("sync")
            kb.step("sync", "dmaL", 16, lambda e, m=m: e.dma_start(
                cx.w1p[:, 0:1024], key("w1q_hbm")[m, :, :]))
            kb.wait("tensor", "dmaL")
            kb.wait("tensor", "dve")
            for kt in range(8):
                kb.step("tensor", "pe", 1, lambda e, m=m, kt=kt: e.matmul(
                    cx.psD[:, 0:512], cx.w1p[:, 128 * kt:128 * (kt + 1)],
                    cx.h2f8[:, 512 * kt:512 * (kt + 1)], start=(kt == 0), stop=(kt == 7)))
            kb.pe_fence()
            kb.wait("vector", "pe")
            kb.step("vector", "dve", 1, lambda e, m=m: e.tensor_scalar(
                cx.dwin[:, 0:512], cx.psD[:, 0:512], cx.bq[:, m:m + 1], 0.0, ALU.add, ALU.max))
            kb.wait("vector", "dmaG")
            kb.step("vector", "dve", 1, lambda e: e.tensor_copy(cx.h1q8[:], cx.dwin[:, 0:512]))
            kb.wait_all("gpsimd")
            kb.step("gpsimd", "dmaG", 16, lambda e, m=m: e.dma_start(key("h1_hbm")[m, :, :], cx.h1q8[:]))
        for dmg in (range(0, 7), range(7, 8)):
            for m in range(32):
                kb.pe_fence()
                kb.wait_all("sync")
                kb.step("sync", "dmaL", 16, lambda e, m=m: e.dma_start(
                    cx.w1p[:, 0:512], key("h1_hbm")[m, :, :]))
                kb.step("sync", "dmaL", 16, lambda e, m=m, dmg=dmg: e.dma_start(
                    cx.w2s[:, 0:128 * len(dmg)],
                    key("w2q_hbm")[m, :, 128 * dmg.start:128 * dmg.stop]))
                kb.wait("tensor", "dmaL")
                kb.wait("tensor", "dve")
                for dm in dmg:
                    ps, po = slots[dm - dmg.start]
                    kb.step("tensor", "pe", 1, lambda e, dm=dm, m=m, ps=ps, po=po, dmg=dmg: e.matmul(
                        ps[:, po:po + 512], cx.w2s[:, 128 * (dm - dmg.start):128 * (dm - dmg.start + 1)],
                        cx.w1p[:, 0:512], start=(m == 0), stop=(m == 31)))
            kb.pe_fence()
            kb.wait("vector", "pe")
            for dm in dmg:
                ps, po = slots[dm - dmg.start]
                kb.wait_all("sync")
                kb.step("sync", "dmaL", 16, lambda e, dm=dm, toff=toff: e.dma_start(
                    cx.w32[:, 512:1024], key("x2_hbm")[dm, :, toff:toff + 512]))
                kb.wait_all("gpsimd")
                kb.step("gpsimd", "dmaG", 16, lambda e, dm=dm, toff=toff: e.dma_start(
                    cx.dlw[:, 0:512], key("maskT")[dm, :, toff:toff + 512]))
                kb.wait("vector", "dmaL")
                kb.wait("vector", "dmaG")
                kb.step("vector", "dve", 1, lambda e, dm=dm, ps=ps, po=po: e.tensor_scalar(
                    cx.dwin[:, 0:512], ps[:, po:po + 512], cx.bq[:, 32 + dm:33 + dm], None, ALU.add))
                kb.step("vector", "dve", 1, lambda e: e.tensor_tensor(
                    cx.w32[:, 0:512], cx.dwin[:, 0:512], cx.dlw[:, 0:512], ALU.mult))
                kb.step("vector", "dve", 1, lambda e, dm=dm: e.scalar_tensor_tensor(
                    cx.w32[:, 512:1024], cx.w32[:, 0:512], cx.mcols[:, 40 + dm:41 + dm],
                    cx.w32[:, 512:1024], ALU.mult, ALU.add))
                kb.wait_all("gpsimd")
                kb.step("gpsimd", "dmaG", 16, lambda e, dm=dm, toff=toff: e.dma_start(
                    key("OUT")[dm, :, toff:toff + 512], cx.w32[:, 512:1024]))


def build_program(fused):
    nc = bass.Bass("TRN2", target_bir_lowering=False)
    dram = {}

    def d(name, shape, dt, kind="Internal"):
        dname = "i_" + name if kind == "ExternalInput" else name
        dram[name] = nc.dram_tensor(dname, shape, dt, kind=kind)
        return dram[name]

    d("XA", [1024, 2051], F32, kind="ExternalInput")
    d("ccol", [128, 8], F32, kind="ExternalInput")
    d("wmod", [1024, 6144], F32, kind="ExternalInput")
    d("fcols", [128, 2], F32, kind="ExternalInput")
    for s in range(NSLOT):
        d(f"winT{s}", [1024, 4096], F32, kind="ExternalInput")
        d(f"wxT{s}", [128, 1536], F32, kind="ExternalInput")
        d(f"wdtTb{s}", [128, 2048], F32, kind="ExternalInput")
        d(f"woT{s}", [2048, 1024], F32, kind="ExternalInput")
        d(f"cw{s}", [128, 64], F32, kind="ExternalInput")
        d(f"cb{s}", [128, 16], F32, kind="ExternalInput")
        d(f"alogT{s}", [16, 2048], F32, kind="ExternalInput")
        d(f"dpar{s}", [1, 2048], F32, kind="ExternalInput")
        if fused:
            d(f"arinH{s}", [16, 2048], F16)
            d(f"aroutH{s}", [16, 2048], F16)
            d(f"arinX{s}", [128, 32], F32)
            d(f"aroutX{s}", [128, 32], F32)
        else:
            d(f"arinH{s}", [16, 2048], F16, kind="ExternalOutput")
            d(f"aroutH{s}", [16, 2048], F16, kind="ExternalInput")
            d(f"arinX{s}", [128, 32], F32, kind="ExternalOutput")
            d(f"aroutX{s}", [128, 32], F32, kind="ExternalInput")
    d("w1T", [1024, 4096], F32, kind="ExternalInput")
    d("b1T", [128, 32], F32, kind="ExternalInput")
    d("w2T", [4096, 1024], F32, kind="ExternalInput")
    d("b2T", [128, 8], F32, kind="ExternalInput")
    d("maskT", [8, 128, 2048], F16, kind="ExternalInput")
    for n, a in host_consts().items():
        dt = {np.dtype("float32"): F32, np.dtype(ml_dtypes.bfloat16): BF16,
              np.dtype("float16"): F16}[a.dtype]
        d(n, list(a.shape), dt, kind="ExternalInput")
    d("u_hbm", [16, 128, TP], F16)
    d("z_hbm", [16, 128, TP], F16)
    d("d_hbm", [17, 128, 2048], F16)
    d("wo_hbm", [2048, 1024], F16)
    d("x2_hbm", [8, 128, 2048], F32)
    d("h2_hbm", [8, 128, 2048], F8)
    d("h1_hbm", [32, 128, 512], F8)
    d("w1q_hbm", [32, 128, 1024], F8)
    d("w2q_hbm", [32, 128, 1024], F8)
    d("OUT", [8, 128, 2048], F32, kind="ExternalOutput")
    if DEBUG:
        d("DBGM", [128, 48], F32, kind="ExternalOutput")
        d("DBGH", [8, 128, 2048], F16, kind="ExternalOutput")
        d("DBGS", [8, 128, 2048], F16, kind="ExternalOutput")
        d("DBGX2", [8, 128, 2048], F32, kind="ExternalOutput")
        d("DBGH2", [8, 128, 2048], F8, kind="ExternalOutput")

    kb = KB(nc)
    cx = Ctx(nc)

    def mkkey(s):
        def key2(name):
            if name in ("winT", "wxT", "wdtTb", "woT", "cw", "cb", "alogT", "dpar"):
                return dram[f"{name}{s}"]
            return dram[name]
        return key2

    gkey = mkkey(0)

    with nc.Block() as block:
        emit_load_consts(kb, cx, gkey)
        emit_modvecs(kb, cx, gkey)
        if DEBUG:
            kb.wait_all("gpsimd")
            kb.step("gpsimd", "dmaG", 16, lambda e: e.dma_start(dram["DBGM"][:], cx.mcols[:]))
        emit_ln(kb, cx, gkey, phase=1)
        if DEBUG:
            kb.wait_all("gpsimd")
            for kt in range(8):
                kb.step("gpsimd", "dmaG", 16, lambda e, kt=kt: e.dma_start(dram["DBGH"][kt, :, :], cx.xm[kt][:]))
        for s in range(NSLOT):
            key = mkkey(s)
            emit_slot_head(kb, cx, key, s)
            emit_layer(kb, cx, key, pre_p3=lambda s=s, key=key: emit_ar_consume(kb, cx, key, s))
        if DEBUG:
            kb.wait_all("gpsimd")
            for kt in range(8):
                kb.step("gpsimd", "dmaG", 16, lambda e, kt=kt: e.dma_start(dram["DBGS"][kt, :, :], cx.xm[kt][:]))
        emit_x2(kb, cx, gkey)
        emit_ln(kb, cx, gkey, phase=2)
        if DEBUG:
            kb.wait_all("sync")
            kb.wait_all("gpsimd")
            kb.step("gpsimd", "dmaG", 16, lambda e: e.dma_start(dram["DBGX2"][:], dram["x2_hbm"][:]))
            kb.step("gpsimd", "dmaG", 16, lambda e: e.dma_start(dram["DBGH2"][:], dram["h2_hbm"][:]))
        emit_ffn_wcast(kb, cx, gkey)
        emit_ffn(kb, cx, gkey)
        kb.wait("gpsimd", "dmaG")
        kb.wait("sync", "dmaL")
        kb.run_into_block(block)
    return nc


def host_inputs(inputs):
    f32 = np.float32
    x = np.asarray(inputs["L0_attn_x"], f32)
    W_in = np.asarray(inputs["L0_attn_W_in"], f32)
    conv_w = np.asarray(inputs["L0_attn_conv_w"], f32)
    conv_b = np.asarray(inputs["L0_attn_conv_b"], f32)
    W_x = np.asarray(inputs["L0_attn_W_x"], f32)
    W_dt = np.asarray(inputs["L0_attn_W_dt"], f32)
    b_dt = np.asarray(inputs["L0_attn_b_dt"], f32)
    A_log = np.asarray(inputs["L0_attn_A_log"], f32)
    D_par = np.asarray(inputs["L0_attn_D_par"], f32)
    W_out = np.asarray(inputs["L0_attn_W_out"], f32)
    w1 = np.asarray(inputs["L0_ffn_w1"], f32)
    b1 = np.asarray(inputs["L0_ffn_b1"], f32)
    w2 = np.asarray(inputs["L0_ffn_w2"], f32)
    b2 = np.asarray(inputs["L0_ffn_b2"], f32)
    c = np.asarray(inputs["L0_c"], f32)
    wmod = np.asarray(inputs["L0_wmod"], f32)

    def layer_set(i):
        if i is None:
            return {
                "winT": np.zeros((1024, 4096), f32), "wxT": np.zeros((128, 1536), f32),
                "wdtTb": np.zeros((128, 2048), f32), "woT": np.zeros((2048, 1024), f32),
                "cw": np.zeros((128, 64), f32), "cb": np.zeros((128, 16), f32),
                "alogT": np.zeros((16, 2048), f32), "dpar": np.zeros((1, 2048), f32),
            }
        return {
            "winT": np.ascontiguousarray(W_in[i].T),
            "wxT": np.ascontiguousarray(W_x[i].T.reshape(16, 128, 96).transpose(1, 0, 2).reshape(128, 16 * 96)),
            "wdtTb": np.ascontiguousarray(np.concatenate(
                [W_dt[i].T, np.zeros((32, 2048), f32), b_dt[i][None, :], np.zeros((31, 2048), f32)], 0)),
            "woT": np.ascontiguousarray(W_out[i].T),
            "cw": np.ascontiguousarray(conv_w[i].reshape(16, 128, 4).transpose(1, 0, 2).reshape(128, 64)),
            "cb": np.ascontiguousarray(conv_b[i].reshape(16, 128).T),
            "alogT": np.ascontiguousarray(A_log[i].T),
            "dpar": np.ascontiguousarray(D_par[i][None, :]),
        }

    lsets = [layer_set(i) for i in range(8)]
    zset = layer_set(None)
    consts = host_consts()

    import jax
    _k42 = lambda: jax.random.key(42, impl="threefry2x32")
    try:
        with jax.default_device(jax.local_devices(backend="cpu")[0]):
            keep = np.asarray(jax.random.bernoulli(_k42(), 0.8, (4, 4096, 1024)))
    except Exception:
        keep = np.asarray(jax.random.bernoulli(_k42(), 0.8, (4, 4096, 1024)))
    maskf = keep.astype(np.float16) * np.float16(1.25)

    shared = {
        "wmod": np.ascontiguousarray(wmod),
        "w1T": np.ascontiguousarray(w1.T),
        "b1T": np.ascontiguousarray(b1.reshape(32, 128).T),
        "w2T": np.ascontiguousarray(w2.T),
        "b2T": np.ascontiguousarray(b2.reshape(8, 128).T),
    }
    shared.update(consts)

    per_core = []
    for core in range(8):
        b, hf = core // 2, core % 2
        off = 2048 * hf
        XA = np.zeros((1024, 2051), f32)
        XA[:, 3:] = x[b, off:off + 2048].T
        if hf:
            XA[:, 0:3] = x[b, 2045:2048].T
        fcols = np.zeros((128, 2), f32)
        fcols[:, 0] = float(hf)
        fcols[:, 1] = 1.0 - float(hf)
        m = {
            "XA": XA,
            "ccol": np.ascontiguousarray(c[b].reshape(8, 128).T),
            "fcols": fcols,
            "maskT": np.ascontiguousarray(maskf[b, off:off + 2048].T.reshape(8, 128, 2048)),
        }
        m.update(shared)
        for s in range(NSLOT):
            li = s if hf == 0 else s - 1
            src = zset if (li < 0 or li > 7) else lsets[li]
            for kname, arr in src.items():
                m[f"{kname}{s}"] = arr
        if not FUSED:
            for s in range(NSLOT):
                m[f"aroutH{s}"] = np.zeros((16, 2048), np.float16)
                m[f"aroutX{s}"] = np.zeros((128, 32), f32)
        per_core.append(m)
    return per_core


def _prefix(per_core):
    return [{("i_" + k): v for k, v in m.items()} for m in per_core]


def kernel(**inputs):
    per_core = host_inputs(inputs)
    nc = build_program(FUSED)
    out = np.empty((4, 4096, 1024), np.float32)
    if FUSED:
        res = run_bass_kernel_spmd(nc, _prefix(per_core), core_ids=list(range(8)))
        for core in range(8):
            b, hf = core // 2, core % 2
            arr = np.asarray(res.results[core]["OUT"])
            out[b, 2048 * hf:2048 * (hf + 1)] = arr.transpose(2, 0, 1).reshape(2048, 1024)
    else:
        res1 = run_bass_kernel_spmd(nc, _prefix(per_core), core_ids=list(range(8)))
        per2 = []
        for core in range(8):
            b, hf = core // 2, core % 2
            m = dict(per_core[core])
            if hf == 1:
                r0 = res1.results[core - 1]
                for s in range(NSLOT):
                    m[f"aroutH{s}"] = np.asarray(r0[f"arinH{s}"])
                    m[f"aroutX{s}"] = np.asarray(r0[f"arinX{s}"])
            per2.append(m)
        res2 = run_bass_kernel_spmd(nc, _prefix(per2), core_ids=list(range(8)))
        for core in range(8):
            b, hf = core // 2, core % 2
            r = (res1 if hf == 0 else res2).results[core]
            arr = np.asarray(r["OUT"])
            out[b, 2048 * hf:2048 * (hf + 1)] = arr.transpose(2, 0, 1).reshape(2048, 1024)
    return out


# version 04
# speedup vs baseline: 1.4647x; 1.0854x over previous
import numpy as np
import ml_dtypes
import concourse.bass as bass
import concourse.mybir as mybir
from concourse.bass_utils import run_bass_kernel_spmd

F32 = mybir.dt.float32
F16 = mybir.dt.float16
BF16 = mybir.dt.bfloat16
F8 = mybir.dt.float8e4
ALU = mybir.AluOpType
ACT = mybir.ActivationFunctionType

FUSED = True
DEBUG = False

Lc = 2048
D = 1024
CC = 2048
NST = 16
WT = 127
NW = 17
LASTV = Lc - (NW - 1) * WT
N_MM = 512
TP = 2176
NSLOT = 9
GRPW = [range(0, 8), range(8, 16), range(16, 17)]
WOG = [range(2 * i, 2 * i + 2) for i in range(8)] + [range(16, 17)]
ENGINES = ("sync", "vector", "scalar", "tensor", "gpsimd")
PERSLOT = ("winT", "wxT", "wdtTb", "woT", "cw", "cb", "alogT", "dpar",
           "arinH", "aroutH", "arinX", "aroutX")


class KB:
    PARANOID = ()

    def __init__(self, nc):
        self.nc = nc
        self.ops = {e: [] for e in ENGINES}
        self.cnt = {}
        self.sems = {}
        self.lastw = {}

    def sem(self, name):
        if name not in self.sems:
            self.sems[name] = self.nc.semaphore(name).__enter__()
            self.cnt[name] = 0
        return self.sems[name]

    def wait(self, engine, name, value=None):
        h = self.sem(name)
        v = self.cnt[name] if value is None else value
        if v > 0:
            self.ops[engine].append(lambda eng, h=h, v=v: eng.wait_ge(h, v))

    REGION = True
    OWN = {"vector": "dve", "scalar": "act", "tensor": "pe"}

    def wait_all(self, engine):
        wl = self.lastw.setdefault(engine, {})
        for nm in list(self.cnt):
            v = self.cnt.get(nm, 0)
            if v > 0 and nm != self.OWN.get(engine) and wl.get(nm, 0) < v:
                self.wait(engine, nm)
                wl[nm] = v

    def step(self, engine, name, amount, fn):
        if (self.PARANOID is True or (self.PARANOID and engine in self.PARANOID)) and self.REGION:
            self.wait_all(engine)
        if engine == "tensor" and name == "pe":
            self.sem(name)
            self.ops[engine].append(lambda eng, fn=fn: fn(eng))
            return self.cnt[name]
        h = self.sem(name)
        self.cnt[name] = self.cnt.get(name, 0) + amount
        if engine in ("vector", "scalar", "tensor") and name != "dmaG" and name != "dmaL":
            def wrapped(eng, fn=fn, h=h, a=amount):
                fn(eng)
                eng.drain().then_inc(h, a)
        else:
            def wrapped(eng, fn=fn, h=h, a=amount):
                fn(eng).then_inc(h, a)
        self.ops[engine].append(wrapped)
        return self.cnt[name]

    def tensor_drain(self):
        self.ops["tensor"].append(lambda eng: eng.drain())

    def pe_fence(self):
        h = self.sem("pe")
        self.cnt["pe"] = self.cnt.get("pe", 0) + 1
        def wrapped(eng, h=h):
            eng.drain().then_inc(h, 1)
        self.ops["tensor"].append(wrapped)
        return self.cnt["pe"]

    def run_into_block(self, block):
        for e in ENGINES:
            ops = self.ops[e]
            if not ops:
                continue
            def make(ops):
                def body(eng):
                    for f in ops:
                        f(eng)
                return body
            getattr(block, e)(make(ops))


def sbuf(nc, name, shape, dt):
    return nc.sbuf_tensor(name, shape, dt).__enter__()


def pst(nc, name, shape, dt):
    return nc.psum_tensor(name, shape, dt).__enter__()


class Ctx:
    def __init__(self, nc):
        self.nc = nc
        self.xm = [sbuf(nc, f"xm{i}", [128, Lc], F16) for i in range(8)]
        self.ustg = sbuf(nc, "ustg", [128, 2048], F16)
        self.zsil = sbuf(nc, "zsil", [128, 16 * WT], F16)
        self.dwin = sbuf(nc, "dwin", [128, CC], F16)
        self.dlw = sbuf(nc, "dlw", [128, CC], F16)
        self.duw = sbuf(nc, "duw", [128, CC], F16)
        self.rg = sbuf(nc, "rg", [128, CC], BF16)
        self.ww = sbuf(nc, "ww", [128, CC], BF16)
        self.yw = sbuf(nc, "yw", [128, CC], F16)
        self.statn = sbuf(nc, "statn", [128, 128], BF16)
        self.trilmid = sbuf(nc, "trilmid", [128, 128], F16)
        self.trilsum = sbuf(nc, "trilsum", [128, 128], BF16)
        self.ident = sbuf(nc, "ident", [128, 128], F16)
        self.bct = sbuf(nc, "bct", [128, 2 * NST], F16)
        self.bctf = sbuf(nc, "bctf", [128, 2 * NST], F32)
        self.abc = [sbuf(nc, f"abc{i}", [128, CC], BF16) for i in range(NST)]
        self.alpha16 = sbuf(nc, "alpha16", [17, CC], F16)
        self.alrep = sbuf(nc, "alrep", [128, CC], BF16)
        self.gA = sbuf(nc, "gA", [128, CC], BF16)
        self.gB = sbuf(nc, "gB", [128, CC], BF16)
        _dmde = sbuf(nc, "dmde", [96, 1024], F16)
        self.dmde = [_dmde[32 * g:32 * (g + 1)] for g in range(3)]
        self.hselA = sbuf(nc, "hselA", [128, 8 * 32], F16)
        self.hselB = sbuf(nc, "hselB", [128, 8 * 32], F16)
        self.indA0 = sbuf(nc, "indA0", [96, 128], F16)
        self.indA1 = sbuf(nc, "indA1", [96, 128], F16)
        self.indB0 = sbuf(nc, "indB0", [96, 128], F16)
        self.indB1 = sbuf(nc, "indB1", [96, 128], F16)
        self.rep8 = sbuf(nc, "rep8", [NST, 128], F16)
        self.maskw16 = sbuf(nc, "maskw16", [128, 1], F32)
        self.hrun = sbuf(nc, "hrun", [NST, CC], F16)
        self.hraw = sbuf(nc, "hraw", [NST, CC], BF16)
        self.gsl = sbuf(nc, "gsl", [NST, CC], BF16)
        self.dpb = sbuf(nc, "dpb", [128, CC], F16)
        self.xdblb = sbuf(nc, "xdblb", [128, Lc + WT], F16)
        self.wrhs = [sbuf(nc, f"wrhs{i}", [128, 2 * WT], F16) for i in range(16)]
        self.w32 = sbuf(nc, "w32", [128, 1024], F32)
        self.wkp = [sbuf(nc, f"wkp{k}", [128, 256], F16) for k in range(8)]
        self.convw = sbuf(nc, "convw", [128, 16 * 4], F32)
        self.convb = sbuf(nc, "convb", [128, 16], F32)
        self.wxtb = sbuf(nc, "wxtb", [128, 16 * 96], F16)
        self.wdtb = sbuf(nc, "wdtb", [128, CC], F16)
        self.xhead = sbuf(nc, "xhead", [128, 3], F16)
        self.zpad = sbuf(nc, "zpad", [128, TP - Lc], F16)
        self.wos = [sbuf(nc, f"wos{k}", [128, D], F16) for k in range(2)]
        self.xmt = sbuf(nc, "xmt", [128, 32], F16)
        self.ones1 = sbuf(nc, "ones1", [1, 128], F16)
        self.onescol = sbuf(nc, "onescol", [128, 1], F16)
        self.xtl = sbuf(nc, "xtl", [128, 32], F16)
        self.mcols = sbuf(nc, "mcols", [128, 48], F32)
        self.fcols = sbuf(nc, "fcols", [128, 2], F32)
        self.scol = sbuf(nc, "scol", [128, 16], F16)
        self.h2f8 = sbuf(nc, "h2f8", [128, 4096], F8)
        self.w1p = sbuf(nc, "w1p", [128, 1024], F8)
        self.w2s = sbuf(nc, "w2s", [128, 1024], F8)
        self.h1q8 = sbuf(nc, "h1q8", [128, 512], F8)
        self.bq = sbuf(nc, "bq", [128, 40], F32)
        self.psA = pst(nc, "psA", [128, 1024], F32)
        self.psB = pst(nc, "psB", [128, 1024], F32)
        self.psC = pst(nc, "psC", [128, 1024], F32)
        self.psD = pst(nc, "psD", [128, 512], F32)
        self.psT = pst(nc, "psT", [128, 1024], F16)


def host_consts():
    bf = ml_dtypes.bfloat16
    trilmid = np.zeros((128, 128), np.float32)
    for t in range(WT):
        for s in range(WT):
            trilmid[s, t] = (1.0 if s <= t else 0.0) - (1.0 if s <= 63 else 0.0)
    trilsum = np.zeros((128, 128), np.float32)
    for t in range(WT):
        trilsum[: t + 1, t] = 1.0
    trilsum[127, :] = 1.0
    trilsum[:WT, 127] = 1.0
    ident = np.eye(128, dtype=np.float32)
    hselA = np.zeros((128, 8 * 32), np.float32)
    hselB = np.zeros((128, 8 * 32), np.float32)
    for wl in range(8):
        hselA[:64, 32 * wl + 2 * wl] = 1.0
        hselA[64:WT, 32 * wl + 2 * wl + 1] = 1.0
        hselB[:64, 32 * wl + 16 + 2 * wl] = 1.0
        hselB[64:WT, 32 * wl + 17 + 2 * wl] = 1.0
    indA0 = np.zeros((32, 128), np.float32)
    indA1 = np.zeros((32, 128), np.float32)
    indB0 = np.zeros((32, 128), np.float32)
    indB1 = np.zeros((32, 128), np.float32)
    for wl in range(8):
        indA0[2 * wl, 16 * wl:16 * wl + 16] = 1.0
        indA1[16 + 2 * wl, 16 * wl:16 * wl + 16] = 1.0
        indB0[2 * wl + 1, 16 * wl:16 * wl + 16] = 1.0
        indB1[16 + 2 * wl + 1, 16 * wl:16 * wl + 16] = 1.0
    rep8 = np.zeros((16, 128), np.float32)
    for wl in range(8):
        for n in range(16):
            rep8[n, 16 * wl + n] = 1.0
    maskw16 = np.zeros((128, 1), np.float32)
    maskw16[:LASTV] = 1.0
    f16 = np.float16
    onesrow = np.zeros((1, Lc + WT), np.float16)
    onesrow[0, :Lc] = 1.0
    return {
        "trilmid": trilmid.astype(f16), "trilsum": trilsum.astype(bf),
        "ident": ident.astype(f16), "hselA": hselA.astype(f16), "hselB": hselB.astype(f16),
        "indA0": np.tile(indA0, (3, 1)).astype(f16), "indA1": np.tile(indA1, (3, 1)).astype(f16),
        "indB0": np.tile(indB0, (3, 1)).astype(f16), "indB1": np.tile(indB1, (3, 1)).astype(f16),
        "rep8": rep8.astype(f16), "maskw16": maskw16,
        "ones1": np.ones((1, 128), np.float32).astype(f16),
        "onescol": np.ones((128, 1), np.float32).astype(f16),
        "onesrow": onesrow,
    }


CONST_NAMES = ("trilmid", "trilsum", "ident", "hselA", "hselB",
               "indA0", "indA1", "indB0", "indB1", "rep8", "maskw16",
               "ones1", "onescol")


def emit_load_consts(kb, cx, key):
    for name in CONST_NAMES:
        t = getattr(cx, name)
        kb.step("sync", "dmaL", 16, lambda e, t=t, n=name: e.dma_start(t[:], key(n)[:]))
    kb.step("sync", "dmaL", 16, lambda e: e.dma_start(cx.fcols[:], key("fcols")[:]))
    kb.wait("vector", "dmaL")
    kb.step("vector", "dve", 1, lambda e: e.memset(cx.zpad[:], 0.0))
    kb.step("vector", "dve", 1, lambda e: e.memset(cx.hrun[:], 0.0))
    kb.wait("gpsimd", "dve")
    kb.step("gpsimd", "dmaG", 16, lambda e: e.dma_start(cx.bct[127:128, :], key("onesrow")[0:1, 0:2 * NST]))


def emit_modvecs(kb, cx, key):
    kb.wait_all("sync")
    kb.step("sync", "dmaL", 16, lambda e: e.dma_start(cx.w32[:, 0:8], key("ccol")[:]))
    kb.wait_all("scalar")
    kb.step("scalar", "act", 1, lambda e: e.activation(cx.w32[:, 8:16], cx.w32[:, 0:8], ACT.Silu))
    kb.wait("vector", "act")
    kb.step("vector", "dve", 1, lambda e: e.memset(cx.scol[:], 0.0))
    for k in range(8):
        kb.step("vector", "dve", 1, lambda e, k=k: e.tensor_copy(cx.scol[:, 2 * k:2 * k + 1], cx.w32[:, 8 + k:9 + k]))
    for jc in range(6):
        for kt in range(8):
            kb.wait_all("sync")
            kb.step("sync", "dmaL", 16, lambda e, jc=jc, kt=kt: e.dma_start(
                cx.w32[:, 0:1024], key("wmod")[128 * kt:128 * (kt + 1), 1024 * jc:1024 * (jc + 1)]))
            kb.wait("vector", "dmaL")
            kb.step("vector", "dve", 1, lambda e: e.tensor_copy(cx.duw[:, 0:1024], cx.w32[:, 0:1024]))
            kb.wait_all("tensor")
            for jj in range(8):
                kb.step("tensor", "pe", 1, lambda e, jj=jj, kt=kt: e.matmul(
                    cx.psD[:, 2 * jj:2 * jj + 2], cx.duw[:, 128 * jj:128 * (jj + 1)],
                    cx.scol[:, 2 * kt:2 * kt + 2], start=True, stop=True))
            kb.pe_fence()
            kb.wait("vector", "pe")
            for jj in range(8):
                if kt == 0:
                    kb.step("vector", "dve", 1, lambda e, jc=jc, jj=jj: e.tensor_copy(
                        cx.mcols[:, 8 * jc + jj:8 * jc + jj + 1], cx.psD[:, 2 * jj:2 * jj + 1]))
                else:
                    kb.step("vector", "dve", 1, lambda e, jc=jc, jj=jj: e.tensor_tensor(
                        cx.mcols[:, 8 * jc + jj:8 * jc + jj + 1],
                        cx.mcols[:, 8 * jc + jj:8 * jc + jj + 1],
                        cx.psD[:, 2 * jj:2 * jj + 1], ALU.add))
    kb.step("vector", "dve", 1, lambda e: e.tensor_scalar(cx.mcols[:, 8:16], cx.mcols[:, 8:16], 1.0, None, ALU.add))
    kb.step("vector", "dve", 1, lambda e: e.tensor_scalar(cx.mcols[:, 32:40], cx.mcols[:, 32:40], 1.0, None, ALU.add))
    for base in (16, 40):
        kb.step("vector", "dve", 1, lambda e, base=base: e.tensor_scalar(
            cx.w32[:, 0:8], cx.mcols[:, base:base + 8], 0.0, None, ALU.is_ge))
        kb.step("vector", "dve", 1, lambda e: e.tensor_scalar(
            cx.w32[:, 0:8], cx.w32[:, 0:8], 2.0, -1.0, ALU.mult, ALU.add))
        kb.step("vector", "dve", 1, lambda e, base=base: e.tensor_scalar(
            cx.w32[:, 8:16], cx.mcols[:, base:base + 8], -1.0, None, ALU.mult))
        kb.step("vector", "dve", 1, lambda e, base=base: e.tensor_tensor(
            cx.w32[:, 8:16], cx.mcols[:, base:base + 8], cx.w32[:, 8:16], ALU.max))
        kb.step("vector", "dve", 1, lambda e: e.tensor_scalar(
            cx.w32[:, 8:16], cx.w32[:, 8:16], 0.25, None, ALU.max))
        kb.step("vector", "dve", 1, lambda e, base=base: e.tensor_tensor(
            cx.mcols[:, base:base + 8], cx.w32[:, 0:8], cx.w32[:, 8:16], ALU.mult))


def emit_ln(kb, cx, key, phase):
    if phase == 1:
        kb.wait("vector", "dmaL")
        kb.step("vector", "dve", 1, lambda e: e.memset(cx.xtl[:], 0.0))
        for kt in range(8):
            for h in range(2):
                kb.wait_all("sync")
                kb.step("sync", "dmaL", 16, lambda e, kt=kt, h=h: e.dma_start(
                    cx.w32[:, 0:1024], key("XA")[128 * kt:128 * (kt + 1), 3 + 1024 * h:3 + 1024 * (h + 1)]))
                kb.wait("vector", "dmaL")
                kb.step("vector", "dve", 1, lambda e, kt=kt, h=h: e.tensor_copy(
                    cx.xm[kt][:, 1024 * h:1024 * (h + 1)], cx.w32[:, 0:1024]))
            kb.wait_all("sync")
            kb.step("sync", "dmaL", 16, lambda e, kt=kt: e.dma_start(
                cx.w32[:, 0:3], key("XA")[128 * kt:128 * (kt + 1), 0:3]))
            kb.wait("vector", "dmaL")
            kb.step("vector", "dve", 1, lambda e, kt=kt: e.tensor_copy(
                cx.xtl[:, 4 * kt:4 * kt + 3], cx.w32[:, 0:3]))
    for th in range(2):
        for kt in range(8):
            kb.wait("vector", "pe")
            kb.step("vector", "dve", 1, lambda e, kt=kt, th=th: e.tensor_tensor(
                cx.duw[:, 0:1024], cx.xm[kt][:, 1024 * th:1024 * (th + 1)],
                cx.xm[kt][:, 1024 * th:1024 * (th + 1)], ALU.mult))
            kb.wait_all("tensor")
            for c in range(2):
                kb.step("tensor", "pe", 1, lambda e, kt=kt, th=th, c=c: e.matmul(
                    cx.psA[0:1, 512 * c:512 * (c + 1)], cx.onescol[:],
                    cx.xm[kt][:, 1024 * th + 512 * c:1024 * th + 512 * (c + 1)],
                    start=(kt == 0), stop=(kt == 7)))
                kb.step("tensor", "pe", 1, lambda e, kt=kt, c=c: e.matmul(
                    cx.psB[0:1, 512 * c:512 * (c + 1)], cx.onescol[:],
                    cx.duw[:, 512 * c:512 * (c + 1)], start=(kt == 0), stop=(kt == 7)))
            kb.pe_fence()
        kb.wait("vector", "pe")
        kb.step("vector", "dve", 1, lambda e, th=th: e.tensor_scalar(
            cx.ustg[0:1, 1024 * th:1024 * (th + 1)], cx.psA[0:1, 0:1024], 1.0 / 1024, None, ALU.mult))
        kb.step("vector", "dve", 1, lambda e, th=th: e.tensor_scalar(
            cx.dpb[0:1, 1024 * th:1024 * (th + 1)], cx.psB[0:1, 0:1024], 1.0 / 1024, None, ALU.mult))
    kb.step("vector", "dve", 1, lambda e: e.tensor_tensor(
        cx.dwin[0:1, :], cx.ustg[0:1, :], cx.ustg[0:1, :], ALU.mult))
    kb.step("vector", "dve", 1, lambda e: e.tensor_tensor(
        cx.dpb[0:1, :], cx.dpb[0:1, :], cx.dwin[0:1, :], ALU.subtract))
    kb.step("vector", "dve", 1, lambda e: e.tensor_scalar(
        cx.dpb[0:1, :], cx.dpb[0:1, :], 1e-5, None, ALU.add))
    kb.wait_all("scalar")
    kb.step("scalar", "act", 1, lambda e: e.activation(
        cx.dwin[0:1, :], cx.dpb[0:1, :], ACT.Ln, bias=0.0, scale=1.0))
    kb.step("scalar", "act", 1, lambda e: e.activation(
        cx.dpb[0:1, :], cx.dwin[0:1, :], ACT.Exp, bias=0.0, scale=-0.5))
    kb.wait_all("tensor")
    for c in range(4):
        kb.step("tensor", "pe", 1, lambda e, c=c: e.matmul(
            (cx.psA if c < 2 else cx.psB)[:, 512 * (c % 2):512 * (c % 2 + 1)],
            cx.ones1[:], cx.ustg[0:1, 512 * c:512 * (c + 1)], start=True, stop=True))
    kb.pe_fence()
    kb.wait("vector", "pe")
    kb.step("vector", "dve", 1, lambda e: e.tensor_copy(cx.dwin[:, 0:1024], cx.psA[:, :1024]))
    kb.step("vector", "dve", 1, lambda e: e.tensor_copy(cx.dwin[:, 1024:], cx.psB[:, :1024]))
    kb.wait_all("tensor")
    for c in range(4):
        kb.step("tensor", "pe", 1, lambda e, c=c: e.matmul(
            (cx.psA if c < 2 else cx.psB)[:, 512 * (c % 2):512 * (c % 2 + 1)],
            cx.ones1[:], cx.dpb[0:1, 512 * c:512 * (c + 1)], start=True, stop=True))
    kb.pe_fence()
    kb.wait("vector", "pe")
    kb.step("vector", "dve", 1, lambda e: e.tensor_copy(cx.dlw[:, 0:1024], cx.psA[:, :1024]))
    kb.step("vector", "dve", 1, lambda e: e.tensor_copy(cx.dlw[:, 1024:], cx.psB[:, :1024]))
    if phase == 1:
        kb.step("vector", "dve", 1, lambda e: e.tensor_tensor(
            cx.duw[:, 0:32], cx.xtl[:], cx.xtl[:], ALU.mult))
        kb.wait_all("tensor")
        for kt in range(8):
            kb.step("tensor", "pe", 1, lambda e, kt=kt: e.matmul(
                cx.psD[0:1, 0:4], cx.onescol[:], cx.xtl[:, 4 * kt:4 * (kt + 1)],
                start=(kt == 0), stop=(kt == 7)))
        for kt in range(8):
            kb.step("tensor", "pe", 1, lambda e, kt=kt: e.matmul(
                cx.psD[0:1, 4:8], cx.onescol[:], cx.duw[:, 4 * kt:4 * (kt + 1)],
                start=(kt == 0), stop=(kt == 7)))
        kb.pe_fence()
        kb.wait("vector", "pe")
        kb.step("vector", "dve", 1, lambda e: e.tensor_scalar(
            cx.bct[0:1, 0:4], cx.psD[0:1, 0:4], 1.0 / 1024, None, ALU.mult))
        kb.step("vector", "dve", 1, lambda e: e.tensor_scalar(
            cx.bct[0:1, 4:8], cx.psD[0:1, 4:8], 1.0 / 1024, None, ALU.mult))
        kb.step("vector", "dve", 1, lambda e: e.tensor_tensor(
            cx.bct[0:1, 8:12], cx.bct[0:1, 0:4], cx.bct[0:1, 0:4], ALU.mult))
        kb.step("vector", "dve", 1, lambda e: e.tensor_tensor(
            cx.bct[0:1, 4:8], cx.bct[0:1, 4:8], cx.bct[0:1, 8:12], ALU.subtract))
        kb.step("vector", "dve", 1, lambda e: e.tensor_scalar(
            cx.bct[0:1, 4:8], cx.bct[0:1, 4:8], 1e-5, None, ALU.add))
        kb.wait_all("scalar")
        kb.step("scalar", "act", 1, lambda e: e.activation(
            cx.bct[0:1, 8:12], cx.bct[0:1, 4:8], ACT.Ln, bias=0.0, scale=1.0))
        kb.step("scalar", "act", 1, lambda e: e.activation(
            cx.bct[0:1, 4:8], cx.bct[0:1, 8:12], ACT.Exp, bias=0.0, scale=-0.5))
        kb.wait_all("tensor")
        kb.step("tensor", "pe", 1, lambda e: e.matmul(
            cx.psD[:, 8:12], cx.ones1[:], cx.bct[0:1, 0:4], start=True, stop=True))
        kb.step("tensor", "pe", 1, lambda e: e.matmul(
            cx.psD[:, 12:16], cx.ones1[:], cx.bct[0:1, 4:8], start=True, stop=True))
        kb.pe_fence()
        kb.wait("vector", "pe")
        kb.step("vector", "dve", 1, lambda e: e.tensor_copy(cx.statn[:, 0:4], cx.psD[:, 8:12]))
        kb.step("vector", "dve", 1, lambda e: e.tensor_copy(cx.statn[:, 4:8], cx.psD[:, 12:16]))
        for kt in range(8):
            kb.step("vector", "dve", 1, lambda e, kt=kt: e.tensor_tensor(
                cx.xtl[:, 4 * kt:4 * (kt + 1)], cx.xtl[:, 4 * kt:4 * (kt + 1)],
                cx.statn[:, 0:4], ALU.subtract))
            kb.step("vector", "dve", 1, lambda e, kt=kt: e.tensor_tensor(
                cx.xtl[:, 4 * kt:4 * (kt + 1)], cx.xtl[:, 4 * kt:4 * (kt + 1)],
                cx.statn[:, 4:8], ALU.mult))
            kb.step("vector", "dve", 1, lambda e, kt=kt: e.tensor_scalar(
                cx.xtl[:, 4 * kt:4 * (kt + 1)], cx.xtl[:, 4 * kt:4 * (kt + 1)],
                cx.mcols[:, 8 + kt:9 + kt], cx.mcols[:, kt:kt + 1], ALU.mult, ALU.add))
    cfoff = 8 if phase == 1 else 32
    soff = 0 if phase == 1 else 24
    for kt in range(8):
        kb.step("vector", "dve", 1, lambda e, kt=kt: e.tensor_tensor(
            cx.duw[:, :], cx.xm[kt][:, :], cx.dwin[:, :], ALU.subtract))
        kb.step("vector", "dve", 1, lambda e: e.tensor_tensor(
            cx.duw[:, :], cx.duw[:, :], cx.dlw[:, :], ALU.mult))
        if phase == 1:
            kb.step("vector", "dve", 1, lambda e, kt=kt: e.tensor_scalar(
                cx.xm[kt][:, :], cx.duw[:, :],
                cx.mcols[:, cfoff + kt:cfoff + kt + 1], cx.mcols[:, soff + kt:soff + kt + 1],
                ALU.mult, ALU.add))
        else:
            kb.wait("vector", "dmaG")
            kb.step("vector", "dve", 1, lambda e, kt=kt: e.tensor_scalar(
                cx.h2f8[:, 0:2048], cx.duw[:, :],
                cx.mcols[:, cfoff + kt:cfoff + kt + 1], cx.mcols[:, soff + kt:soff + kt + 1],
                ALU.mult, ALU.add))
            kb.wait_all("gpsimd")
            kb.step("gpsimd", "dmaG", 16, lambda e, kt=kt: e.dma_start(
                key("h2_hbm")[kt, :, :], cx.h2f8[:, 0:2048]))
    if phase == 1:
        kb.step("vector", "dve", 1, lambda e: e.tensor_scalar(
            cx.xmt[:], cx.xtl[:], cx.fcols[:, 0:1], None, ALU.mult))


def emit_slot_head(kb, cx, key, s):
    kb.wait_all("gpsimd")
    for kt in range(8):
        kb.step("gpsimd", "dmaG", 16, lambda e, kt=kt: e.dma_start(
            cx.xtl[:, 4 * kt:4 * kt + 3], cx.xm[kt][:, 2045:2048]))
    kb.wait_all("vector")
    kb.step("vector", "dve", 1, lambda e: e.tensor_copy(cx.w32[:, 0:32], cx.xtl[:]))
    kb.step("vector", "dve", 1, lambda e: e.tensor_scalar(
        cx.w32[:, 32:64], cx.w32[:, 0:32], cx.fcols[:, 1:2], None, ALU.mult))
    kb.step("vector", "dve", 1, lambda e: e.tensor_scalar(
        cx.hrun[:], cx.hrun[:], cx.fcols[0:16, 1:2], None, ALU.mult))
    kb.wait_all("gpsimd")
    kb.step("gpsimd", "dmaG", 16, lambda e, s=s: e.dma_start(key(f"arinX{s}")[:], cx.w32[:, 32:64]))
    kb.step("gpsimd", "dmaG", 16, lambda e, s=s: e.dma_start(key(f"arinH{s}")[:], cx.hrun[:]))
    if FUSED:
        kb.wait("gpsimd", "dmaG")
        kb.step("gpsimd", "cc", 1, lambda e, s=s: e.collective_compute(
            "AllReduce", ALU.add, replica_groups=[[0, 1], [2, 3], [4, 5], [6, 7]],
            ins=[key(f"arinH{s}")[:].opt()], outs=[key(f"aroutH{s}")[:].opt()]))
        kb.step("gpsimd", "cc", 1, lambda e, s=s: e.collective_compute(
            "AllReduce", ALU.add, replica_groups=[[0, 1], [2, 3], [4, 5], [6, 7]],
            ins=[key(f"arinX{s}")[:].opt()], outs=[key(f"aroutX{s}")[:].opt()]))
    if s > 0:
        if FUSED:
            kb.wait("gpsimd", "cc", 2 * s)
        kb.step("gpsimd", "dmaG", 16, lambda e, s=s: e.dma_start(
            cx.w32[:, 64:96], key(f"aroutX{s - 1}")[:]))
        kb.wait_all("vector")
        kb.step("vector", "dve", 1, lambda e: e.tensor_scalar(
            cx.w32[:, 64:96], cx.w32[:, 64:96], cx.fcols[:, 0:1], None, ALU.mult))
        kb.step("vector", "dve", 1, lambda e: e.tensor_tensor(
            cx.w32[:, 32:64], cx.w32[:, 32:64], cx.w32[:, 64:96], ALU.add))
        kb.step("vector", "dve", 1, lambda e: e.tensor_copy(cx.xmt[:], cx.w32[:, 32:64]))


def emit_ar_consume(kb, cx, key, s):
    if FUSED:
        kb.wait("gpsimd", "cc", 2 * s + 2)
    kb.wait_all("gpsimd")
    kb.step("gpsimd", "dmaG", 16, lambda e, s=s: e.dma_start(cx.hrun[:], key(f"aroutH{s}")[:]))
    kb.wait_all("vector")
    kb.step("vector", "dve", 1, lambda e: e.tensor_scalar(
        cx.hrun[:], cx.hrun[:], cx.fcols[0:16, 0:1], None, ALU.mult))


def emit_layer(kb, cx, key, pre_p3=None):
    nc = kb.nc

    kb.wait("gpsimd", "dmaL")
    kb.wait("vector", "pe")
    kb.wait("sync", "dve")
    kb.step("sync", "dmaL", 16, lambda e: e.dma_start(cx.convw[:], key("cw")[:]))
    kb.step("sync", "dmaL", 16, lambda e: e.dma_start(cx.convb[:], key("cb")[:]))
    for h in range(2):
        kb.wait("sync", "dve")
        kb.step("sync", "dmaL", 16, lambda e, h=h: e.dma_start(cx.w32[0:16, :], key("alogT")[:, 1024 * h:1024 * (h + 1)]))
        kb.wait("scalar", "dmaL")
        kb.step("scalar", "act", 1, lambda e: e.activation(cx.w32[0:16, :], cx.w32[0:16, :], ACT.Exp))
        kb.wait("vector", "act")
        kb.step("vector", "dve", 1, lambda e, h=h: e.tensor_copy(cx.alpha16[0:16, 1024 * h:1024 * (h + 1)], cx.w32[0:16, :]))
    for h in range(2):
        kb.wait("sync", "dve")
        kb.step("sync", "dmaL", 16, lambda e, h=h: e.dma_start(cx.w32[0:1, :], key("dpar")[:, 1024 * h:1024 * (h + 1)]))
        kb.wait("vector", "dmaL")
        kb.step("vector", "dve", 1, lambda e, h=h: e.tensor_copy(cx.ustg[0:1, 0:1024], cx.w32[0:1, :]))
        kb.wait("gpsimd", "dve")
        kb.step("gpsimd", "dmaG", 16, lambda e, h=h: e.dma_start(cx.alpha16[16:17, 1024 * h:1024 * (h + 1)], cx.ustg[0:1, 0:1024]))
        kb.wait("vector", "dmaG")
        kb.wait("sync", "dve")
        kb.step("sync", "dmaL", 16, lambda e, h=h: e.dma_start(cx.w32[0:128, :], key("wdtTb")[:, 1024 * h:1024 * (h + 1)]))
        kb.wait("vector", "dmaL")
        kb.step("vector", "dve", 1, lambda e, h=h: e.tensor_copy(cx.wdtb[:, 1024 * h:1024 * (h + 1)], cx.w32[0:128, :]))
    for h in range(2):
        w0, w1 = 1024 * h, min(16 * 96, 1024 * (h + 1))
        kb.wait("sync", "dve")
        kb.step("sync", "dmaL", 16, lambda e, w0=w0, w1=w1: e.dma_start(cx.w32[:, 0:w1 - w0], key("wxT")[:, w0:w1]))
        kb.wait("vector", "dmaL")
        kb.step("vector", "dve", 1, lambda e, w0=w0, w1=w1: e.tensor_copy(cx.wxtb[:, w0:w1], cx.w32[:, 0:w1 - w0]))
    for n in range(NST + 1):
        kb.wait("gpsimd", "pe")
        kb.wait("gpsimd", "dve")
        kb.step("gpsimd", "dmaG", 16, lambda e, n=n: e.dma_start(cx.ustg[0:1, 0:CC], cx.alpha16[n:n + 1, :]))
        kb.wait("tensor", "dmaG")
        for h in range(4):
            kb.step("tensor", "pe", 1, lambda e, h=h: e.matmul(
                (cx.psA if h < 2 else cx.psB)[:, 512 * (h % 2):512 * (h % 2 + 1)], cx.ones1[:],
                cx.ustg[0:1, 512 * h:512 * (h + 1)], start=True, stop=True))
        kb.pe_fence()
        kb.wait("vector", "pe")
        if n < NST:
            kb.step("vector", "dve", 1, lambda e, n=n: e.tensor_copy(cx.abc[n][:, :1024], cx.psA[:, :1024]))
            kb.step("vector", "dve", 1, lambda e, n=n: e.tensor_copy(cx.abc[n][:, 1024:], cx.psB[:, :1024]))
        else:
            kb.step("vector", "dve", 1, lambda e: e.tensor_copy(cx.dpb[:, :1024], cx.psA[:, :1024]))
            kb.step("vector", "dve", 1, lambda e: e.tensor_copy(cx.dpb[:, 1024:], cx.psB[:, :1024]))
    kb.wait("tensor", "dve")
    for h in range(4):
        kb.step("tensor", "pe", 1, lambda e, h=h: e.matmul(
            (cx.psA if h < 2 else cx.psB)[:, 512 * (h % 2):512 * (h % 2 + 1)], cx.rep8[:],
            cx.alpha16[0:16, 512 * h:512 * (h + 1)], start=True, stop=True))
    kb.pe_fence()
    kb.wait("vector", "pe")
    kb.step("vector", "dve", 1, lambda e: e.tensor_copy(cx.alrep[:, 0:1024], cx.psA[:, :1024]))
    kb.step("vector", "dve", 1, lambda e: e.tensor_copy(cx.alrep[:, 1024:], cx.psB[:, :1024]))
    for k in range(16):
        kb.wait("sync", "dve")
        kb.step("sync", "dmaL", 16, lambda e, k=k: e.dma_start(cx.w32[:, 0:1024], key("woT")[128 * k:128 * (k + 1), :]))
        kb.wait("vector", "dmaL")
        kb.step("vector", "dve", 1, lambda e: e.tensor_copy(cx.wos[0][:], cx.w32[:, 0:1024]))
        kb.wait("gpsimd", "dve")
        kb.step("gpsimd", "dmaG", 16, lambda e, k=k: e.dma_start(key("wo_hbm")[128 * k:128 * (k + 1), :], cx.wos[0][:]))
        kb.wait("vector", "dmaG")

    for mg in range(16):
        kb.wait("vector", "pe")
        for k in range(8):
            kb.wait("sync", "dve")
            kb.step("sync", "dmaL", 16, lambda e, mg=mg, k=k: e.dma_start(
                cx.w32[:, 0:256], key("winT")[128 * k:128 * (k + 1), 256 * mg:256 * (mg + 1)]))
            kb.wait("vector", "dmaL")
            kb.step("vector", "dve", 1, lambda e, k=k: e.tensor_copy(cx.wkp[k][:], cx.w32[:, 0:256]))
        for mi in range(2):
            m = mg * 2 + mi
            kb.wait("tensor", "dve")
            kb.wait("tensor", "act")
            for th in range(2):
                tgt = cx.psA if th == 0 else cx.psB
                for nn in range(2):
                    for k in range(8):
                        kb.step("tensor", "pe", 1, lambda e, mi=mi, th=th, nn=nn, k=k, tgt=tgt: e.matmul(
                            tgt[:, N_MM * nn:N_MM * (nn + 1)],
                            cx.wkp[k][:, 128 * mi:128 * (mi + 1)],
                            cx.xm[k][:, 1024 * th + N_MM * nn:1024 * th + N_MM * (nn + 1)],
                            start=(k == 0), stop=(k == 7)))
            if m < 16:
                ct = m
                for k in range(8):
                    kb.step("tensor", "pe", 1, lambda e, mi=mi, k=k: e.matmul(
                        cx.psD[:, 0:4], cx.wkp[k][:, 128 * mi:128 * (mi + 1)],
                        cx.xmt[:, 4 * k:4 * (k + 1)], start=(k == 0), stop=(k == 7)))
                kb.pe_fence()
                kb.wait("vector", "pe")
                kb.wait("vector", "dmaG")
                kb.step("vector", "dve", 1, lambda e: e.tensor_copy(cx.dwin[:, 0:1024], cx.psA[:, :1024]))
                kb.step("vector", "dve", 1, lambda e: e.tensor_copy(cx.dwin[:, 1024:], cx.psB[:, :1024]))
                kb.step("vector", "dve", 1, lambda e: e.tensor_copy(cx.xhead[:, 0:3], cx.psD[:, 0:3]))
                kb.step("vector", "dve", 1, lambda e, ct=ct: e.tensor_scalar(
                    cx.ustg[:, 3:Lc], cx.dwin[:, 0:Lc - 3], cx.convw[:, 4 * ct:4 * ct + 1], None, ALU.mult))
                kb.step("vector", "dve", 1, lambda e, ct=ct: e.tensor_scalar(
                    cx.ustg[:, 0:3], cx.xhead[:, 0:3], cx.convw[:, 4 * ct:4 * ct + 1], None, ALU.mult))
                for tap in range(1, 4):
                    kb.step("vector", "dve", 1, lambda e, ct=ct, tap=tap: e.scalar_tensor_tensor(
                        cx.ustg[:, 3 - tap:Lc], cx.dwin[:, 0:Lc - 3 + tap],
                        cx.convw[:, 4 * ct + tap:4 * ct + tap + 1],
                        cx.ustg[:, 3 - tap:Lc], ALU.mult, ALU.add))
                    if tap < 3:
                        kb.step("vector", "dve", 1, lambda e, ct=ct, tap=tap: e.scalar_tensor_tensor(
                            cx.ustg[:, 0:3 - tap], cx.xhead[:, tap:3],
                            cx.convw[:, 4 * ct + tap:4 * ct + tap + 1],
                            cx.ustg[:, 0:3 - tap], ALU.mult, ALU.add))
                kb.wait("scalar", "dve")
                kb.step("scalar", "act", 1, lambda e, ct=ct: e.activation(
                    cx.dwin[:, 0:Lc], cx.ustg[:, 0:Lc], ACT.Silu, bias=cx.convb[:, ct:ct + 1], scale=1.0))
                kb.wait("gpsimd", "act")
                kb.wait("gpsimd", "dmaL")
                kb.step("gpsimd", "dmaG", 16, lambda e, ct=ct: e.dma_start(
                    key("u_hbm")[ct, :, 0:Lc], cx.dwin[:, 0:Lc]))
                kb.step("gpsimd", "dmaG", 16, lambda e, ct=ct: e.dma_start(
                    key("u_hbm")[ct, :, Lc:TP], cx.zpad[:]))
            else:
                ct = m - 16
                kb.pe_fence()
                kb.wait("scalar", "pe")
                kb.wait("scalar", "dmaG")
                kb.step("scalar", "act", 1, lambda e: e.activation(
                    cx.ustg[:, 0:1024], cx.psA[:, :1024], ACT.Silu))
                kb.step("scalar", "act", 1, lambda e: e.activation(
                    cx.ustg[:, 1024:2048], cx.psB[:, :1024], ACT.Silu))
                kb.wait("gpsimd", "act")
                kb.wait("gpsimd", "dmaL")
                kb.step("gpsimd", "dmaG", 16, lambda e, ct=ct: e.dma_start(key("z_hbm")[ct, :, 0:Lc], cx.ustg[:]))
                kb.step("gpsimd", "dmaG", 16, lambda e, ct=ct: e.dma_start(key("z_hbm")[ct, :, Lc:TP], cx.zpad[:]))

    kb.wait("vector", "pe")
    kb.step("vector", "dve", 1, lambda e: e.memset(cx.xdblb[:], 0.0))
    kb.wait("gpsimd", "dve")
    kb.step("gpsimd", "dmaG", 16, lambda e: e.dma_start(cx.xdblb[96:97, :], key("onesrow")[:]))
    kb.wait("vector", "dmaG")
    for ct in range(16):
        kb.wait("sync", "pe")
        kb.wait("sync", "dve")
        kb.wait("sync", "dmaG")
        kb.step("sync", "dmaL", 16, lambda e, ct=ct: e.dma_start(cx.ustg[:, 0:Lc], key("u_hbm")[ct, :, 0:Lc]))
        kb.wait("tensor", "dmaL")
        for nn in range(4):
            tgt = cx.psA if nn < 2 else cx.psB
            kb.step("tensor", "pe", 1, lambda e, ct=ct, nn=nn, tgt=tgt: e.matmul(
                tgt[0:96, N_MM * (nn % 2):N_MM * (nn % 2 + 1)],
                cx.wxtb[:, 96 * ct:96 * (ct + 1)],
                cx.ustg[:, N_MM * nn:N_MM * (nn + 1)],
                start=(ct == 0), stop=(ct == 15)))
    kb.pe_fence()
    kb.wait("vector", "pe")
    kb.step("vector", "dve", 1, lambda e: e.tensor_copy(cx.xdblb[0:96, 0:1024], cx.psA[0:96, :]))
    kb.step("vector", "dve", 1, lambda e: e.tensor_copy(cx.xdblb[0:96, 1024:2048], cx.psB[0:96, :]))

    for g, wr in enumerate(GRPW):
        for w in wr:
            wl = w - wr.start
            off = w * WT
            kb.wait("tensor", "dve")
            kb.wait("tensor", "dmaG")
            for ch in range(2):
                tgt = cx.psA if ch == 0 else cx.psB
                for nn in range(2):
                    kb.step("tensor", "pe", 1, lambda e, off=off, ch=ch, nn=nn, tgt=tgt: e.matmul(
                        tgt[:WT, N_MM * nn:N_MM * (nn + 1)],
                        cx.xdblb[0:64, off:off + WT],
                        cx.wdtb[0:64, 1024 * ch + N_MM * nn:1024 * ch + N_MM * (nn + 1)],
                        start=True, stop=False))
                    kb.tensor_drain()
                    kb.step("tensor", "pe", 1, lambda e, off=off, ch=ch, nn=nn, tgt=tgt: e.matmul(
                        tgt[:WT, N_MM * nn:N_MM * (nn + 1)],
                        cx.xdblb[64:128, off:off + WT],
                        cx.wdtb[64:128, 1024 * ch + N_MM * nn:1024 * ch + N_MM * (nn + 1)],
                        start=False, stop=True))
            kb.pe_fence()
            kb.wait("vector", "pe")
            for q in range(4):
                src = cx.psA if q < 2 else cx.psB
                so = 512 * (q % 2)
                kb.step("vector", "dve", 1, lambda e, src=src, so=so: e.tensor_scalar(
                    cx.w32[:WT, 0:512], src[:WT, so:so + 512], -30.0, 30.0, ALU.max, ALU.min))
                kb.wait("scalar", "dve")
                kb.step("scalar", "act", 1, lambda e: e.activation(
                    cx.w32[:WT, 512:1024], cx.w32[:WT, 0:512], ACT.Exp))
                kb.step("scalar", "act", 1, lambda e, q=q: e.activation(
                    cx.dwin[:WT, 512 * q:512 * (q + 1)], cx.w32[:WT, 512:1024],
                    ACT.Ln, bias=1.0, scale=1.0))
                kb.wait("vector", "act")
            if w == NW - 1:
                kb.step("vector", "dve", 1, lambda e: e.tensor_scalar(
                    cx.dwin[:WT, :], cx.dwin[:WT, :], cx.maskw16[:WT, :], None, ALU.mult))
            kb.wait("gpsimd", "dve")
            kb.wait("gpsimd", "dmaL")
            kb.step("gpsimd", "dmaG", 16, lambda e, w=w: e.dma_start(key("d_hbm")[w, :, :], cx.dwin[:]))
            kb.wait("tensor", "dve")
            for ch in range(2):
                sel = cx.hselA if ch == 0 else cx.hselB
                for nn in range(2):
                    kb.step("tensor", "pe", 1, lambda e, sel=sel, wl=wl, ch=ch, nn=nn, w=w, wr=wr, g=g: e.matmul(
                        cx.psC[32 * g:32 * (g + 1), N_MM * nn:N_MM * (nn + 1)],
                        sel[:, 32 * wl:32 * (wl + 1)],
                        cx.dwin[:, 1024 * ch + N_MM * nn:1024 * ch + N_MM * (nn + 1)],
                        start=(wl == 0 and ch == 0), stop=(w == wr.stop - 1 and ch == 1)))
        kb.pe_fence()
        kb.wait("vector", "pe")
        kb.step("vector", "dve", 1, lambda e, g=g: e.tensor_copy(cx.dmde[g], cx.psC[32 * g:32 * (g + 1), :]))
        kb.wait("tensor", "dve")

    if pre_p3 is not None:
        pre_p3()

    wog_idx = 0
    for g, wr in enumerate(GRPW):
        kb.wait("tensor", "dve")
        for which in range(2):
            i0 = cx.indA0 if which == 0 else cx.indB0
            i1 = cx.indA1 if which == 0 else cx.indB1
            gdst = cx.gA if which == 0 else cx.gB
            for ch in range(2):
                ind = i0 if ch == 0 else i1
                kb.wait("tensor", "dve")
                for nn in range(2):
                    kb.step("tensor", "pe", 1, lambda e, ind=ind, nn=nn, g=g: e.matmul(
                        cx.psA[:, 512 * nn:512 * (nn + 1)], ind[32 * g:32 * (g + 1), :],
                        cx.dmde[g][:, 512 * nn:512 * (nn + 1)], start=True, stop=True))
                kb.pe_fence()
                kb.wait("vector", "pe")
                kb.step("vector", "dve", 1, lambda e, ch=ch: e.tensor_tensor(
                    cx.w32[:, :], cx.alrep[:, 1024 * ch:1024 * (ch + 1)],
                    cx.psA[:, :1024], ALU.mult))
                kb.wait("scalar", "dve")
                kb.step("scalar", "act", 1, lambda e, gdst=gdst, ch=ch: e.activation(
                    gdst[:, 1024 * ch:1024 * (ch + 1)], cx.w32[:], ACT.Exp, bias=0.0, scale=-1.0))
                kb.wait("vector", "act")
        kb.wait("vector", "act")

        for w in wr:
            wl = w - wr.start
            off = w * WT
            wo_g = WOG[wog_idx]
            wo_l = w - wo_g.start
            kb.wait("sync", "dve")
            kb.wait("sync", "act")
            kb.wait("sync", "dmaG")
            kb.step("sync", "dmaL", 16, lambda e, w=w: e.dma_start(cx.dwin[:], key("d_hbm")[w, :, :]))
            kb.wait("sync", "pe")
            for cb in range(16):
                kb.step("sync", "dmaL", 16, lambda e, cb=cb, off=off: e.dma_start(
                    cx.ustg[:, WT * cb:WT * (cb + 1)], key("u_hbm")[cb, :, off:off + WT]))
            uld = kb.cnt["dmaL"]
            kb.wait("gpsimd", "dve")
            kb.wait("gpsimd", "act")
            kb.wait("gpsimd", "dmaG", None)
            for cb in range(16):
                kb.step("gpsimd", "dmaG", 16, lambda e, cb=cb, off=off: e.dma_start(
                    cx.zsil[:, WT * cb:WT * (cb + 1)], key("z_hbm")[cb, :, off:off + WT]))
            kb.wait("tensor", "dmaL", uld)
            kb.wait("tensor", "dve")
            for rnd in range(2):
                for cb8 in range(8):
                    cb = rnd * 8 + cb8
                    kb.step("tensor", "pe", 1, lambda e, cb=cb, cb8=cb8: e.transpose(
                        cx.psT[0:128, 128 * cb8:128 * (cb8 + 1)],
                        cx.ustg[:, WT * cb:WT * cb + 128], cx.ident[:]))
                kb.pe_fence()
                kb.wait("vector", "pe")
                kb.step("vector", "dve", 1, lambda e, rnd=rnd: e.tensor_copy(
                    cx.duw[:WT, 1024 * rnd:1024 * (rnd + 1)], cx.psT[:WT, :]))
                if rnd == 0:
                    kb.wait("tensor", "dve")
            kb.step("vector", "dve", 1, lambda e: e.tensor_tensor(cx.yw[:WT, :], cx.duw[:WT, :], cx.dpb[:WT, :], ALU.mult))
            kb.wait("tensor", "dve")
            for ch in range(2):
                tgt = cx.psA if ch == 0 else cx.psB
                for nn in range(2):
                    kb.step("tensor", "pe", 1, lambda e, ch=ch, nn=nn, tgt=tgt: e.matmul(
                        tgt[:, N_MM * nn:N_MM * (nn + 1)], cx.trilmid[:],
                        cx.dwin[:, 1024 * ch + N_MM * nn:1024 * ch + N_MM * (nn + 1)],
                        start=True, stop=True))
            kb.pe_fence()
            kb.wait("vector", "pe")
            kb.step("vector", "dve", 1, lambda e: e.tensor_copy(cx.dlw[:, 0:1024], cx.psA[:, :1024]))
            kb.step("vector", "dve", 1, lambda e: e.tensor_copy(cx.dlw[:, 1024:], cx.psB[:, :1024]))
            kb.step("vector", "dve", 1, lambda e: e.tensor_tensor(cx.duw[:WT, :], cx.dwin[:WT, :], cx.duw[:WT, :], ALU.mult))
            kb.wait("tensor", "dve")
            kb.step("tensor", "pe", 1, lambda e, off=off: e.transpose(
                cx.psT[0:128, 0:32], cx.xdblb[64:96, off:off + 128], cx.ident[64:96, 64:96]))
            kb.pe_fence()
            kb.wait("vector", "pe")
            kb.step("vector", "dve", 1, lambda e: e.tensor_copy(cx.bct[:WT, :], cx.psT[:WT, 0:32]))
            kb.step("vector", "dve", 1, lambda e: e.tensor_copy(cx.bctf[:], cx.bct[:]))
            kb.wait("gpsimd", "act")
            kb.wait("gpsimd", "dve")
            kb.wait("gpsimd", "dmaG", None)
            kb.step("gpsimd", "dmaG", 16, lambda e, wl=wl: e.dma_start(cx.gsl[:], cx.gA[16 * wl:16 * (wl + 1), :]))
            kb.wait("vector", "dmaG")
            kb.step("vector", "dve", 1, lambda e: e.tensor_tensor(
                cx.gsl[:], cx.hrun[:], cx.gsl[:], ALU.mult))

            kb.wait("gpsimd", "dve")
            kb.wait("gpsimd", "dmaG", None)
            kb.step("gpsimd", "dmaG", 16, lambda e: e.dma_start(
                cx.ww[127:128, :], cx.gsl[0:1, :]))
            wg = kb.cnt["dmaG"]
            for n in range(NST):
                kb.wait("vector", "pe")
                kb.wait("vector", "act")
                kb.step("vector", "dve", 1, lambda e, n=n: e.tensor_tensor(
                    cx.dwin[:], cx.dlw[:], cx.abc[n][:], ALU.mult))
                kb.step("vector", "dve", 1, lambda e, n=n: e.tensor_scalar(
                    cx.statn[:], cx.trilsum[:], cx.bctf[:, n:n + 1], None, ALU.mult))
                kb.wait("scalar", "dve")
                kb.step("scalar", "act", 1, lambda e: e.activation(cx.rg[:], cx.dwin[:], ACT.Exp))
                kb.wait("vector", "act")
                kb.step("vector", "dve", 1, lambda e: e.tensor_tensor(
                    cx.ww[:WT, :], cx.rg[:WT, :], cx.duw[:WT, :], ALU.mult))
                kb.wait("scalar", "dve")
                kb.step("scalar", "act", 1, lambda e: e.activation(cx.rg[:], cx.dwin[:], ACT.Exp, bias=0.0, scale=-1.0))
                wd = kb.cnt["dve"]
                kb.wait("tensor", "dve", wd)
                kb.wait("tensor", "dmaG", wg)
                for ch in range(2):
                    tgt = cx.psA if ch == 0 else cx.psB
                    for nn in range(2):
                        kb.step("tensor", "pe", 1, lambda e, ch=ch, nn=nn, tgt=tgt: e.matmul(
                            tgt[:, N_MM * nn:N_MM * (nn + 1)], cx.statn[:],
                            cx.ww[:, 1024 * ch + N_MM * nn:1024 * ch + N_MM * (nn + 1)],
                            start=True, stop=True))
                mmd = kb.pe_fence()
                kb.wait("vector", "pe", mmd)
                kb.wait("vector", "act")
                kb.wait("vector", "dmaG")
                kb.step("vector", "dve", 1, lambda e: e.tensor_tensor(
                    cx.ww[:, 0:1024], cx.rg[:, 0:1024], cx.psA[:, :], ALU.mult))
                kb.step("vector", "dve", 1, lambda e: e.tensor_tensor(
                    cx.ww[:, 1024:], cx.rg[:, 1024:], cx.psB[:, :], ALU.mult))
                kb.wait("gpsimd", "dve")
                kb.step("gpsimd", "dmaG", 16, lambda e, n=n: e.dma_start(cx.hraw[n:n + 1, :], cx.ww[127:128, :]))
                if n < NST - 1:
                    kb.wait("gpsimd", "pe", mmd)
                    kb.wait("gpsimd", "dmaG", None)
                    kb.step("gpsimd", "dmaG", 16, lambda e, n=n: e.dma_start(
                        cx.ww[127:128, :], cx.gsl[n + 1:n + 2, :]))
                    wg = kb.cnt["dmaG"]
                kb.step("vector", "dve", 1, lambda e, n=n: e.scalar_tensor_tensor(
                    cx.yw[:WT, :], cx.ww[:WT, :], cx.bctf[:WT, NST + n:NST + n + 1], cx.yw[:WT, :],
                    ALU.mult, ALU.add))
            kb.wait("gpsimd", "dve")
            kb.wait("gpsimd", "dmaG", None)
            kb.step("gpsimd", "dmaG", 16, lambda e, wl=wl: e.dma_start(cx.gsl[:], cx.gB[16 * wl:16 * (wl + 1), :]))
            kb.wait("vector", "dmaG")
            kb.step("vector", "dve", 1, lambda e: e.tensor_tensor(
                cx.hrun[:], cx.hraw[:], cx.gsl[:], ALU.mult))
            kb.wait("tensor", "dve")
            for rnd in range(2):
                for cb8 in range(8):
                    cb = rnd * 8 + cb8
                    kb.step("tensor", "pe", 1, lambda e, cb=cb, cb8=cb8: e.transpose(
                        cx.psT[0:128, 128 * cb8:128 * (cb8 + 1)],
                        cx.yw[:, 128 * cb:128 * (cb + 1)], cx.ident[:]))
                kb.pe_fence()
                kb.wait("vector", "pe")
                for cb8 in range(8):
                    cb = rnd * 8 + cb8
                    kb.step("vector", "dve", 1, lambda e, cb=cb, cb8=cb8, wo_l=wo_l: e.tensor_tensor(
                        cx.wrhs[cb][:, WT * wo_l:WT * (wo_l + 1)],
                        cx.psT[:, 128 * cb8:128 * cb8 + WT],
                        cx.zsil[:, WT * cb:WT * (cb + 1)], ALU.mult))
                if rnd == 0:
                    kb.wait("tensor", "dve")
            if w == WOG[wog_idx].stop - 1:
                emit_wout(kb, cx, key, WOG[wog_idx])
                wog_idx += 1


def emit_wout(kb, cx, key, wog):
    vw = len(wog) * WT if wog.stop <= 16 else LASTV
    goff = wog.start * WT
    kb.wait("tensor", "dve")
    kb.wait("sync", "dmaG")
    slots = [(cx.psA, 0), (cx.psA, 512), (cx.psB, 0), (cx.psB, 512),
             (cx.psC, 0), (cx.psC, 512), (cx.psD, 0)]
    n0 = 0
    n1 = vw
    for mpass, ms in enumerate((range(0, 7), range(7, 8))):
        fcnt = {}
        for cb in range(16):
            sl = cx.wos[cb % 2]
            if cb >= 2:
                kb.wait("sync", "pe", fcnt[cb - 2])
            else:
                kb.wait("sync", "pe")
            kb.wait("sync", "dve")
            kb.step("sync", "dmaL", 16, lambda e, cb=cb, sl=sl: e.dma_start(
                sl[:], key("wo_hbm")[128 * cb:128 * (cb + 1), :]))
            kb.wait("tensor", "dmaL")
            for m in ms:
                ps, po = slots[m - ms.start]
                kb.step("tensor", "pe", 1, lambda e, m=m, cb=cb, n0=n0, n1=n1, ps=ps, po=po, sl=sl: e.matmul(
                    ps[:, po:po + (n1 - n0)],
                    sl[:, 128 * m:128 * (m + 1)],
                    cx.wrhs[cb][:, n0:n1], start=(cb == 0), stop=(cb == 15)))
            fcnt[cb] = kb.pe_fence()
        kb.wait("vector", "pe")
        for m in ms:
            ps, po = slots[m - ms.start]
            kb.step("vector", "dve", 1, lambda e, m=m, goff=goff, n0=n0, n1=n1, ps=ps, po=po: e.tensor_tensor(
                cx.xm[m][:, goff + n0:goff + n1], cx.xm[m][:, goff + n0:goff + n1],
                ps[:, po:po + (n1 - n0)], ALU.add))
        kb.wait("tensor", "dve")


def emit_x2(kb, cx, key):
    for kt in range(8):
        for h in range(2):
            kb.wait_all("sync")
            kb.step("sync", "dmaL", 16, lambda e, kt=kt, h=h: e.dma_start(
                cx.w32[:, 0:1024], key("XA")[128 * kt:128 * (kt + 1), 3 + 1024 * h:3 + 1024 * (h + 1)]))
            kb.wait("vector", "dmaL")
            kb.step("vector", "dve", 1, lambda e, kt=kt, h=h: e.scalar_tensor_tensor(
                cx.w32[:, 0:1024], cx.xm[kt][:, 1024 * h:1024 * (h + 1)],
                cx.mcols[:, 16 + kt:17 + kt], cx.w32[:, 0:1024], ALU.mult, ALU.add))
            kb.step("vector", "dve", 1, lambda e, kt=kt, h=h: e.tensor_copy(
                cx.xm[kt][:, 1024 * h:1024 * (h + 1)], cx.w32[:, 0:1024]))
            kb.wait_all("gpsimd")
            kb.step("gpsimd", "dmaG", 16, lambda e, kt=kt, h=h: e.dma_start(
                key("x2_hbm")[kt, :, 1024 * h:1024 * (h + 1)], cx.w32[:, 0:1024]))


def emit_ffn_wcast(kb, cx, key):
    kb.wait_all("sync")
    kb.step("sync", "dmaL", 16, lambda e: e.dma_start(cx.w32[:, 0:32], key("b1T")[:]))
    kb.step("sync", "dmaL", 16, lambda e: e.dma_start(cx.w32[:, 32:40], key("b2T")[:]))
    kb.wait("vector", "dmaL")
    kb.step("vector", "dve", 1, lambda e: e.tensor_copy(cx.w1p[:, 0:40], cx.w32[:, 0:40]))
    kb.step("vector", "dve", 1, lambda e: e.tensor_copy(cx.bq[:, 0:40], cx.w1p[:, 0:40]))
    for kt in range(8):
        for c in range(4):
            kb.wait_all("sync")
            kb.step("sync", "dmaL", 16, lambda e, kt=kt, c=c: e.dma_start(
                cx.w32[:, 0:1024], key("w1T")[128 * kt:128 * (kt + 1), 1024 * c:1024 * (c + 1)]))
            kb.wait("vector", "dmaL")
            kb.wait("vector", "dmaG")
            kb.step("vector", "dve", 1, lambda e: e.tensor_copy(cx.w1p[:, 0:1024], cx.w32[:, 0:1024]))
            kb.wait_all("gpsimd")
            for mj in range(8):
                kb.step("gpsimd", "dmaG", 16, lambda e, kt=kt, c=c, mj=mj: e.dma_start(
                    key("w1q_hbm")[8 * c + mj, :, 128 * kt:128 * (kt + 1)],
                    cx.w1p[:, 128 * mj:128 * (mj + 1)]))
    for m in range(32):
        kb.wait_all("sync")
        kb.step("sync", "dmaL", 16, lambda e, m=m: e.dma_start(
            cx.w32[:, 0:1024], key("w2T")[128 * m:128 * (m + 1), :]))
        kb.wait("vector", "dmaL")
        kb.wait("vector", "dmaG")
        kb.step("vector", "dve", 1, lambda e: e.tensor_copy(cx.w1p[:, 0:1024], cx.w32[:, 0:1024]))
        kb.wait_all("gpsimd")
        kb.step("gpsimd", "dmaG", 16, lambda e, m=m: e.dma_start(key("w2q_hbm")[m, :, :], cx.w1p[:, 0:1024]))


def emit_ffn(kb, cx, key):
    slots = [(cx.psA, 0), (cx.psA, 512), (cx.psB, 0), (cx.psB, 512),
             (cx.psC, 0), (cx.psC, 512), (cx.psD, 0)]
    for tch in range(4):
        toff = 512 * tch
        kb.wait_all("sync")
        for kt in range(8):
            kb.step("sync", "dmaL", 16, lambda e, kt=kt, toff=toff: e.dma_start(
                cx.h2f8[:, 512 * kt:512 * (kt + 1)], key("h2_hbm")[kt, :, toff:toff + 512]))
        for m in range(32):
            kb.wait_all("sync")
            kb.step("sync", "dmaL", 16, lambda e, m=m: e.dma_start(
                cx.w1p[:, 0:1024], key("w1q_hbm")[m, :, :]))
            kb.wait("tensor", "dmaL")
            kb.wait("tensor", "dve")
            for kt in range(8):
                kb.step("tensor", "pe", 1, lambda e, m=m, kt=kt: e.matmul(
                    cx.psD[:, 0:512], cx.w1p[:, 128 * kt:128 * (kt + 1)],
                    cx.h2f8[:, 512 * kt:512 * (kt + 1)], start=(kt == 0), stop=(kt == 7)))
            kb.pe_fence()
            kb.wait("vector", "pe")
            kb.step("vector", "dve", 1, lambda e, m=m: e.tensor_scalar(
                cx.dwin[:, 0:512], cx.psD[:, 0:512], cx.bq[:, m:m + 1], 0.0, ALU.add, ALU.max))
            kb.wait("vector", "dmaG")
            kb.step("vector", "dve", 1, lambda e: e.tensor_copy(cx.h1q8[:], cx.dwin[:, 0:512]))
            kb.wait_all("gpsimd")
            kb.step("gpsimd", "dmaG", 16, lambda e, m=m: e.dma_start(key("h1_hbm")[m, :, :], cx.h1q8[:]))
        for dmg in (range(0, 7), range(7, 8)):
            for m in range(32):
                kb.pe_fence()
                kb.wait_all("sync")
                kb.step("sync", "dmaL", 16, lambda e, m=m: e.dma_start(
                    cx.w1p[:, 0:512], key("h1_hbm")[m, :, :]))
                kb.step("sync", "dmaL", 16, lambda e, m=m, dmg=dmg: e.dma_start(
                    cx.w2s[:, 0:128 * len(dmg)],
                    key("w2q_hbm")[m, :, 128 * dmg.start:128 * dmg.stop]))
                kb.wait("tensor", "dmaL")
                kb.wait("tensor", "dve")
                for dm in dmg:
                    ps, po = slots[dm - dmg.start]
                    kb.step("tensor", "pe", 1, lambda e, dm=dm, m=m, ps=ps, po=po, dmg=dmg: e.matmul(
                        ps[:, po:po + 512], cx.w2s[:, 128 * (dm - dmg.start):128 * (dm - dmg.start + 1)],
                        cx.w1p[:, 0:512], start=(m == 0), stop=(m == 31)))
            kb.pe_fence()
            kb.wait("vector", "pe")
            for dm in dmg:
                ps, po = slots[dm - dmg.start]
                kb.wait_all("sync")
                kb.step("sync", "dmaL", 16, lambda e, dm=dm, toff=toff: e.dma_start(
                    cx.w32[:, 512:1024], key("x2_hbm")[dm, :, toff:toff + 512]))
                kb.wait_all("gpsimd")
                kb.step("gpsimd", "dmaG", 16, lambda e, dm=dm, toff=toff: e.dma_start(
                    cx.dlw[:, 0:512], key("maskT")[dm, :, toff:toff + 512]))
                kb.wait("vector", "dmaL")
                kb.wait("vector", "dmaG")
                kb.step("vector", "dve", 1, lambda e, dm=dm, ps=ps, po=po: e.tensor_scalar(
                    cx.dwin[:, 0:512], ps[:, po:po + 512], cx.bq[:, 32 + dm:33 + dm], None, ALU.add))
                kb.step("vector", "dve", 1, lambda e: e.tensor_tensor(
                    cx.w32[:, 0:512], cx.dwin[:, 0:512], cx.dlw[:, 0:512], ALU.mult))
                kb.step("vector", "dve", 1, lambda e, dm=dm: e.scalar_tensor_tensor(
                    cx.w32[:, 512:1024], cx.w32[:, 0:512], cx.mcols[:, 40 + dm:41 + dm],
                    cx.w32[:, 512:1024], ALU.mult, ALU.add))
                kb.wait_all("gpsimd")
                kb.step("gpsimd", "dmaG", 16, lambda e, dm=dm, toff=toff: e.dma_start(
                    key("OUT")[dm, :, toff:toff + 512], cx.w32[:, 512:1024]))


def build_program(fused):
    nc = bass.Bass("TRN2", target_bir_lowering=False)
    dram = {}

    def d(name, shape, dt, kind="Internal"):
        dname = "i_" + name if kind == "ExternalInput" else name
        dram[name] = nc.dram_tensor(dname, shape, dt, kind=kind)
        return dram[name]

    d("XA", [1024, 2051], F32, kind="ExternalInput")
    d("ccol", [128, 8], F32, kind="ExternalInput")
    d("wmod", [1024, 6144], F32, kind="ExternalInput")
    d("fcols", [128, 2], F32, kind="ExternalInput")
    for s in range(NSLOT):
        d(f"winT{s}", [1024, 4096], F32, kind="ExternalInput")
        d(f"wxT{s}", [128, 1536], F32, kind="ExternalInput")
        d(f"wdtTb{s}", [128, 2048], F32, kind="ExternalInput")
        d(f"woT{s}", [2048, 1024], F32, kind="ExternalInput")
        d(f"cw{s}", [128, 64], F32, kind="ExternalInput")
        d(f"cb{s}", [128, 16], F32, kind="ExternalInput")
        d(f"alogT{s}", [16, 2048], F32, kind="ExternalInput")
        d(f"dpar{s}", [1, 2048], F32, kind="ExternalInput")
        if fused:
            d(f"arinH{s}", [16, 2048], F16)
            d(f"aroutH{s}", [16, 2048], F16)
            d(f"arinX{s}", [128, 32], F32)
            d(f"aroutX{s}", [128, 32], F32)
        else:
            d(f"arinH{s}", [16, 2048], F16, kind="ExternalOutput")
            d(f"aroutH{s}", [16, 2048], F16, kind="ExternalInput")
            d(f"arinX{s}", [128, 32], F32, kind="ExternalOutput")
            d(f"aroutX{s}", [128, 32], F32, kind="ExternalInput")
    d("w1T", [1024, 4096], F32, kind="ExternalInput")
    d("b1T", [128, 32], F32, kind="ExternalInput")
    d("w2T", [4096, 1024], F32, kind="ExternalInput")
    d("b2T", [128, 8], F32, kind="ExternalInput")
    d("maskT", [8, 128, 2048], F16, kind="ExternalInput")
    for n, a in host_consts().items():
        dt = {np.dtype("float32"): F32, np.dtype(ml_dtypes.bfloat16): BF16,
              np.dtype("float16"): F16}[a.dtype]
        d(n, list(a.shape), dt, kind="ExternalInput")
    d("u_hbm", [16, 128, TP], F16)
    d("z_hbm", [16, 128, TP], F16)
    d("d_hbm", [17, 128, 2048], F16)
    d("wo_hbm", [2048, 1024], F16)
    d("x2_hbm", [8, 128, 2048], F32)
    d("h2_hbm", [8, 128, 2048], F8)
    d("h1_hbm", [32, 128, 512], F8)
    d("w1q_hbm", [32, 128, 1024], F8)
    d("w2q_hbm", [32, 128, 1024], F8)
    d("OUT", [8, 128, 2048], F32, kind="ExternalOutput")
    if DEBUG:
        d("DBGM", [128, 48], F32, kind="ExternalOutput")
        d("DBGH", [8, 128, 2048], F16, kind="ExternalOutput")
        d("DBGS", [8, 128, 2048], F16, kind="ExternalOutput")
        d("DBGX2", [8, 128, 2048], F32, kind="ExternalOutput")
        d("DBGH2", [8, 128, 2048], F8, kind="ExternalOutput")

    kb = KB(nc)
    cx = Ctx(nc)

    def mkkey(s):
        def key2(name):
            if name in ("winT", "wxT", "wdtTb", "woT", "cw", "cb", "alogT", "dpar"):
                return dram[f"{name}{s}"]
            return dram[name]
        return key2

    gkey = mkkey(0)

    with nc.Block() as block:
        emit_load_consts(kb, cx, gkey)
        emit_modvecs(kb, cx, gkey)
        if DEBUG:
            kb.wait_all("gpsimd")
            kb.step("gpsimd", "dmaG", 16, lambda e: e.dma_start(dram["DBGM"][:], cx.mcols[:]))
        emit_ln(kb, cx, gkey, phase=1)
        if DEBUG:
            kb.wait_all("gpsimd")
            for kt in range(8):
                kb.step("gpsimd", "dmaG", 16, lambda e, kt=kt: e.dma_start(dram["DBGH"][kt, :, :], cx.xm[kt][:]))
        for s in range(NSLOT):
            key = mkkey(s)
            emit_slot_head(kb, cx, key, s)
            emit_layer(kb, cx, key, pre_p3=lambda s=s, key=key: emit_ar_consume(kb, cx, key, s))
        if DEBUG:
            kb.wait_all("gpsimd")
            for kt in range(8):
                kb.step("gpsimd", "dmaG", 16, lambda e, kt=kt: e.dma_start(dram["DBGS"][kt, :, :], cx.xm[kt][:]))
        emit_x2(kb, cx, gkey)
        emit_ln(kb, cx, gkey, phase=2)
        if DEBUG:
            kb.wait_all("sync")
            kb.wait_all("gpsimd")
            kb.step("gpsimd", "dmaG", 16, lambda e: e.dma_start(dram["DBGX2"][:], dram["x2_hbm"][:]))
            kb.step("gpsimd", "dmaG", 16, lambda e: e.dma_start(dram["DBGH2"][:], dram["h2_hbm"][:]))
        emit_ffn_wcast(kb, cx, gkey)
        emit_ffn(kb, cx, gkey)
        kb.wait("gpsimd", "dmaG")
        kb.wait("sync", "dmaL")
        kb.run_into_block(block)
    return nc


def host_inputs(inputs):
    f32 = np.float32
    x = np.asarray(inputs["L0_attn_x"], f32)
    W_in = np.asarray(inputs["L0_attn_W_in"], f32)
    conv_w = np.asarray(inputs["L0_attn_conv_w"], f32)
    conv_b = np.asarray(inputs["L0_attn_conv_b"], f32)
    W_x = np.asarray(inputs["L0_attn_W_x"], f32)
    W_dt = np.asarray(inputs["L0_attn_W_dt"], f32)
    b_dt = np.asarray(inputs["L0_attn_b_dt"], f32)
    A_log = np.asarray(inputs["L0_attn_A_log"], f32)
    D_par = np.asarray(inputs["L0_attn_D_par"], f32)
    W_out = np.asarray(inputs["L0_attn_W_out"], f32)
    w1 = np.asarray(inputs["L0_ffn_w1"], f32)
    b1 = np.asarray(inputs["L0_ffn_b1"], f32)
    w2 = np.asarray(inputs["L0_ffn_w2"], f32)
    b2 = np.asarray(inputs["L0_ffn_b2"], f32)
    c = np.asarray(inputs["L0_c"], f32)
    wmod = np.asarray(inputs["L0_wmod"], f32)

    def layer_set(i):
        if i is None:
            return {
                "winT": np.zeros((1024, 4096), f32), "wxT": np.zeros((128, 1536), f32),
                "wdtTb": np.zeros((128, 2048), f32), "woT": np.zeros((2048, 1024), f32),
                "cw": np.zeros((128, 64), f32), "cb": np.zeros((128, 16), f32),
                "alogT": np.zeros((16, 2048), f32), "dpar": np.zeros((1, 2048), f32),
            }
        return {
            "winT": np.ascontiguousarray(W_in[i].T),
            "wxT": np.ascontiguousarray(W_x[i].T.reshape(16, 128, 96).transpose(1, 0, 2).reshape(128, 16 * 96)),
            "wdtTb": np.ascontiguousarray(np.concatenate(
                [W_dt[i].T, np.zeros((32, 2048), f32), b_dt[i][None, :], np.zeros((31, 2048), f32)], 0)),
            "woT": np.ascontiguousarray(W_out[i].T),
            "cw": np.ascontiguousarray(conv_w[i].reshape(16, 128, 4).transpose(1, 0, 2).reshape(128, 64)),
            "cb": np.ascontiguousarray(conv_b[i].reshape(16, 128).T),
            "alogT": np.ascontiguousarray(A_log[i].T),
            "dpar": np.ascontiguousarray(D_par[i][None, :]),
        }

    lsets = [layer_set(i) for i in range(8)]
    zset = layer_set(None)
    consts = host_consts()

    import jax
    _k42 = lambda: jax.random.key(42, impl="threefry2x32")
    try:
        with jax.default_device(jax.local_devices(backend="cpu")[0]):
            keep = np.asarray(jax.random.bernoulli(_k42(), 0.8, (4, 4096, 1024)))
    except Exception:
        keep = np.asarray(jax.random.bernoulli(_k42(), 0.8, (4, 4096, 1024)))
    maskf = keep.astype(np.float16) * np.float16(1.25)

    shared = {
        "wmod": np.ascontiguousarray(wmod),
        "w1T": np.ascontiguousarray(w1.T),
        "b1T": np.ascontiguousarray(b1.reshape(32, 128).T),
        "w2T": np.ascontiguousarray(w2.T),
        "b2T": np.ascontiguousarray(b2.reshape(8, 128).T),
    }
    shared.update(consts)

    per_core = []
    for core in range(8):
        b, hf = core // 2, core % 2
        off = 2048 * hf
        XA = np.zeros((1024, 2051), f32)
        XA[:, 3:] = x[b, off:off + 2048].T
        if hf:
            XA[:, 0:3] = x[b, 2045:2048].T
        fcols = np.zeros((128, 2), f32)
        fcols[:, 0] = float(hf)
        fcols[:, 1] = 1.0 - float(hf)
        m = {
            "XA": XA,
            "ccol": np.ascontiguousarray(c[b].reshape(8, 128).T),
            "fcols": fcols,
            "maskT": np.ascontiguousarray(maskf[b, off:off + 2048].T.reshape(8, 128, 2048)),
        }
        m.update(shared)
        for s in range(NSLOT):
            li = s if hf == 0 else s - 1
            src = zset if (li < 0 or li > 7) else lsets[li]
            for kname, arr in src.items():
                m[f"{kname}{s}"] = arr
        if not FUSED:
            for s in range(NSLOT):
                m[f"aroutH{s}"] = np.zeros((16, 2048), np.float16)
                m[f"aroutX{s}"] = np.zeros((128, 32), f32)
        per_core.append(m)
    return per_core


def _prefix(per_core):
    return [{("i_" + k): v for k, v in m.items()} for m in per_core]


def kernel(**inputs):
    per_core = host_inputs(inputs)
    nc = build_program(FUSED)
    out = np.empty((4, 4096, 1024), np.float32)
    if FUSED:
        res = run_bass_kernel_spmd(nc, _prefix(per_core), core_ids=list(range(8)))
        for core in range(8):
            b, hf = core // 2, core % 2
            arr = np.asarray(res.results[core]["OUT"])
            out[b, 2048 * hf:2048 * (hf + 1)] = arr.transpose(2, 0, 1).reshape(2048, 1024)
    else:
        res1 = run_bass_kernel_spmd(nc, _prefix(per_core), core_ids=list(range(8)))
        per2 = []
        for core in range(8):
            b, hf = core // 2, core % 2
            m = dict(per_core[core])
            if hf == 1:
                r0 = res1.results[core - 1]
                for s in range(NSLOT):
                    m[f"aroutH{s}"] = np.asarray(r0[f"arinH{s}"])
                    m[f"aroutX{s}"] = np.asarray(r0[f"arinX{s}"])
            per2.append(m)
        res2 = run_bass_kernel_spmd(nc, _prefix(per2), core_ids=list(range(8)))
        for core in range(8):
            b, hf = core // 2, core % 2
            r = (res1 if hf == 0 else res2).results[core]
            arr = np.asarray(r["OUT"])
            out[b, 2048 * hf:2048 * (hf + 1)] = arr.transpose(2, 0, 1).reshape(2048, 1024)
    return out


# version 08
# speedup vs baseline: 1.5111x; 1.0317x over previous
import numpy as np
import ml_dtypes
import concourse.bass as bass
import concourse.mybir as mybir
from concourse.bass_utils import run_bass_kernel_spmd

F32 = mybir.dt.float32
F16 = mybir.dt.float16
BF16 = mybir.dt.bfloat16
F8 = mybir.dt.float8e4
ALU = mybir.AluOpType
ACT = mybir.ActivationFunctionType

FUSED = True
DEBUG = False

Lc = 2048
D = 1024
CC = 2048
NST = 16
WT = 127
NW = 17
LASTV = Lc - (NW - 1) * WT
N_MM = 512
TP = 2176
NSLOT = 9
GRPW = [range(0, 8), range(8, 16), range(16, 17)]
WOG = [range(2 * i, 2 * i + 2) for i in range(8)] + [range(16, 17)]
ENGINES = ("sync", "vector", "scalar", "tensor", "gpsimd")
PERSLOT = ("winT", "wxT", "wdtTb", "woT", "cw", "cb", "alogT", "dpar",
           "arinH", "aroutH", "arinX", "aroutX")


class KB:
    PARANOID = ()

    def __init__(self, nc):
        self.nc = nc
        self.ops = {e: [] for e in ENGINES}
        self.cnt = {}
        self.sems = {}
        self.lastw = {}

    def sem(self, name):
        if name not in self.sems:
            self.sems[name] = self.nc.semaphore(name).__enter__()
            self.cnt[name] = 0
        return self.sems[name]

    def wait(self, engine, name, value=None):
        h = self.sem(name)
        v = self.cnt[name] if value is None else value
        if v > 0:
            self.ops[engine].append(lambda eng, h=h, v=v: eng.wait_ge(h, v))

    REGION = True
    OWN = {"vector": "dve", "scalar": "act", "tensor": "pe"}

    def wait_all(self, engine):
        wl = self.lastw.setdefault(engine, {})
        for nm in list(self.cnt):
            v = self.cnt.get(nm, 0)
            if v > 0 and nm != self.OWN.get(engine) and wl.get(nm, 0) < v:
                self.wait(engine, nm)
                wl[nm] = v

    def step(self, engine, name, amount, fn):
        if (self.PARANOID is True or (self.PARANOID and engine in self.PARANOID)) and self.REGION:
            self.wait_all(engine)
        if engine == "tensor" and name == "pe":
            self.sem(name)
            self.ops[engine].append(lambda eng, fn=fn: fn(eng))
            return self.cnt[name]
        h = self.sem(name)
        self.cnt[name] = self.cnt.get(name, 0) + amount
        if engine in ("vector", "scalar", "tensor") and name != "dmaG" and name != "dmaL":
            def wrapped(eng, fn=fn, h=h, a=amount):
                fn(eng)
                eng.drain().then_inc(h, a)
        else:
            def wrapped(eng, fn=fn, h=h, a=amount):
                fn(eng).then_inc(h, a)
        self.ops[engine].append(wrapped)
        return self.cnt[name]

    def tensor_drain(self):
        self.ops["tensor"].append(lambda eng: eng.drain())

    def plain(self, engine, fn):
        self.ops[engine].append(fn)

    def pe_fence(self):
        h = self.sem("pe")
        self.cnt["pe"] = self.cnt.get("pe", 0) + 1
        def wrapped(eng, h=h):
            eng.drain().then_inc(h, 1)
        self.ops["tensor"].append(wrapped)
        return self.cnt["pe"]

    def run_into_block(self, block):
        for e in ENGINES:
            ops = self.ops[e]
            if not ops:
                continue
            def make(ops):
                def body(eng):
                    for f in ops:
                        f(eng)
                return body
            getattr(block, e)(make(ops))


def sbuf(nc, name, shape, dt):
    return nc.sbuf_tensor(name, shape, dt).__enter__()


def pst(nc, name, shape, dt):
    return nc.psum_tensor(name, shape, dt).__enter__()


class Ctx:
    def __init__(self, nc):
        self.nc = nc
        self.xm = [sbuf(nc, f"xm{i}", [128, Lc], F16) for i in range(8)]
        self.ustg = sbuf(nc, "ustg", [128, 2048], F16)
        self.zsil = sbuf(nc, "zsil", [128, 16 * WT], F16)
        self.dwin = sbuf(nc, "dwin", [128, CC], F16)
        self.dlw = sbuf(nc, "dlw", [128, CC], F16)
        self.duw = sbuf(nc, "duw", [128, CC], F16)
        self.rg = sbuf(nc, "rg", [128, CC], BF16)
        self.ww = sbuf(nc, "ww", [128, CC], BF16)
        self.yw = sbuf(nc, "yw", [128, CC], F16)
        self.statn = sbuf(nc, "statn", [128, 128], BF16)
        self.trilmid = sbuf(nc, "trilmid", [128, 128], F16)
        self.trilsum = sbuf(nc, "trilsum", [128, 128], BF16)
        self.ident = sbuf(nc, "ident", [128, 128], F16)
        self.bct = sbuf(nc, "bct", [128, 2 * NST], F16)
        self.bctf = sbuf(nc, "bctf", [128, 2 * NST], F32)
        self.abc = [sbuf(nc, f"abc{i}", [128, CC], BF16) for i in range(NST)]
        self.alpha16 = sbuf(nc, "alpha16", [17, CC], F16)
        self.alrep = sbuf(nc, "alrep", [128, CC], BF16)
        self.gA = sbuf(nc, "gA", [128, CC], BF16)
        self.gB = sbuf(nc, "gB", [128, CC], BF16)
        _dmde = sbuf(nc, "dmde", [96, 1024], F16)
        self.dmde = [_dmde[32 * g:32 * (g + 1)] for g in range(3)]
        self.hselA = sbuf(nc, "hselA", [128, 8 * 32], F16)
        self.hselB = sbuf(nc, "hselB", [128, 8 * 32], F16)
        self.indA0 = sbuf(nc, "indA0", [96, 128], F16)
        self.indA1 = sbuf(nc, "indA1", [96, 128], F16)
        self.indB0 = sbuf(nc, "indB0", [96, 128], F16)
        self.indB1 = sbuf(nc, "indB1", [96, 128], F16)
        self.rep8 = sbuf(nc, "rep8", [NST, 128], F16)
        self.maskw16 = sbuf(nc, "maskw16", [128, 1], F32)
        self.hrun = sbuf(nc, "hrun", [NST, CC], F16)
        self.hraw = sbuf(nc, "hraw", [NST, CC], BF16)
        self.gsl = sbuf(nc, "gsl", [NST, CC], BF16)
        self.dpb = sbuf(nc, "dpb", [128, CC], F16)
        self.xdblb = sbuf(nc, "xdblb", [128, Lc + WT], F16)
        self.wrhs = [sbuf(nc, f"wrhs{i}", [128, 2 * WT], F16) for i in range(16)]
        self.w32 = sbuf(nc, "w32", [128, 1024], F32)
        self.wkp = [sbuf(nc, f"wkp{k}", [128, 256], F16) for k in range(8)]
        self.convw = sbuf(nc, "convw", [128, 16 * 4], F32)
        self.convb = sbuf(nc, "convb", [128, 16], F32)
        self.wxtb = sbuf(nc, "wxtb", [128, 16 * 96], F16)
        self.wdtb = sbuf(nc, "wdtb", [128, CC], F16)
        self.xhead = sbuf(nc, "xhead", [128, 3], F16)
        self.zpad = sbuf(nc, "zpad", [128, TP - Lc], F16)
        self.wos = [sbuf(nc, f"wos{k}", [128, D], F16) for k in range(2)]
        self.xmt = sbuf(nc, "xmt", [128, 32], F16)
        self.ones1 = sbuf(nc, "ones1", [1, 128], F16)
        self.onescol = sbuf(nc, "onescol", [128, 1], F16)
        self.xtl = sbuf(nc, "xtl", [128, 32], F16)
        self.mcols = sbuf(nc, "mcols", [128, 48], F32)
        self.fcols = sbuf(nc, "fcols", [128, 2], F32)
        self.scol = sbuf(nc, "scol", [128, 16], F16)
        self.h2f8 = sbuf(nc, "h2f8", [128, 4096], F8)
        self.w1p = sbuf(nc, "w1p", [128, 1024], F8)
        self.w2s = sbuf(nc, "w2s", [128, 1024], F8)
        self.h1q8 = sbuf(nc, "h1q8", [128, 512], F8)
        self.bq = sbuf(nc, "bq", [128, 40], F32)
        self.psA = pst(nc, "psA", [128, 1024], F32)
        self.psB = pst(nc, "psB", [128, 1024], F32)
        self.psC = pst(nc, "psC", [128, 1024], F32)
        self.psD = pst(nc, "psD", [128, 512], F32)
        self.psT = pst(nc, "psT", [128, 1024], F16)


def host_consts():
    bf = ml_dtypes.bfloat16
    trilmid = np.zeros((128, 128), np.float32)
    for t in range(WT):
        for s in range(WT):
            trilmid[s, t] = (1.0 if s <= t else 0.0) - (1.0 if s <= 63 else 0.0)
    trilsum = np.zeros((128, 128), np.float32)
    for t in range(WT):
        trilsum[: t + 1, t] = 1.0
    trilsum[127, :] = 1.0
    trilsum[:WT, 127] = 1.0
    ident = np.eye(128, dtype=np.float32)
    hselA = np.zeros((128, 8 * 32), np.float32)
    hselB = np.zeros((128, 8 * 32), np.float32)
    for wl in range(8):
        hselA[:64, 32 * wl + 2 * wl] = 1.0
        hselA[64:WT, 32 * wl + 2 * wl + 1] = 1.0
        hselB[:64, 32 * wl + 16 + 2 * wl] = 1.0
        hselB[64:WT, 32 * wl + 17 + 2 * wl] = 1.0
    indA0 = np.zeros((32, 128), np.float32)
    indA1 = np.zeros((32, 128), np.float32)
    indB0 = np.zeros((32, 128), np.float32)
    indB1 = np.zeros((32, 128), np.float32)
    for wl in range(8):
        indA0[2 * wl, 16 * wl:16 * wl + 16] = 1.0
        indA1[16 + 2 * wl, 16 * wl:16 * wl + 16] = 1.0
        indB0[2 * wl + 1, 16 * wl:16 * wl + 16] = 1.0
        indB1[16 + 2 * wl + 1, 16 * wl:16 * wl + 16] = 1.0
    rep8 = np.zeros((16, 128), np.float32)
    for wl in range(8):
        for n in range(16):
            rep8[n, 16 * wl + n] = 1.0
    maskw16 = np.zeros((128, 1), np.float32)
    maskw16[:LASTV] = 1.0
    f16 = np.float16
    onesrow = np.zeros((1, Lc + WT), np.float16)
    onesrow[0, :Lc] = 1.0
    return {
        "trilmid": trilmid.astype(f16), "trilsum": trilsum.astype(bf),
        "ident": ident.astype(f16), "hselA": hselA.astype(f16), "hselB": hselB.astype(f16),
        "indA0": np.tile(indA0, (3, 1)).astype(f16), "indA1": np.tile(indA1, (3, 1)).astype(f16),
        "indB0": np.tile(indB0, (3, 1)).astype(f16), "indB1": np.tile(indB1, (3, 1)).astype(f16),
        "rep8": rep8.astype(f16), "maskw16": maskw16,
        "ones1": np.ones((1, 128), np.float32).astype(f16),
        "onescol": np.ones((128, 1), np.float32).astype(f16),
        "onesrow": onesrow,
    }


CONST_NAMES = ("trilmid", "trilsum", "ident", "hselA", "hselB",
               "indA0", "indA1", "indB0", "indB1", "rep8", "maskw16",
               "ones1", "onescol")


def emit_load_consts(kb, cx, key):
    for name in CONST_NAMES:
        t = getattr(cx, name)
        kb.step("sync", "dmaL", 16, lambda e, t=t, n=name: e.dma_start(t[:], key(n)[:]))
    kb.step("sync", "dmaL", 16, lambda e: e.dma_start(cx.fcols[:], key("fcols")[:]))
    kb.wait("vector", "dmaL")
    kb.step("vector", "dve", 1, lambda e: e.memset(cx.zpad[:], 0.0))
    kb.step("vector", "dve", 1, lambda e: e.memset(cx.hrun[:], 0.0))
    kb.wait("gpsimd", "dve")
    kb.step("gpsimd", "dmaG", 16, lambda e: e.dma_start(cx.bct[127:128, :], key("onesrow")[0:1, 0:2 * NST]))


def emit_modvecs(kb, cx, key):
    kb.wait_all("sync")
    kb.step("sync", "dmaL", 16, lambda e: e.dma_start(cx.w32[:, 0:8], key("ccol")[:]))
    kb.wait_all("scalar")
    kb.step("scalar", "act", 1, lambda e: e.activation(cx.w32[:, 8:16], cx.w32[:, 0:8], ACT.Silu))
    kb.wait("vector", "act")
    kb.step("vector", "dve", 1, lambda e: e.memset(cx.scol[:], 0.0))
    for k in range(8):
        kb.step("vector", "dve", 1, lambda e, k=k: e.tensor_copy(cx.scol[:, 2 * k:2 * k + 1], cx.w32[:, 8 + k:9 + k]))
    for jc in range(6):
        for kt in range(8):
            kb.wait_all("sync")
            kb.step("sync", "dmaL", 16, lambda e, jc=jc, kt=kt: e.dma_start(
                cx.w32[:, 0:1024], key("wmod")[128 * kt:128 * (kt + 1), 1024 * jc:1024 * (jc + 1)]))
            kb.wait("vector", "dmaL")
            kb.step("vector", "dve", 1, lambda e: e.tensor_copy(cx.duw[:, 0:1024], cx.w32[:, 0:1024]))
            kb.wait_all("tensor")
            for jj in range(8):
                kb.step("tensor", "pe", 1, lambda e, jj=jj, kt=kt: e.matmul(
                    cx.psD[:, 2 * jj:2 * jj + 2], cx.duw[:, 128 * jj:128 * (jj + 1)],
                    cx.scol[:, 2 * kt:2 * kt + 2], start=True, stop=True))
            kb.pe_fence()
            kb.wait("vector", "pe")
            for jj in range(8):
                if kt == 0:
                    kb.step("vector", "dve", 1, lambda e, jc=jc, jj=jj: e.tensor_copy(
                        cx.mcols[:, 8 * jc + jj:8 * jc + jj + 1], cx.psD[:, 2 * jj:2 * jj + 1]))
                else:
                    kb.step("vector", "dve", 1, lambda e, jc=jc, jj=jj: e.tensor_tensor(
                        cx.mcols[:, 8 * jc + jj:8 * jc + jj + 1],
                        cx.mcols[:, 8 * jc + jj:8 * jc + jj + 1],
                        cx.psD[:, 2 * jj:2 * jj + 1], ALU.add))
    kb.step("vector", "dve", 1, lambda e: e.tensor_scalar(cx.mcols[:, 8:16], cx.mcols[:, 8:16], 1.0, None, ALU.add))
    kb.step("vector", "dve", 1, lambda e: e.tensor_scalar(cx.mcols[:, 32:40], cx.mcols[:, 32:40], 1.0, None, ALU.add))
    for base in (16, 40):
        kb.step("vector", "dve", 1, lambda e, base=base: e.tensor_scalar(
            cx.w32[:, 0:8], cx.mcols[:, base:base + 8], 0.0, None, ALU.is_ge))
        kb.step("vector", "dve", 1, lambda e: e.tensor_scalar(
            cx.w32[:, 0:8], cx.w32[:, 0:8], 2.0, -1.0, ALU.mult, ALU.add))
        kb.step("vector", "dve", 1, lambda e, base=base: e.tensor_scalar(
            cx.w32[:, 8:16], cx.mcols[:, base:base + 8], -1.0, None, ALU.mult))
        kb.step("vector", "dve", 1, lambda e, base=base: e.tensor_tensor(
            cx.w32[:, 8:16], cx.mcols[:, base:base + 8], cx.w32[:, 8:16], ALU.max))
        kb.step("vector", "dve", 1, lambda e: e.tensor_scalar(
            cx.w32[:, 8:16], cx.w32[:, 8:16], 0.25, None, ALU.max))
        kb.step("vector", "dve", 1, lambda e, base=base: e.tensor_tensor(
            cx.mcols[:, base:base + 8], cx.w32[:, 0:8], cx.w32[:, 8:16], ALU.mult))


def emit_ln(kb, cx, key, phase):
    if phase == 1:
        kb.wait("vector", "dmaL")
        kb.step("vector", "dve", 1, lambda e: e.memset(cx.xtl[:], 0.0))
        for kt in range(8):
            for h in range(2):
                kb.wait_all("sync")
                kb.step("sync", "dmaL", 16, lambda e, kt=kt, h=h: e.dma_start(
                    cx.w32[:, 0:1024], key("XA")[128 * kt:128 * (kt + 1), 3 + 1024 * h:3 + 1024 * (h + 1)]))
                kb.wait("vector", "dmaL")
                kb.step("vector", "dve", 1, lambda e, kt=kt, h=h: e.tensor_copy(
                    cx.xm[kt][:, 1024 * h:1024 * (h + 1)], cx.w32[:, 0:1024]))
            kb.wait_all("sync")
            kb.step("sync", "dmaL", 16, lambda e, kt=kt: e.dma_start(
                cx.w32[:, 0:3], key("XA")[128 * kt:128 * (kt + 1), 0:3]))
            kb.wait("vector", "dmaL")
            kb.step("vector", "dve", 1, lambda e, kt=kt: e.tensor_copy(
                cx.xtl[:, 4 * kt:4 * kt + 3], cx.w32[:, 0:3]))
    for th in range(2):
        for kt in range(8):
            kb.wait("vector", "pe")
            kb.step("vector", "dve", 1, lambda e, kt=kt, th=th: e.tensor_tensor(
                cx.duw[:, 0:1024], cx.xm[kt][:, 1024 * th:1024 * (th + 1)],
                cx.xm[kt][:, 1024 * th:1024 * (th + 1)], ALU.mult))
            kb.wait_all("tensor")
            for c in range(2):
                kb.step("tensor", "pe", 1, lambda e, kt=kt, th=th, c=c: e.matmul(
                    cx.psA[0:1, 512 * c:512 * (c + 1)], cx.onescol[:],
                    cx.xm[kt][:, 1024 * th + 512 * c:1024 * th + 512 * (c + 1)],
                    start=(kt == 0), stop=(kt == 7)))
                kb.step("tensor", "pe", 1, lambda e, kt=kt, c=c: e.matmul(
                    cx.psB[0:1, 512 * c:512 * (c + 1)], cx.onescol[:],
                    cx.duw[:, 512 * c:512 * (c + 1)], start=(kt == 0), stop=(kt == 7)))
            kb.pe_fence()
        kb.wait("vector", "pe")
        kb.step("vector", "dve", 1, lambda e, th=th: e.tensor_scalar(
            cx.ustg[0:1, 1024 * th:1024 * (th + 1)], cx.psA[0:1, 0:1024], 1.0 / 1024, None, ALU.mult))
        kb.step("vector", "dve", 1, lambda e, th=th: e.tensor_scalar(
            cx.dpb[0:1, 1024 * th:1024 * (th + 1)], cx.psB[0:1, 0:1024], 1.0 / 1024, None, ALU.mult))
    kb.step("vector", "dve", 1, lambda e: e.tensor_tensor(
        cx.dwin[0:1, :], cx.ustg[0:1, :], cx.ustg[0:1, :], ALU.mult))
    kb.step("vector", "dve", 1, lambda e: e.tensor_tensor(
        cx.dpb[0:1, :], cx.dpb[0:1, :], cx.dwin[0:1, :], ALU.subtract))
    kb.step("vector", "dve", 1, lambda e: e.tensor_scalar(
        cx.dpb[0:1, :], cx.dpb[0:1, :], 1e-5, None, ALU.add))
    kb.wait_all("scalar")
    kb.step("scalar", "act", 1, lambda e: e.activation(
        cx.dwin[0:1, :], cx.dpb[0:1, :], ACT.Ln, bias=0.0, scale=1.0))
    kb.step("scalar", "act", 1, lambda e: e.activation(
        cx.dpb[0:1, :], cx.dwin[0:1, :], ACT.Exp, bias=0.0, scale=-0.5))
    kb.wait_all("tensor")
    for c in range(4):
        kb.step("tensor", "pe", 1, lambda e, c=c: e.matmul(
            (cx.psA if c < 2 else cx.psB)[:, 512 * (c % 2):512 * (c % 2 + 1)],
            cx.ones1[:], cx.ustg[0:1, 512 * c:512 * (c + 1)], start=True, stop=True))
    kb.pe_fence()
    kb.wait("vector", "pe")
    kb.plain("vector", lambda e: e.tensor_copy(cx.dwin[:, 0:1024], cx.psA[:, :1024]))
    kb.step("vector", "dve", 1, lambda e: e.tensor_copy(cx.dwin[:, 1024:], cx.psB[:, :1024]))
    kb.wait_all("tensor")
    for c in range(4):
        kb.step("tensor", "pe", 1, lambda e, c=c: e.matmul(
            (cx.psA if c < 2 else cx.psB)[:, 512 * (c % 2):512 * (c % 2 + 1)],
            cx.ones1[:], cx.dpb[0:1, 512 * c:512 * (c + 1)], start=True, stop=True))
    kb.pe_fence()
    kb.wait("vector", "pe")
    kb.plain("vector", lambda e: e.tensor_copy(cx.dlw[:, 0:1024], cx.psA[:, :1024]))
    kb.step("vector", "dve", 1, lambda e: e.tensor_copy(cx.dlw[:, 1024:], cx.psB[:, :1024]))
    if phase == 1:
        kb.step("vector", "dve", 1, lambda e: e.tensor_tensor(
            cx.duw[:, 0:32], cx.xtl[:], cx.xtl[:], ALU.mult))
        kb.wait_all("tensor")
        for kt in range(8):
            kb.step("tensor", "pe", 1, lambda e, kt=kt: e.matmul(
                cx.psD[0:1, 0:4], cx.onescol[:], cx.xtl[:, 4 * kt:4 * (kt + 1)],
                start=(kt == 0), stop=(kt == 7)))
        for kt in range(8):
            kb.step("tensor", "pe", 1, lambda e, kt=kt: e.matmul(
                cx.psD[0:1, 4:8], cx.onescol[:], cx.duw[:, 4 * kt:4 * (kt + 1)],
                start=(kt == 0), stop=(kt == 7)))
        kb.pe_fence()
        kb.wait("vector", "pe")
        kb.step("vector", "dve", 1, lambda e: e.tensor_scalar(
            cx.bct[0:1, 0:4], cx.psD[0:1, 0:4], 1.0 / 1024, None, ALU.mult))
        kb.step("vector", "dve", 1, lambda e: e.tensor_scalar(
            cx.bct[0:1, 4:8], cx.psD[0:1, 4:8], 1.0 / 1024, None, ALU.mult))
        kb.step("vector", "dve", 1, lambda e: e.tensor_tensor(
            cx.bct[0:1, 8:12], cx.bct[0:1, 0:4], cx.bct[0:1, 0:4], ALU.mult))
        kb.step("vector", "dve", 1, lambda e: e.tensor_tensor(
            cx.bct[0:1, 4:8], cx.bct[0:1, 4:8], cx.bct[0:1, 8:12], ALU.subtract))
        kb.step("vector", "dve", 1, lambda e: e.tensor_scalar(
            cx.bct[0:1, 4:8], cx.bct[0:1, 4:8], 1e-5, None, ALU.add))
        kb.wait_all("scalar")
        kb.step("scalar", "act", 1, lambda e: e.activation(
            cx.bct[0:1, 8:12], cx.bct[0:1, 4:8], ACT.Ln, bias=0.0, scale=1.0))
        kb.step("scalar", "act", 1, lambda e: e.activation(
            cx.bct[0:1, 4:8], cx.bct[0:1, 8:12], ACT.Exp, bias=0.0, scale=-0.5))
        kb.wait_all("tensor")
        kb.step("tensor", "pe", 1, lambda e: e.matmul(
            cx.psD[:, 8:12], cx.ones1[:], cx.bct[0:1, 0:4], start=True, stop=True))
        kb.step("tensor", "pe", 1, lambda e: e.matmul(
            cx.psD[:, 12:16], cx.ones1[:], cx.bct[0:1, 4:8], start=True, stop=True))
        kb.pe_fence()
        kb.wait("vector", "pe")
        kb.step("vector", "dve", 1, lambda e: e.tensor_copy(cx.statn[:, 0:4], cx.psD[:, 8:12]))
        kb.step("vector", "dve", 1, lambda e: e.tensor_copy(cx.statn[:, 4:8], cx.psD[:, 12:16]))
        for kt in range(8):
            kb.step("vector", "dve", 1, lambda e, kt=kt: e.tensor_tensor(
                cx.xtl[:, 4 * kt:4 * (kt + 1)], cx.xtl[:, 4 * kt:4 * (kt + 1)],
                cx.statn[:, 0:4], ALU.subtract))
            kb.step("vector", "dve", 1, lambda e, kt=kt: e.tensor_tensor(
                cx.xtl[:, 4 * kt:4 * (kt + 1)], cx.xtl[:, 4 * kt:4 * (kt + 1)],
                cx.statn[:, 4:8], ALU.mult))
            kb.step("vector", "dve", 1, lambda e, kt=kt: e.tensor_scalar(
                cx.xtl[:, 4 * kt:4 * (kt + 1)], cx.xtl[:, 4 * kt:4 * (kt + 1)],
                cx.mcols[:, 8 + kt:9 + kt], cx.mcols[:, kt:kt + 1], ALU.mult, ALU.add))
    cfoff = 8 if phase == 1 else 32
    soff = 0 if phase == 1 else 24
    for kt in range(8):
        kb.step("vector", "dve", 1, lambda e, kt=kt: e.tensor_tensor(
            cx.duw[:, :], cx.xm[kt][:, :], cx.dwin[:, :], ALU.subtract))
        kb.step("vector", "dve", 1, lambda e: e.tensor_tensor(
            cx.duw[:, :], cx.duw[:, :], cx.dlw[:, :], ALU.mult))
        if phase == 1:
            kb.step("vector", "dve", 1, lambda e, kt=kt: e.tensor_scalar(
                cx.xm[kt][:, :], cx.duw[:, :],
                cx.mcols[:, cfoff + kt:cfoff + kt + 1], cx.mcols[:, soff + kt:soff + kt + 1],
                ALU.mult, ALU.add))
        else:
            kb.wait("vector", "dmaG")
            kb.step("vector", "dve", 1, lambda e, kt=kt: e.tensor_scalar(
                cx.h2f8[:, 0:2048], cx.duw[:, :],
                cx.mcols[:, cfoff + kt:cfoff + kt + 1], cx.mcols[:, soff + kt:soff + kt + 1],
                ALU.mult, ALU.add))
            kb.wait_all("gpsimd")
            kb.step("gpsimd", "dmaG", 16, lambda e, kt=kt: e.dma_start(
                key("h2_hbm")[kt, :, :], cx.h2f8[:, 0:2048]))
    if phase == 1:
        kb.step("vector", "dve", 1, lambda e: e.tensor_scalar(
            cx.xmt[:], cx.xtl[:], cx.fcols[:, 0:1], None, ALU.mult))


def emit_slot_head(kb, cx, key, s):
    kb.wait_all("gpsimd")
    for kt in range(8):
        kb.step("gpsimd", "dmaG", 16, lambda e, kt=kt: e.dma_start(
            cx.xtl[:, 4 * kt:4 * kt + 3], cx.xm[kt][:, 2045:2048]))
    kb.wait_all("vector")
    kb.step("vector", "dve", 1, lambda e: e.tensor_copy(cx.w32[:, 0:32], cx.xtl[:]))
    kb.step("vector", "dve", 1, lambda e: e.tensor_scalar(
        cx.w32[:, 32:64], cx.w32[:, 0:32], cx.fcols[:, 1:2], None, ALU.mult))
    kb.step("vector", "dve", 1, lambda e: e.tensor_scalar(
        cx.hrun[:], cx.hrun[:], cx.fcols[0:16, 1:2], None, ALU.mult))
    kb.wait_all("gpsimd")
    kb.step("gpsimd", "dmaG", 16, lambda e, s=s: e.dma_start(key(f"arinX{s}")[:], cx.w32[:, 32:64]))
    kb.step("gpsimd", "dmaG", 16, lambda e, s=s: e.dma_start(key(f"arinH{s}")[:], cx.hrun[:]))
    if FUSED:
        kb.wait("gpsimd", "dmaG")
        kb.step("gpsimd", "cc", 1, lambda e, s=s: e.collective_compute(
            "AllReduce", ALU.add, replica_groups=[[0, 1], [2, 3], [4, 5], [6, 7]],
            ins=[key(f"arinH{s}")[:].opt()], outs=[key(f"aroutH{s}")[:].opt()]))
        kb.step("gpsimd", "cc", 1, lambda e, s=s: e.collective_compute(
            "AllReduce", ALU.add, replica_groups=[[0, 1], [2, 3], [4, 5], [6, 7]],
            ins=[key(f"arinX{s}")[:].opt()], outs=[key(f"aroutX{s}")[:].opt()]))
    if s > 0:
        if FUSED:
            kb.wait("gpsimd", "cc", 2 * s)
        kb.step("gpsimd", "dmaG", 16, lambda e, s=s: e.dma_start(
            cx.w32[:, 64:96], key(f"aroutX{s - 1}")[:]))
        kb.wait_all("vector")
        kb.step("vector", "dve", 1, lambda e: e.tensor_scalar(
            cx.w32[:, 64:96], cx.w32[:, 64:96], cx.fcols[:, 0:1], None, ALU.mult))
        kb.step("vector", "dve", 1, lambda e: e.tensor_tensor(
            cx.w32[:, 32:64], cx.w32[:, 32:64], cx.w32[:, 64:96], ALU.add))
        kb.step("vector", "dve", 1, lambda e: e.tensor_copy(cx.xmt[:], cx.w32[:, 32:64]))


def emit_ar_consume(kb, cx, key, s):
    if FUSED:
        kb.wait("gpsimd", "cc", 2 * s + 2)
    kb.wait_all("gpsimd")
    kb.step("gpsimd", "dmaG", 16, lambda e, s=s: e.dma_start(cx.hrun[:], key(f"aroutH{s}")[:]))
    kb.wait_all("vector")
    kb.step("vector", "dve", 1, lambda e: e.tensor_scalar(
        cx.hrun[:], cx.hrun[:], cx.fcols[0:16, 0:1], None, ALU.mult))


def emit_layer(kb, cx, key, pre_p3=None):
    nc = kb.nc

    kb.wait("gpsimd", "dmaL")
    kb.wait("vector", "pe")
    kb.wait("sync", "dve")
    kb.step("sync", "dmaL", 16, lambda e: e.dma_start(cx.convw[:], key("cw")[:]))
    kb.step("sync", "dmaL", 16, lambda e: e.dma_start(cx.convb[:], key("cb")[:]))
    for h in range(2):
        kb.wait("sync", "dve")
        kb.step("sync", "dmaL", 16, lambda e, h=h: e.dma_start(cx.w32[0:16, :], key("alogT")[:, 1024 * h:1024 * (h + 1)]))
        kb.wait("scalar", "dmaL")
        kb.step("scalar", "act", 1, lambda e: e.activation(cx.w32[0:16, :], cx.w32[0:16, :], ACT.Exp))
        kb.wait("vector", "act")
        kb.step("vector", "dve", 1, lambda e, h=h: e.tensor_copy(cx.alpha16[0:16, 1024 * h:1024 * (h + 1)], cx.w32[0:16, :]))
    for h in range(2):
        kb.wait("sync", "dve")
        kb.step("sync", "dmaL", 16, lambda e, h=h: e.dma_start(cx.w32[0:1, :], key("dpar")[:, 1024 * h:1024 * (h + 1)]))
        kb.wait("vector", "dmaL")
        kb.step("vector", "dve", 1, lambda e, h=h: e.tensor_copy(cx.ustg[0:1, 0:1024], cx.w32[0:1, :]))
        kb.wait("gpsimd", "dve")
        kb.step("gpsimd", "dmaG", 16, lambda e, h=h: e.dma_start(cx.alpha16[16:17, 1024 * h:1024 * (h + 1)], cx.ustg[0:1, 0:1024]))
        kb.wait("vector", "dmaG")
        kb.wait("sync", "dve")
        kb.step("sync", "dmaL", 16, lambda e, h=h: e.dma_start(cx.w32[0:128, :], key("wdtTb")[:, 1024 * h:1024 * (h + 1)]))
        kb.wait("vector", "dmaL")
        kb.step("vector", "dve", 1, lambda e, h=h: e.tensor_copy(cx.wdtb[:, 1024 * h:1024 * (h + 1)], cx.w32[0:128, :]))
    for h in range(2):
        w0, w1 = 1024 * h, min(16 * 96, 1024 * (h + 1))
        kb.wait("sync", "dve")
        kb.step("sync", "dmaL", 16, lambda e, w0=w0, w1=w1: e.dma_start(cx.w32[:, 0:w1 - w0], key("wxT")[:, w0:w1]))
        kb.wait("vector", "dmaL")
        kb.step("vector", "dve", 1, lambda e, w0=w0, w1=w1: e.tensor_copy(cx.wxtb[:, w0:w1], cx.w32[:, 0:w1 - w0]))
    for n in range(NST + 1):
        kb.wait("gpsimd", "pe")
        kb.wait("gpsimd", "dve")
        kb.step("gpsimd", "dmaG", 16, lambda e, n=n: e.dma_start(cx.ustg[0:1, 0:CC], cx.alpha16[n:n + 1, :]))
        kb.wait("tensor", "dmaG")
        for h in range(4):
            kb.step("tensor", "pe", 1, lambda e, h=h: e.matmul(
                (cx.psA if h < 2 else cx.psB)[:, 512 * (h % 2):512 * (h % 2 + 1)], cx.ones1[:],
                cx.ustg[0:1, 512 * h:512 * (h + 1)], start=True, stop=True))
        kb.pe_fence()
        kb.wait("vector", "pe")
        if n < NST:
            kb.plain("vector", lambda e, n=n: e.tensor_copy(cx.abc[n][:, :1024], cx.psA[:, :1024]))
            kb.step("vector", "dve", 1, lambda e, n=n: e.tensor_copy(cx.abc[n][:, 1024:], cx.psB[:, :1024]))
        else:
            kb.plain("vector", lambda e: e.tensor_copy(cx.dpb[:, :1024], cx.psA[:, :1024]))
            kb.step("vector", "dve", 1, lambda e: e.tensor_copy(cx.dpb[:, 1024:], cx.psB[:, :1024]))
    kb.wait("tensor", "dve")
    for h in range(4):
        kb.step("tensor", "pe", 1, lambda e, h=h: e.matmul(
            (cx.psA if h < 2 else cx.psB)[:, 512 * (h % 2):512 * (h % 2 + 1)], cx.rep8[:],
            cx.alpha16[0:16, 512 * h:512 * (h + 1)], start=True, stop=True))
    kb.pe_fence()
    kb.wait("vector", "pe")
    kb.plain("vector", lambda e: e.tensor_copy(cx.alrep[:, 0:1024], cx.psA[:, :1024]))
    kb.step("vector", "dve", 1, lambda e: e.tensor_copy(cx.alrep[:, 1024:], cx.psB[:, :1024]))
    for k in range(16):
        kb.wait("sync", "dve")
        kb.step("sync", "dmaL", 16, lambda e, k=k: e.dma_start(cx.w32[:, 0:1024], key("woT")[128 * k:128 * (k + 1), :]))
        kb.wait("vector", "dmaL")
        kb.step("vector", "dve", 1, lambda e: e.tensor_copy(cx.wos[0][:], cx.w32[:, 0:1024]))
        kb.wait("gpsimd", "dve")
        kb.step("gpsimd", "dmaG", 16, lambda e, k=k: e.dma_start(key("wo_hbm")[128 * k:128 * (k + 1), :], cx.wos[0][:]))
        kb.wait("vector", "dmaG")

    for mg in range(16):
        kb.wait("vector", "pe")
        for k in range(8):
            kb.wait("sync", "dve")
            kb.step("sync", "dmaL", 16, lambda e, mg=mg, k=k: e.dma_start(
                cx.w32[:, 0:256], key("winT")[128 * k:128 * (k + 1), 256 * mg:256 * (mg + 1)]))
            kb.wait("vector", "dmaL")
            kb.step("vector", "dve", 1, lambda e, k=k: e.tensor_copy(cx.wkp[k][:], cx.w32[:, 0:256]))
        for mi in range(2):
            m = mg * 2 + mi
            kb.wait("tensor", "dve")
            kb.wait("tensor", "act")
            for th in range(2):
                tgt = cx.psA if th == 0 else cx.psB
                for nn in range(2):
                    for k in range(8):
                        kb.step("tensor", "pe", 1, lambda e, mi=mi, th=th, nn=nn, k=k, tgt=tgt: e.matmul(
                            tgt[:, N_MM * nn:N_MM * (nn + 1)],
                            cx.wkp[k][:, 128 * mi:128 * (mi + 1)],
                            cx.xm[k][:, 1024 * th + N_MM * nn:1024 * th + N_MM * (nn + 1)],
                            start=(k == 0), stop=(k == 7)))
            if m < 16:
                ct = m
                for k in range(8):
                    kb.step("tensor", "pe", 1, lambda e, mi=mi, k=k: e.matmul(
                        cx.psD[:, 0:4], cx.wkp[k][:, 128 * mi:128 * (mi + 1)],
                        cx.xmt[:, 4 * k:4 * (k + 1)], start=(k == 0), stop=(k == 7)))
                kb.pe_fence()
                kb.wait("vector", "pe")
                kb.wait("vector", "dmaG")
                kb.plain("vector", lambda e: e.tensor_copy(cx.dwin[:, 0:1024], cx.psA[:, :1024]))
                kb.plain("vector", lambda e: e.tensor_copy(cx.dwin[:, 1024:], cx.psB[:, :1024]))
                kb.step("vector", "dve", 1, lambda e: e.tensor_copy(cx.xhead[:, 0:3], cx.psD[:, 0:3]))
                kb.step("vector", "dve", 1, lambda e, ct=ct: e.tensor_scalar(
                    cx.ustg[:, 3:Lc], cx.dwin[:, 0:Lc - 3], cx.convw[:, 4 * ct:4 * ct + 1], None, ALU.mult))
                kb.step("vector", "dve", 1, lambda e, ct=ct: e.tensor_scalar(
                    cx.ustg[:, 0:3], cx.xhead[:, 0:3], cx.convw[:, 4 * ct:4 * ct + 1], None, ALU.mult))
                for tap in range(1, 4):
                    kb.step("vector", "dve", 1, lambda e, ct=ct, tap=tap: e.scalar_tensor_tensor(
                        cx.ustg[:, 3 - tap:Lc], cx.dwin[:, 0:Lc - 3 + tap],
                        cx.convw[:, 4 * ct + tap:4 * ct + tap + 1],
                        cx.ustg[:, 3 - tap:Lc], ALU.mult, ALU.add))
                    if tap < 3:
                        kb.step("vector", "dve", 1, lambda e, ct=ct, tap=tap: e.scalar_tensor_tensor(
                            cx.ustg[:, 0:3 - tap], cx.xhead[:, tap:3],
                            cx.convw[:, 4 * ct + tap:4 * ct + tap + 1],
                            cx.ustg[:, 0:3 - tap], ALU.mult, ALU.add))
                kb.wait("scalar", "dve")
                kb.step("scalar", "act", 1, lambda e, ct=ct: e.activation(
                    cx.dwin[:, 0:Lc], cx.ustg[:, 0:Lc], ACT.Silu, bias=cx.convb[:, ct:ct + 1], scale=1.0))
                kb.wait("gpsimd", "act")
                kb.wait("gpsimd", "dmaL")
                kb.step("gpsimd", "dmaG", 16, lambda e, ct=ct: e.dma_start(
                    key("u_hbm")[ct, :, 0:Lc], cx.dwin[:, 0:Lc]))
                kb.step("gpsimd", "dmaG", 16, lambda e, ct=ct: e.dma_start(
                    key("u_hbm")[ct, :, Lc:TP], cx.zpad[:]))
            else:
                ct = m - 16
                kb.pe_fence()
                kb.wait("scalar", "pe")
                kb.wait("scalar", "dmaG")
                kb.plain("scalar", lambda e: e.activation(
                    cx.ustg[:, 0:1024], cx.psA[:, :1024], ACT.Silu))
                kb.step("scalar", "act", 1, lambda e: e.activation(
                    cx.ustg[:, 1024:2048], cx.psB[:, :1024], ACT.Silu))
                kb.wait("gpsimd", "act")
                kb.wait("gpsimd", "dmaL")
                kb.step("gpsimd", "dmaG", 16, lambda e, ct=ct: e.dma_start(key("z_hbm")[ct, :, 0:Lc], cx.ustg[:]))
                kb.step("gpsimd", "dmaG", 16, lambda e, ct=ct: e.dma_start(key("z_hbm")[ct, :, Lc:TP], cx.zpad[:]))

    kb.wait("vector", "pe")
    kb.step("vector", "dve", 1, lambda e: e.memset(cx.xdblb[:], 0.0))
    kb.wait("gpsimd", "dve")
    kb.step("gpsimd", "dmaG", 16, lambda e: e.dma_start(cx.xdblb[96:97, :], key("onesrow")[:]))
    kb.wait("vector", "dmaG")
    for ct in range(16):
        kb.wait("sync", "pe")
        kb.wait("sync", "dve")
        kb.wait("sync", "dmaG")
        kb.step("sync", "dmaL", 16, lambda e, ct=ct: e.dma_start(cx.ustg[:, 0:Lc], key("u_hbm")[ct, :, 0:Lc]))
        kb.wait("tensor", "dmaL")
        for nn in range(4):
            tgt = cx.psA if nn < 2 else cx.psB
            kb.step("tensor", "pe", 1, lambda e, ct=ct, nn=nn, tgt=tgt: e.matmul(
                tgt[0:96, N_MM * (nn % 2):N_MM * (nn % 2 + 1)],
                cx.wxtb[:, 96 * ct:96 * (ct + 1)],
                cx.ustg[:, N_MM * nn:N_MM * (nn + 1)],
                start=(ct == 0), stop=(ct == 15)))
    kb.pe_fence()
    kb.wait("vector", "pe")
    kb.plain("vector", lambda e: e.tensor_copy(cx.xdblb[0:96, 0:1024], cx.psA[0:96, :]))
    kb.step("vector", "dve", 1, lambda e: e.tensor_copy(cx.xdblb[0:96, 1024:2048], cx.psB[0:96, :]))

    for g, wr in enumerate(GRPW):
        for w in wr:
            wl = w - wr.start
            off = w * WT
            kb.wait("tensor", "dve")
            kb.wait("tensor", "dmaG")
            for ch in range(2):
                tgt = cx.psA if ch == 0 else cx.psB
                for nn in range(2):
                    kb.step("tensor", "pe", 1, lambda e, off=off, ch=ch, nn=nn, tgt=tgt: e.matmul(
                        tgt[:WT, N_MM * nn:N_MM * (nn + 1)],
                        cx.xdblb[0:128, off:off + WT],
                        cx.wdtb[0:128, 1024 * ch + N_MM * nn:1024 * ch + N_MM * (nn + 1)],
                        start=True, stop=True))
            kb.pe_fence()
            kb.wait("vector", "pe")
            for q in range(4):
                src = cx.psA if q < 2 else cx.psB
                so = 512 * (q % 2)
                kb.step("vector", "dve", 1, lambda e, src=src, so=so: e.tensor_scalar(
                    cx.w32[:WT, 0:512], src[:WT, so:so + 512], -30.0, 30.0, ALU.max, ALU.min))
                kb.wait("scalar", "dve")
                kb.step("scalar", "act", 1, lambda e: e.activation(
                    cx.w32[:WT, 512:1024], cx.w32[:WT, 0:512], ACT.Exp))
                kb.step("scalar", "act", 1, lambda e, q=q: e.activation(
                    cx.dwin[:WT, 512 * q:512 * (q + 1)], cx.w32[:WT, 512:1024],
                    ACT.Ln, bias=1.0, scale=1.0))
                kb.wait("vector", "act")
            if w == NW - 1:
                kb.step("vector", "dve", 1, lambda e: e.tensor_scalar(
                    cx.dwin[:WT, :], cx.dwin[:WT, :], cx.maskw16[:WT, :], None, ALU.mult))
            kb.wait("gpsimd", "dve")
            kb.wait("gpsimd", "dmaL")
            kb.step("gpsimd", "dmaG", 16, lambda e, w=w: e.dma_start(key("d_hbm")[w, :, :], cx.dwin[:]))
            kb.wait("tensor", "dve")
            for ch in range(2):
                sel = cx.hselA if ch == 0 else cx.hselB
                for nn in range(2):
                    kb.step("tensor", "pe", 1, lambda e, sel=sel, wl=wl, ch=ch, nn=nn, w=w, wr=wr, g=g: e.matmul(
                        cx.psC[32 * g:32 * (g + 1), N_MM * nn:N_MM * (nn + 1)],
                        sel[:, 32 * wl:32 * (wl + 1)],
                        cx.dwin[:, 1024 * ch + N_MM * nn:1024 * ch + N_MM * (nn + 1)],
                        start=(wl == 0 and ch == 0), stop=(w == wr.stop - 1 and ch == 1)))
        kb.pe_fence()
        kb.wait("vector", "pe")
        kb.step("vector", "dve", 1, lambda e, g=g: e.tensor_copy(cx.dmde[g], cx.psC[32 * g:32 * (g + 1), :]))
        kb.wait("tensor", "dve")

    if pre_p3 is not None:
        pre_p3()

    wog_idx = 0
    for g, wr in enumerate(GRPW):
        kb.wait("tensor", "dve")
        for which in range(2):
            i0 = cx.indA0 if which == 0 else cx.indB0
            i1 = cx.indA1 if which == 0 else cx.indB1
            gdst = cx.gA if which == 0 else cx.gB
            for ch in range(2):
                ind = i0 if ch == 0 else i1
                kb.wait("tensor", "dve")
                for nn in range(2):
                    kb.step("tensor", "pe", 1, lambda e, ind=ind, nn=nn, g=g: e.matmul(
                        cx.psA[:, 512 * nn:512 * (nn + 1)], ind[32 * g:32 * (g + 1), :],
                        cx.dmde[g][:, 512 * nn:512 * (nn + 1)], start=True, stop=True))
                kb.pe_fence()
                kb.wait("vector", "pe")
                kb.step("vector", "dve", 1, lambda e, ch=ch: e.tensor_tensor(
                    cx.w32[:, :], cx.alrep[:, 1024 * ch:1024 * (ch + 1)],
                    cx.psA[:, :1024], ALU.mult))
                kb.wait("scalar", "dve")
                kb.step("scalar", "act", 1, lambda e, gdst=gdst, ch=ch: e.activation(
                    gdst[:, 1024 * ch:1024 * (ch + 1)], cx.w32[:], ACT.Exp, bias=0.0, scale=-1.0))
                kb.wait("vector", "act")
        kb.wait("vector", "act")

        for w in wr:
            wl = w - wr.start
            off = w * WT
            wo_g = WOG[wog_idx]
            wo_l = w - wo_g.start
            kb.wait("sync", "dve")
            kb.wait("sync", "act")
            kb.wait("sync", "dmaG")
            kb.step("sync", "dmaL", 16, lambda e, w=w: e.dma_start(cx.dwin[:], key("d_hbm")[w, :, :]))
            kb.wait("sync", "pe")
            for cb in range(16):
                kb.step("sync", "dmaL", 16, lambda e, cb=cb, off=off: e.dma_start(
                    cx.ustg[:, WT * cb:WT * (cb + 1)], key("u_hbm")[cb, :, off:off + WT]))
            uld = kb.cnt["dmaL"]
            kb.wait("gpsimd", "dve")
            kb.wait("gpsimd", "act")
            kb.wait("gpsimd", "dmaG", None)
            for cb in range(16):
                kb.step("gpsimd", "dmaG", 16, lambda e, cb=cb, off=off: e.dma_start(
                    cx.zsil[:, WT * cb:WT * (cb + 1)], key("z_hbm")[cb, :, off:off + WT]))
            kb.wait("tensor", "dmaL", uld)
            kb.wait("tensor", "dve")
            for rnd in range(2):
                for cb8 in range(8):
                    cb = rnd * 8 + cb8
                    kb.step("tensor", "pe", 1, lambda e, cb=cb, cb8=cb8: e.transpose(
                        cx.psT[0:128, 128 * cb8:128 * (cb8 + 1)],
                        cx.ustg[:, WT * cb:WT * cb + 128], cx.ident[:]))
                kb.pe_fence()
                kb.wait("vector", "pe")
                kb.step("vector", "dve", 1, lambda e, rnd=rnd: e.tensor_copy(
                    cx.duw[:WT, 1024 * rnd:1024 * (rnd + 1)], cx.psT[:WT, :]))
                if rnd == 0:
                    kb.wait("tensor", "dve")
            kb.step("vector", "dve", 1, lambda e: e.tensor_tensor(cx.yw[:WT, :], cx.duw[:WT, :], cx.dpb[:WT, :], ALU.mult))
            kb.wait("tensor", "dve")
            for ch in range(2):
                tgt = cx.psA if ch == 0 else cx.psB
                for nn in range(2):
                    kb.step("tensor", "pe", 1, lambda e, ch=ch, nn=nn, tgt=tgt: e.matmul(
                        tgt[:, N_MM * nn:N_MM * (nn + 1)], cx.trilmid[:],
                        cx.dwin[:, 1024 * ch + N_MM * nn:1024 * ch + N_MM * (nn + 1)],
                        start=True, stop=True))
            kb.pe_fence()
            kb.wait("vector", "pe")
            kb.plain("vector", lambda e: e.tensor_copy(cx.dlw[:, 0:1024], cx.psA[:, :1024]))
            kb.step("vector", "dve", 1, lambda e: e.tensor_copy(cx.dlw[:, 1024:], cx.psB[:, :1024]))
            kb.step("vector", "dve", 1, lambda e: e.tensor_tensor(cx.duw[:WT, :], cx.dwin[:WT, :], cx.duw[:WT, :], ALU.mult))
            kb.wait("tensor", "dve")
            kb.step("tensor", "pe", 1, lambda e, off=off: e.transpose(
                cx.psT[0:128, 0:32], cx.xdblb[64:96, off:off + 128], cx.ident[64:96, 64:96]))
            kb.pe_fence()
            kb.wait("vector", "pe")
            kb.step("vector", "dve", 1, lambda e: e.tensor_copy(cx.bct[:WT, :], cx.psT[:WT, 0:32]))
            kb.step("vector", "dve", 1, lambda e: e.tensor_copy(cx.bctf[:], cx.bct[:]))
            kb.wait("gpsimd", "act")
            kb.wait("gpsimd", "dve")
            kb.wait("gpsimd", "dmaG", None)
            kb.step("gpsimd", "dmaG", 16, lambda e, wl=wl: e.dma_start(cx.gsl[:], cx.gA[16 * wl:16 * (wl + 1), :]))
            kb.wait("vector", "dmaG")
            kb.step("vector", "dve", 1, lambda e: e.tensor_tensor(
                cx.gsl[:], cx.hrun[:], cx.gsl[:], ALU.mult))

            kb.wait("gpsimd", "dve")
            kb.wait("gpsimd", "dmaG", None)
            kb.step("gpsimd", "dmaG", 16, lambda e: e.dma_start(
                cx.ww[127:128, :], cx.gsl[0:1, :]))
            wg = kb.cnt["dmaG"]
            for n in range(NST):
                kb.wait("vector", "pe")
                kb.wait("vector", "act")
                kb.plain("vector", lambda e, n=n: e.tensor_scalar(
                    cx.statn[:], cx.trilsum[:], cx.bctf[:, n:n + 1], None, ALU.mult))
                kb.step("vector", "dve", 1, lambda e, n=n: e.tensor_tensor(
                    cx.dwin[:], cx.dlw[:], cx.abc[n][:], ALU.mult))
                kb.wait("scalar", "dve")
                kb.step("scalar", "act", 1, lambda e: e.activation(cx.rg[:], cx.dwin[:], ACT.Exp))
                kb.wait("vector", "act")
                kb.step("vector", "dve", 1, lambda e: e.tensor_tensor(
                    cx.ww[:WT, :], cx.rg[:WT, :], cx.duw[:WT, :], ALU.mult))
                kb.wait("scalar", "dve")
                kb.step("scalar", "act", 1, lambda e: e.activation(cx.rg[:], cx.dwin[:], ACT.Exp, bias=0.0, scale=-1.0))
                wd = kb.cnt["dve"]
                kb.wait("tensor", "dve", wd)
                kb.wait("tensor", "dmaG", wg)
                for ch in range(2):
                    tgt = cx.psA if ch == 0 else cx.psB
                    for nn in range(2):
                        kb.step("tensor", "pe", 1, lambda e, ch=ch, nn=nn, tgt=tgt: e.matmul(
                            tgt[:, N_MM * nn:N_MM * (nn + 1)], cx.statn[:],
                            cx.ww[:, 1024 * ch + N_MM * nn:1024 * ch + N_MM * (nn + 1)],
                            start=True, stop=True))
                mmd = kb.pe_fence()
                kb.wait("vector", "pe", mmd)
                kb.wait("vector", "act")
                kb.wait("vector", "dmaG")
                kb.plain("vector", lambda e: e.tensor_tensor(
                    cx.ww[:, 0:1024], cx.rg[:, 0:1024], cx.psA[:, :], ALU.mult))
                kb.step("vector", "dve", 1, lambda e: e.tensor_tensor(
                    cx.ww[:, 1024:], cx.rg[:, 1024:], cx.psB[:, :], ALU.mult))
                kb.wait("gpsimd", "dve")
                kb.step("gpsimd", "dmaG", 16, lambda e, n=n: e.dma_start(cx.hraw[n:n + 1, :], cx.ww[127:128, :]))
                if n < NST - 1:
                    kb.wait("gpsimd", "pe", mmd)
                    kb.wait("gpsimd", "dmaG", None)
                    kb.step("gpsimd", "dmaG", 16, lambda e, n=n: e.dma_start(
                        cx.ww[127:128, :], cx.gsl[n + 1:n + 2, :]))
                    wg = kb.cnt["dmaG"]
                kb.step("vector", "dve", 1, lambda e, n=n: e.scalar_tensor_tensor(
                    cx.yw[:WT, :], cx.ww[:WT, :], cx.bctf[:WT, NST + n:NST + n + 1], cx.yw[:WT, :],
                    ALU.mult, ALU.add))
            kb.wait("gpsimd", "dve")
            kb.wait("gpsimd", "dmaG", None)
            kb.step("gpsimd", "dmaG", 16, lambda e, wl=wl: e.dma_start(cx.gsl[:], cx.gB[16 * wl:16 * (wl + 1), :]))
            kb.wait("vector", "dmaG")
            kb.step("vector", "dve", 1, lambda e: e.tensor_tensor(
                cx.hrun[:], cx.hraw[:], cx.gsl[:], ALU.mult))
            kb.wait("tensor", "dve")
            for rnd in range(2):
                for cb8 in range(8):
                    cb = rnd * 8 + cb8
                    kb.step("tensor", "pe", 1, lambda e, cb=cb, cb8=cb8: e.transpose(
                        cx.psT[0:128, 128 * cb8:128 * (cb8 + 1)],
                        cx.yw[:, 128 * cb:128 * (cb + 1)], cx.ident[:]))
                kb.pe_fence()
                kb.wait("vector", "pe")
                for cb8 in range(8):
                    cb = rnd * 8 + cb8
                    fn = lambda e, cb=cb, cb8=cb8, wo_l=wo_l: e.tensor_tensor(
                        cx.wrhs[cb][:, WT * wo_l:WT * (wo_l + 1)],
                        cx.psT[:, 128 * cb8:128 * cb8 + WT],
                        cx.zsil[:, WT * cb:WT * (cb + 1)], ALU.mult)
                    if cb8 < 7:
                        kb.plain("vector", fn)
                    else:
                        kb.step("vector", "dve", 1, fn)
                if rnd == 0:
                    kb.wait("tensor", "dve")
            if w == WOG[wog_idx].stop - 1:
                emit_wout(kb, cx, key, WOG[wog_idx])
                wog_idx += 1


def emit_wout(kb, cx, key, wog):
    vw = len(wog) * WT if wog.stop <= 16 else LASTV
    goff = wog.start * WT
    kb.wait("tensor", "dve")
    kb.wait("sync", "dmaG")
    slots = [(cx.psA, 0), (cx.psA, 512), (cx.psB, 0), (cx.psB, 512),
             (cx.psC, 0), (cx.psC, 512), (cx.psD, 0)]
    n0 = 0
    n1 = vw
    for mpass, ms in enumerate((range(0, 7), range(7, 8))):
        fcnt = {}
        for cb in range(16):
            sl = cx.wos[cb % 2]
            if cb >= 2:
                kb.wait("sync", "pe", fcnt[cb - 2])
            else:
                kb.wait("sync", "pe")
            kb.wait("sync", "dve")
            kb.step("sync", "dmaL", 16, lambda e, cb=cb, sl=sl: e.dma_start(
                sl[:], key("wo_hbm")[128 * cb:128 * (cb + 1), :]))
            kb.wait("tensor", "dmaL")
            for m in ms:
                ps, po = slots[m - ms.start]
                kb.step("tensor", "pe", 1, lambda e, m=m, cb=cb, n0=n0, n1=n1, ps=ps, po=po, sl=sl: e.matmul(
                    ps[:, po:po + (n1 - n0)],
                    sl[:, 128 * m:128 * (m + 1)],
                    cx.wrhs[cb][:, n0:n1], start=(cb == 0), stop=(cb == 15)))
            fcnt[cb] = kb.pe_fence()
        kb.wait("vector", "pe")
        for m in ms:
            ps, po = slots[m - ms.start]
            fn = lambda e, m=m, goff=goff, n0=n0, n1=n1, ps=ps, po=po: e.tensor_tensor(
                cx.xm[m][:, goff + n0:goff + n1], cx.xm[m][:, goff + n0:goff + n1],
                ps[:, po:po + (n1 - n0)], ALU.add)
            if m != ms.stop - 1:
                kb.plain("vector", fn)
            else:
                kb.step("vector", "dve", 1, fn)
        kb.wait("tensor", "dve")


def emit_x2(kb, cx, key):
    for kt in range(8):
        for h in range(2):
            kb.wait_all("sync")
            kb.step("sync", "dmaL", 16, lambda e, kt=kt, h=h: e.dma_start(
                cx.w32[:, 0:1024], key("XA")[128 * kt:128 * (kt + 1), 3 + 1024 * h:3 + 1024 * (h + 1)]))
            kb.wait("vector", "dmaL")
            kb.step("vector", "dve", 1, lambda e, kt=kt, h=h: e.scalar_tensor_tensor(
                cx.w32[:, 0:1024], cx.xm[kt][:, 1024 * h:1024 * (h + 1)],
                cx.mcols[:, 16 + kt:17 + kt], cx.w32[:, 0:1024], ALU.mult, ALU.add))
            kb.step("vector", "dve", 1, lambda e, kt=kt, h=h: e.tensor_copy(
                cx.xm[kt][:, 1024 * h:1024 * (h + 1)], cx.w32[:, 0:1024]))
            kb.wait_all("gpsimd")
            kb.step("gpsimd", "dmaG", 16, lambda e, kt=kt, h=h: e.dma_start(
                key("x2_hbm")[kt, :, 1024 * h:1024 * (h + 1)], cx.w32[:, 0:1024]))


def emit_ffn_wcast(kb, cx, key):
    kb.wait_all("sync")
    kb.step("sync", "dmaL", 16, lambda e: e.dma_start(cx.w32[:, 0:32], key("b1T")[:]))
    kb.step("sync", "dmaL", 16, lambda e: e.dma_start(cx.w32[:, 32:40], key("b2T")[:]))
    kb.wait("vector", "dmaL")
    kb.step("vector", "dve", 1, lambda e: e.tensor_copy(cx.w1p[:, 0:40], cx.w32[:, 0:40]))
    kb.step("vector", "dve", 1, lambda e: e.tensor_copy(cx.bq[:, 0:40], cx.w1p[:, 0:40]))
    for kt in range(8):
        for c in range(4):
            kb.wait_all("sync")
            kb.step("sync", "dmaL", 16, lambda e, kt=kt, c=c: e.dma_start(
                cx.w32[:, 0:1024], key("w1T")[128 * kt:128 * (kt + 1), 1024 * c:1024 * (c + 1)]))
            kb.wait("vector", "dmaL")
            kb.wait("vector", "dmaG")
            kb.step("vector", "dve", 1, lambda e: e.tensor_copy(cx.w1p[:, 0:1024], cx.w32[:, 0:1024]))
            kb.wait_all("gpsimd")
            for mj in range(8):
                kb.step("gpsimd", "dmaG", 16, lambda e, kt=kt, c=c, mj=mj: e.dma_start(
                    key("w1q_hbm")[8 * c + mj, :, 128 * kt:128 * (kt + 1)],
                    cx.w1p[:, 128 * mj:128 * (mj + 1)]))
    for m in range(32):
        kb.wait_all("sync")
        kb.step("sync", "dmaL", 16, lambda e, m=m: e.dma_start(
            cx.w32[:, 0:1024], key("w2T")[128 * m:128 * (m + 1), :]))
        kb.wait("vector", "dmaL")
        kb.wait("vector", "dmaG")
        kb.step("vector", "dve", 1, lambda e: e.tensor_copy(cx.w1p[:, 0:1024], cx.w32[:, 0:1024]))
        kb.wait_all("gpsimd")
        kb.step("gpsimd", "dmaG", 16, lambda e, m=m: e.dma_start(key("w2q_hbm")[m, :, :], cx.w1p[:, 0:1024]))


def emit_ffn(kb, cx, key):
    slots = [(cx.psA, 0), (cx.psA, 512), (cx.psB, 0), (cx.psB, 512),
             (cx.psC, 0), (cx.psC, 512), (cx.psD, 0)]
    for tch in range(4):
        toff = 512 * tch
        kb.wait_all("sync")
        for kt in range(8):
            kb.step("sync", "dmaL", 16, lambda e, kt=kt, toff=toff: e.dma_start(
                cx.h2f8[:, 512 * kt:512 * (kt + 1)], key("h2_hbm")[kt, :, toff:toff + 512]))
        for m in range(32):
            kb.wait_all("sync")
            kb.step("sync", "dmaL", 16, lambda e, m=m: e.dma_start(
                cx.w1p[:, 0:1024], key("w1q_hbm")[m, :, :]))
            kb.wait("tensor", "dmaL")
            kb.wait("tensor", "dve")
            for kt in range(8):
                kb.step("tensor", "pe", 1, lambda e, m=m, kt=kt: e.matmul(
                    cx.psD[:, 0:512], cx.w1p[:, 128 * kt:128 * (kt + 1)],
                    cx.h2f8[:, 512 * kt:512 * (kt + 1)], start=(kt == 0), stop=(kt == 7)))
            kb.pe_fence()
            kb.wait("vector", "pe")
            kb.step("vector", "dve", 1, lambda e, m=m: e.tensor_scalar(
                cx.dwin[:, 0:512], cx.psD[:, 0:512], cx.bq[:, m:m + 1], 0.0, ALU.add, ALU.max))
            kb.wait("vector", "dmaG")
            kb.step("vector", "dve", 1, lambda e: e.tensor_copy(cx.h1q8[:], cx.dwin[:, 0:512]))
            kb.wait_all("gpsimd")
            kb.step("gpsimd", "dmaG", 16, lambda e, m=m: e.dma_start(key("h1_hbm")[m, :, :], cx.h1q8[:]))
        for dmg in (range(0, 7), range(7, 8)):
            for m in range(32):
                kb.pe_fence()
                kb.wait_all("sync")
                kb.step("sync", "dmaL", 16, lambda e, m=m: e.dma_start(
                    cx.w1p[:, 0:512], key("h1_hbm")[m, :, :]))
                kb.step("sync", "dmaL", 16, lambda e, m=m, dmg=dmg: e.dma_start(
                    cx.w2s[:, 0:128 * len(dmg)],
                    key("w2q_hbm")[m, :, 128 * dmg.start:128 * dmg.stop]))
                kb.wait("tensor", "dmaL")
                kb.wait("tensor", "dve")
                for dm in dmg:
                    ps, po = slots[dm - dmg.start]
                    kb.step("tensor", "pe", 1, lambda e, dm=dm, m=m, ps=ps, po=po, dmg=dmg: e.matmul(
                        ps[:, po:po + 512], cx.w2s[:, 128 * (dm - dmg.start):128 * (dm - dmg.start + 1)],
                        cx.w1p[:, 0:512], start=(m == 0), stop=(m == 31)))
            kb.pe_fence()
            kb.wait("vector", "pe")
            for dm in dmg:
                ps, po = slots[dm - dmg.start]
                kb.wait_all("sync")
                kb.step("sync", "dmaL", 16, lambda e, dm=dm, toff=toff: e.dma_start(
                    cx.w32[:, 512:1024], key("x2_hbm")[dm, :, toff:toff + 512]))
                kb.wait_all("gpsimd")
                kb.step("gpsimd", "dmaG", 16, lambda e, dm=dm, toff=toff: e.dma_start(
                    cx.dlw[:, 0:512], key("maskT")[dm, :, toff:toff + 512]))
                kb.wait("vector", "dmaL")
                kb.wait("vector", "dmaG")
                kb.step("vector", "dve", 1, lambda e, dm=dm, ps=ps, po=po: e.tensor_scalar(
                    cx.dwin[:, 0:512], ps[:, po:po + 512], cx.bq[:, 32 + dm:33 + dm], None, ALU.add))
                kb.step("vector", "dve", 1, lambda e: e.tensor_tensor(
                    cx.w32[:, 0:512], cx.dwin[:, 0:512], cx.dlw[:, 0:512], ALU.mult))
                kb.step("vector", "dve", 1, lambda e, dm=dm: e.scalar_tensor_tensor(
                    cx.w32[:, 512:1024], cx.w32[:, 0:512], cx.mcols[:, 40 + dm:41 + dm],
                    cx.w32[:, 512:1024], ALU.mult, ALU.add))
                kb.wait_all("gpsimd")
                kb.step("gpsimd", "dmaG", 16, lambda e, dm=dm, toff=toff: e.dma_start(
                    key("OUT")[dm, :, toff:toff + 512], cx.w32[:, 512:1024]))


def build_program(fused):
    nc = bass.Bass("TRN2", target_bir_lowering=False)
    dram = {}

    def d(name, shape, dt, kind="Internal"):
        dname = "i_" + name if kind == "ExternalInput" else name
        dram[name] = nc.dram_tensor(dname, shape, dt, kind=kind)
        return dram[name]

    d("XA", [1024, 2051], F32, kind="ExternalInput")
    d("ccol", [128, 8], F32, kind="ExternalInput")
    d("wmod", [1024, 6144], F32, kind="ExternalInput")
    d("fcols", [128, 2], F32, kind="ExternalInput")
    for s in range(NSLOT):
        d(f"winT{s}", [1024, 4096], F32, kind="ExternalInput")
        d(f"wxT{s}", [128, 1536], F32, kind="ExternalInput")
        d(f"wdtTb{s}", [128, 2048], F32, kind="ExternalInput")
        d(f"woT{s}", [2048, 1024], F32, kind="ExternalInput")
        d(f"cw{s}", [128, 64], F32, kind="ExternalInput")
        d(f"cb{s}", [128, 16], F32, kind="ExternalInput")
        d(f"alogT{s}", [16, 2048], F32, kind="ExternalInput")
        d(f"dpar{s}", [1, 2048], F32, kind="ExternalInput")
        if fused:
            d(f"arinH{s}", [16, 2048], F16)
            d(f"aroutH{s}", [16, 2048], F16)
            d(f"arinX{s}", [128, 32], F32)
            d(f"aroutX{s}", [128, 32], F32)
        else:
            d(f"arinH{s}", [16, 2048], F16, kind="ExternalOutput")
            d(f"aroutH{s}", [16, 2048], F16, kind="ExternalInput")
            d(f"arinX{s}", [128, 32], F32, kind="ExternalOutput")
            d(f"aroutX{s}", [128, 32], F32, kind="ExternalInput")
    d("w1T", [1024, 4096], F32, kind="ExternalInput")
    d("b1T", [128, 32], F32, kind="ExternalInput")
    d("w2T", [4096, 1024], F32, kind="ExternalInput")
    d("b2T", [128, 8], F32, kind="ExternalInput")
    d("maskT", [8, 128, 2048], F16, kind="ExternalInput")
    for n, a in host_consts().items():
        dt = {np.dtype("float32"): F32, np.dtype(ml_dtypes.bfloat16): BF16,
              np.dtype("float16"): F16}[a.dtype]
        d(n, list(a.shape), dt, kind="ExternalInput")
    d("u_hbm", [16, 128, TP], F16)
    d("z_hbm", [16, 128, TP], F16)
    d("d_hbm", [17, 128, 2048], F16)
    d("wo_hbm", [2048, 1024], F16)
    d("x2_hbm", [8, 128, 2048], F32)
    d("h2_hbm", [8, 128, 2048], F8)
    d("h1_hbm", [32, 128, 512], F8)
    d("w1q_hbm", [32, 128, 1024], F8)
    d("w2q_hbm", [32, 128, 1024], F8)
    d("OUT", [8, 128, 2048], F32, kind="ExternalOutput")
    if DEBUG:
        d("DBGM", [128, 48], F32, kind="ExternalOutput")
        d("DBGH", [8, 128, 2048], F16, kind="ExternalOutput")
        d("DBGS", [8, 128, 2048], F16, kind="ExternalOutput")
        d("DBGX2", [8, 128, 2048], F32, kind="ExternalOutput")
        d("DBGH2", [8, 128, 2048], F8, kind="ExternalOutput")

    kb = KB(nc)
    cx = Ctx(nc)

    def mkkey(s):
        def key2(name):
            if name in ("winT", "wxT", "wdtTb", "woT", "cw", "cb", "alogT", "dpar"):
                return dram[f"{name}{s}"]
            return dram[name]
        return key2

    gkey = mkkey(0)

    with nc.Block() as block:
        emit_load_consts(kb, cx, gkey)
        emit_modvecs(kb, cx, gkey)
        if DEBUG:
            kb.wait_all("gpsimd")
            kb.step("gpsimd", "dmaG", 16, lambda e: e.dma_start(dram["DBGM"][:], cx.mcols[:]))
        emit_ln(kb, cx, gkey, phase=1)
        if DEBUG:
            kb.wait_all("gpsimd")
            for kt in range(8):
                kb.step("gpsimd", "dmaG", 16, lambda e, kt=kt: e.dma_start(dram["DBGH"][kt, :, :], cx.xm[kt][:]))
        for s in range(NSLOT):
            key = mkkey(s)
            emit_slot_head(kb, cx, key, s)
            emit_layer(kb, cx, key, pre_p3=lambda s=s, key=key: emit_ar_consume(kb, cx, key, s))
        if DEBUG:
            kb.wait_all("gpsimd")
            for kt in range(8):
                kb.step("gpsimd", "dmaG", 16, lambda e, kt=kt: e.dma_start(dram["DBGS"][kt, :, :], cx.xm[kt][:]))
        emit_x2(kb, cx, gkey)
        emit_ln(kb, cx, gkey, phase=2)
        if DEBUG:
            kb.wait_all("sync")
            kb.wait_all("gpsimd")
            kb.step("gpsimd", "dmaG", 16, lambda e: e.dma_start(dram["DBGX2"][:], dram["x2_hbm"][:]))
            kb.step("gpsimd", "dmaG", 16, lambda e: e.dma_start(dram["DBGH2"][:], dram["h2_hbm"][:]))
        emit_ffn_wcast(kb, cx, gkey)
        emit_ffn(kb, cx, gkey)
        kb.wait("gpsimd", "dmaG")
        kb.wait("sync", "dmaL")
        kb.run_into_block(block)
    return nc


def host_inputs(inputs):
    f32 = np.float32
    x = np.asarray(inputs["L0_attn_x"], f32)
    W_in = np.asarray(inputs["L0_attn_W_in"], f32)
    conv_w = np.asarray(inputs["L0_attn_conv_w"], f32)
    conv_b = np.asarray(inputs["L0_attn_conv_b"], f32)
    W_x = np.asarray(inputs["L0_attn_W_x"], f32)
    W_dt = np.asarray(inputs["L0_attn_W_dt"], f32)
    b_dt = np.asarray(inputs["L0_attn_b_dt"], f32)
    A_log = np.asarray(inputs["L0_attn_A_log"], f32)
    D_par = np.asarray(inputs["L0_attn_D_par"], f32)
    W_out = np.asarray(inputs["L0_attn_W_out"], f32)
    w1 = np.asarray(inputs["L0_ffn_w1"], f32)
    b1 = np.asarray(inputs["L0_ffn_b1"], f32)
    w2 = np.asarray(inputs["L0_ffn_w2"], f32)
    b2 = np.asarray(inputs["L0_ffn_b2"], f32)
    c = np.asarray(inputs["L0_c"], f32)
    wmod = np.asarray(inputs["L0_wmod"], f32)

    def layer_set(i):
        if i is None:
            return {
                "winT": np.zeros((1024, 4096), f32), "wxT": np.zeros((128, 1536), f32),
                "wdtTb": np.zeros((128, 2048), f32), "woT": np.zeros((2048, 1024), f32),
                "cw": np.zeros((128, 64), f32), "cb": np.zeros((128, 16), f32),
                "alogT": np.zeros((16, 2048), f32), "dpar": np.zeros((1, 2048), f32),
            }
        return {
            "winT": np.ascontiguousarray(W_in[i].T),
            "wxT": np.ascontiguousarray(W_x[i].T.reshape(16, 128, 96).transpose(1, 0, 2).reshape(128, 16 * 96)),
            "wdtTb": np.ascontiguousarray(np.concatenate(
                [W_dt[i].T, np.zeros((32, 2048), f32), b_dt[i][None, :], np.zeros((31, 2048), f32)], 0)),
            "woT": np.ascontiguousarray(W_out[i].T),
            "cw": np.ascontiguousarray(conv_w[i].reshape(16, 128, 4).transpose(1, 0, 2).reshape(128, 64)),
            "cb": np.ascontiguousarray(conv_b[i].reshape(16, 128).T),
            "alogT": np.ascontiguousarray(A_log[i].T),
            "dpar": np.ascontiguousarray(D_par[i][None, :]),
        }

    lsets = [layer_set(i) for i in range(8)]
    zset = layer_set(None)
    consts = host_consts()

    import jax
    _k42 = lambda: jax.random.key(42, impl="threefry2x32")
    try:
        with jax.default_device(jax.local_devices(backend="cpu")[0]):
            keep = np.asarray(jax.random.bernoulli(_k42(), 0.8, (4, 4096, 1024)))
    except Exception:
        keep = np.asarray(jax.random.bernoulli(_k42(), 0.8, (4, 4096, 1024)))
    maskf = keep.astype(np.float16) * np.float16(1.25)

    shared = {
        "wmod": np.ascontiguousarray(wmod),
        "w1T": np.ascontiguousarray(w1.T),
        "b1T": np.ascontiguousarray(b1.reshape(32, 128).T),
        "w2T": np.ascontiguousarray(w2.T),
        "b2T": np.ascontiguousarray(b2.reshape(8, 128).T),
    }
    shared.update(consts)

    per_core = []
    for core in range(8):
        b, hf = core // 2, core % 2
        off = 2048 * hf
        XA = np.zeros((1024, 2051), f32)
        XA[:, 3:] = x[b, off:off + 2048].T
        if hf:
            XA[:, 0:3] = x[b, 2045:2048].T
        fcols = np.zeros((128, 2), f32)
        fcols[:, 0] = float(hf)
        fcols[:, 1] = 1.0 - float(hf)
        m = {
            "XA": XA,
            "ccol": np.ascontiguousarray(c[b].reshape(8, 128).T),
            "fcols": fcols,
            "maskT": np.ascontiguousarray(maskf[b, off:off + 2048].T.reshape(8, 128, 2048)),
        }
        m.update(shared)
        for s in range(NSLOT):
            li = s if hf == 0 else s - 1
            src = zset if (li < 0 or li > 7) else lsets[li]
            for kname, arr in src.items():
                m[f"{kname}{s}"] = arr
        if not FUSED:
            for s in range(NSLOT):
                m[f"aroutH{s}"] = np.zeros((16, 2048), np.float16)
                m[f"aroutX{s}"] = np.zeros((128, 32), f32)
        per_core.append(m)
    return per_core


def _prefix(per_core):
    return [{("i_" + k): v for k, v in m.items()} for m in per_core]


def kernel(**inputs):
    per_core = host_inputs(inputs)
    nc = build_program(FUSED)
    out = np.empty((4, 4096, 1024), np.float32)
    if FUSED:
        res = run_bass_kernel_spmd(nc, _prefix(per_core), core_ids=list(range(8)))
        for core in range(8):
            b, hf = core // 2, core % 2
            arr = np.asarray(res.results[core]["OUT"])
            out[b, 2048 * hf:2048 * (hf + 1)] = arr.transpose(2, 0, 1).reshape(2048, 1024)
    else:
        res1 = run_bass_kernel_spmd(nc, _prefix(per_core), core_ids=list(range(8)))
        per2 = []
        for core in range(8):
            b, hf = core // 2, core % 2
            m = dict(per_core[core])
            if hf == 1:
                r0 = res1.results[core - 1]
                for s in range(NSLOT):
                    m[f"aroutH{s}"] = np.asarray(r0[f"arinH{s}"])
                    m[f"aroutX{s}"] = np.asarray(r0[f"arinX{s}"])
            per2.append(m)
        res2 = run_bass_kernel_spmd(nc, _prefix(per2), core_ids=list(range(8)))
        for core in range(8):
            b, hf = core // 2, core % 2
            r = (res1 if hf == 0 else res2).results[core]
            arr = np.asarray(r["OUT"])
            out[b, 2048 * hf:2048 * (hf + 1)] = arr.transpose(2, 0, 1).reshape(2048, 1024)
    return out
